# version 05
# baseline (speedup 1.0000x reference)
import math
from contextlib import ExitStack
import numpy as np
import ml_dtypes
import concourse.bass as bass
import concourse.mybir as mybir
from concourse.bass_utils import run_bass_kernel_spmd

F32 = mybir.dt.float32
BF16 = mybir.dt.bfloat16
AF = mybir.ActivationFunctionType
ALU = mybir.AluOpType
AX = mybir.AxisListType

NCORES = 8
E = 1024
H = 16
DH = 64
DR = 32
DKV = 256
I3 = 3072
EPS = 1e-5
SCALE_ATT = 1.0 / math.sqrt(DH + DR)
INVQ = 1.0 / 32767.0
MAGIC = 12582912.0

S_X0, S_X1 = 0, 1
S_S1, S_I1 = 2, 3
S_S2, S_C2 = 4, 5
S_S3, S_C3 = 6, 7
S_S4, S_C4 = 8, 9
S_S5, S_C5 = 10, 11
S_LU, S_LG, S_LGS, S_LOR = 12, 13, 14, 15
B_I1, B_C2, B_C3, B_S3, B_C4, B_C5, B_S5 = 0, 1, 2, 3, 4, 5, 6


def cdiv(a, b):
    return (a + b - 1) // b


class KB:
    def __init__(self, T):
        self.T = T
        self.NT = T // 128
        self.nc = bass.Bass("TRN2", target_bir_lowering=False)
        self.c = {}
        self.root = ExitStack()

    def sem(self, n):
        self.c[n] = [self.root.enter_context(self.nc.semaphore(n)), 0]

    def H(self, n):
        return self.c[n][0]

    def V(self, n):
        return self.c[n][1]

    def inc(self, n, k=1):
        self.c[n][1] += k
        return self.c[n][1]

    def mark(self, inst, n, k=1):
        inst.then_inc(self.H(n), k)
        return self.inc(n, k)

    def edge(self, eng, inst, n="sVo"):
        v = self.mark(inst, n)
        eng.wait_ge(self.H(n), v)

    def edges(self, eng, insts, n="sVo"):
        for i in insts:
            v = self.mark(i, n)
        eng.wait_ge(self.H(n), v)

    def sb(self, es, name, shape, dt, side="left"):
        return es.enter_context(self.nc.sbuf_tensor(name, shape, dt, side=side))

    def psum(self, es, name, shape, dt):
        return es.enter_context(self.nc.psum_tensor(name, shape, dt))

    def acc_mm(self, t_, out_ap, pairs):
        n = len(pairs)
        li = None
        for i, (l, r) in enumerate(pairs):
            li = t_.matmul(out_ap, l, r, start=(i == 0), stop=(i == n - 1))
        return li

    def mm_evac(self, items, emit_mms, emit_evac, bufs, evac_eng="vector"):
        nb = len(bufs)
        with self.nc.Block() as blk:
            tb = self.V("sMM")
            vb = self.V("sEV")

            @blk.tensor
            def _(t_):
                for k, it in enumerate(items):
                    if k >= nb:
                        t_.wait_ge(self.H("sEV"), vb + k - nb + 1)
                    li = emit_mms(t_, it, bufs[k % nb])
                    li.then_inc(self.H("sMM"), 1)
            self.inc("sMM", len(items))

            def body(eng):
                for k, it in enumerate(items):
                    eng.wait_ge(self.H("sMM"), tb + k + 1)
                    li = emit_evac(eng, it, bufs[k % nb])
                    li.then_inc(self.H("sEV"), 1)
            (blk.vector if evac_eng == "vector" else blk.scalar)(body)
            self.inc("sEV", len(items))

    def transpose_tm_to_fm(self, src_tm, dst_fm, pa, pb):
        T, NT = self.T, self.NT
        groups = []
        for fc in range(8):
            for half in range(cdiv(NT, 8)):
                groups.append((fc, half, list(range(half * 8, min((half + 1) * 8, NT)))))

        def mms(t_, it, buf):
            fc, half, tts = it
            li = None
            for j, tt in enumerate(tts):
                li = t_.transpose(buf[:, j * 128:(j + 1) * 128],
                                  src_tm[:, tt * E + fc * 128: tt * E + (fc + 1) * 128],
                                  self.idb[:])
            return li

        def ev(v, it, buf):
            fc, half, tts = it
            n = len(tts) * 128
            return v.tensor_copy(out=dst_fm[:, fc * T + half * 1024: fc * T + half * 1024 + n],
                                 in_=buf[:, 0:n])

        self.mm_evac(groups, mms, ev, [pa, pb])

    def ln_stats(self, src_fn, n_tiles):
        with self.nc.Block() as blk:
            @blk.vector
            def _(v):
                for t in range(n_tiles):
                    ap = src_fn(t)
                    v.bn_stats(out=self.lnstat[:, 12 * t:12 * t + 6], in_=ap[:, 0:512])
                    v.bn_stats(out=self.lnstat[:, 12 * t + 6:12 * t + 12], in_=ap[:, 512:1024])

    def ln_finish(self, n_tiles):
        nc = self.nc
        mv2 = self.lnmv[:].rearrange("p (t two) -> p t two", two=2)
        ab2 = self.lnab[:].rearrange("p (t two) -> p t two", two=2)
        with nc.Block() as blk:
            @blk.vector
            def _(v):
                for t in range(n_tiles):
                    i = v.bn_aggr(out=self.lnmv[:, 2 * t:2 * t + 2],
                                  in_=self.lnstat[:, 12 * t:12 * t + 12])
                self.edge(v, i)
                v.tensor_scalar(out=mv2[:, 0:n_tiles, 1:2], in0=mv2[:, 0:n_tiles, 1:2],
                                scalar1=EPS, scalar2=None, op0=ALU.add)
        with nc.Block() as blk:
            @blk.scalar
            def _(sc):
                i1 = sc.activation(out=ab2[:, 0:n_tiles, 0:1], in_=mv2[:, 0:n_tiles, 1:2],
                                   func=AF.Ln, bias=0.0, scale=1.0)
                self.edge(sc, i1, "sSo")
                sc.activation(out=ab2[:, 0:n_tiles, 0:1], in_=ab2[:, 0:n_tiles, 0:1],
                              func=AF.Exp, bias=0.0, scale=-0.5)
        with nc.Block() as blk:
            @blk.vector
            def _(v):
                i1 = v.tensor_tensor(out=ab2[:, 0:n_tiles, 1:2], in0=mv2[:, 0:n_tiles, 0:1],
                                     in1=ab2[:, 0:n_tiles, 0:1], op=ALU.mult)
                self.edge(v, i1)
                v.tensor_scalar(out=ab2[:, 0:n_tiles, 1:2], in0=ab2[:, 0:n_tiles, 1:2],
                                scalar1=-1.0, scalar2=None, op0=ALU.mult)

    def allreduce_max(self, n):
        nc = self.nc
        with nc.Block() as blk:
            @blk.gpsimd
            def _(g):
                self.mark(g.dma_start(out=self.cc_in.ap()[0:1, 0:8],
                                      in_=self.srow[0:1, 0:8]), "sG", 16)
                g.wait_ge(self.H("sG"), self.V("sG"))
                self.mark(g.collective_compute("AllReduce", ALU.max,
                                               replica_groups=[list(range(NCORES))],
                                               ins=[self.cc_in.ap().opt()],
                                               outs=[self.cc_out.ap().opt()]), "sC", 1)
                g.wait_ge(self.H("sC"), self.V("sC"))
                self.mark(g.dma_start(out=self.srow[0:1, 0:n],
                                      in_=self.cc_out.ap()[0:1, 0:n]), "sG", 16)
                g.wait_ge(self.H("sG"), self.V("sG"))

    def bcast_scalars(self, pairs):
        nc = self.nc
        with nc.Block() as blk:
            @blk.gpsimd
            def _(g):
                for i, (sc_, _dc) in enumerate(pairs):
                    self.mark(g.dma_start(out=self.bc_d.ap()[0:1, i:i + 1],
                                          in_=self.srow[0:1, sc_:sc_ + 1]), "sG", 16)
                g.wait_ge(self.H("sG"), self.V("sG"))
                for i, (_sc, dc) in enumerate(pairs):
                    self.mark(g.dma_start(out=self.scal[:, dc:dc + 1],
                                          in_=self.bc_d.ap()[0:1, i:i + 1].broadcast_to((128, 1))),
                              "sG", 16)
                g.wait_ge(self.H("sG"), self.V("sG"))

    def global_absmax(self, pcol_ap, srow_col):
        with self.nc.Block() as blk:
            @blk.gpsimd
            def _(g):
                g.tensor_reduce(out=self.srow[0:1, srow_col:srow_col + 1], in_=pcol_ap,
                                axis=AX.XYZWC, op=ALU.max, apply_absolute_value=True)

    def emit_round(self, v, in_ap, c_ap, out_ap, y_ap, m_ap=None):
        i1 = v.tensor_scalar(out=y_ap, in0=in_ap, scalar1=c_ap, scalar2=MAGIC,
                             op0=ALU.mult, op1=ALU.add)
        self.edge(v, i1)
        return v.tensor_scalar(out=out_ap, in0=y_ap, scalar1=MAGIC, scalar2=None,
                               op0=ALU.subtract)

    def build(self):
        nc, T, NT = self.nc, self.T, self.NT
        n512 = T // 512
        d = nc.dram_tensor
        x_in = d("x", [T, E], F32, kind="ExternalInput")
        wq_d = d("wq", [E, H * DH], F32, kind="ExternalInput")
        wqr_d = d("wqr", [E, H * DR], F32, kind="ExternalInput")
        wkr_d = d("wkr", [E, DR], F32, kind="ExternalInput")
        wkvd_d = d("wkvd", [E, DKV], F32, kind="ExternalInput")
        wku_d = d("wku", [DKV, H * DH], F32, kind="ExternalInput")
        wvu_d = d("wvu", [DKV, H * DH], F32, kind="ExternalInput")
        wo_d = d("wo", [E, E], F32, kind="ExternalInput")
        lobo_d = d("lobo", [H], F32, kind="ExternalInput")
        wup_d = d("wup", [I3, E], F32, kind="ExternalInput")
        wgate_d = d("wgate", [I3, E], F32, kind="ExternalInput")
        wdown_d = d("wdown", [E, I3], F32, kind="ExternalInput")
        identb_d = d("identb", [128, 128], BF16, kind="ExternalInput")
        identf_d = d("identf", [128, 128], F32, kind="ExternalInput")
        cosT_d = d("cosT", [128, T], F32, kind="ExternalInput")
        sinT_d = d("sinT", [128, T], F32, kind="ExternalInput")
        iota_d = d("iota128", [128, 1], F32, kind="ExternalInput")
        sel_d = d("sel16", [16, 16 * 64], F32, kind="ExternalInput")
        cmask_d = d("cmask", [128, 128], F32, kind="ExternalInput")
        out_d = d("out", [T, E], F32, kind="ExternalOutput")
        r_spill = d("r_spill", [T, E], F32)
        u_spill = d("u_spill", [I3, T], BF16)
        y_spill = d("y_spill", [H * 64, T], BF16)
        g_spill = d("g_spill", [I3, T], BF16)
        self.cc_in = d("cc_in", [1, 8], F32)
        self.cc_out = d("cc_out", [1, 8], F32)
        self.bc_d = d("bc_d", [1, 16], F32)

        for s in ["sD", "sG", "sC", "sMM", "sEV", "sEX", "sMK", "sTM", "sAV",
                  "sFZ", "sVo", "sSo", "sDQ"]:
            self.sem(s)

        rs = self.root
        self.idb = self.sb(rs, "idb", [128, 128], BF16)
        self.idf = self.sb(rs, "idf", [128, 128], F32)
        self.iota_sb = self.sb(rs, "iota", [128, 1], F32)
        self.c128_sb = self.sb(rs, "c128", [128, 1], F32)
        self.ones_b = self.sb(rs, "onesb", [128, 128], BF16)
        self.ones_f = self.sb(rs, "onesf", [128, 64], F32)
        self.lnstat = self.sb(rs, "lnstat", [128, 192], F32)
        self.lnmv = self.sb(rs, "lnmv", [128, 32], F32)
        self.lnab = self.sb(rs, "lnab", [128, 32], F32)
        self.scal = self.sb(rs, "scal", [128, 8], F32)
        self.srow = self.sb(rs, "srow", [1, 16], F32)
        self.crow_b = self.sb(rs, "crow_b", [1, 16], BF16)
        self.crep = self.sb(rs, "crep", [128, 16], F32)
        self.junk = self.sb(rs, "junk", [128, 1], F32)
        self.pstage = self.sb(rs, "pstage", [128, 32], F32)
        self.mstage = self.sb(rs, "mstage", [128, 16], F32)
        self.mq = self.sb(rs, "mq", [16, 128], F32)
        self.dq = self.sb(rs, "dq", [16, 128], F32)
        self.rq = self.sb(rs, "rq", [16, 128], F32)
        self.sel_f = self.sb(rs, "sel_f", [16, 16 * 64], F32)
        self.cmask = self.sb(rs, "cmask_sb", [128, 128], F32)

        att = ExitStack()
        cos_sb = self.sb(att, "cos_sb", [128, T], F32)
        sin_sb = self.sb(att, "sin_sb", [128, T], F32)
        wvu_sb = self.sb(att, "wvu_sb", [128, 2 * 1024], BF16)
        wq_g = self.sb(att, "wq_g", [128, 8 * 256], BF16)
        wku_g = self.sb(att, "wku_g", [128, 2 * 256], BF16)
        wqr_g = self.sb(att, "wqr_g", [128, 8 * 128], BF16)
        krr_a = self.sb(att, "krr_a", [16, T], BF16)
        krr_b = self.sb(att, "krr_b", [16, T], BF16)
        hp_q = self.sb(att, "hp_q", [64, 2 * T], BF16)
        rtmp = self.sb(att, "rtmp", [64, 2048], F32)
        s_fm = ExitStack()
        xn1_fm = self.sb(s_fm, "xn1_fm", [128, 8 * T], BF16)
        latent = self.sb(s_fm, "latent", [128, 2 * T], BF16)
        s_pro = ExitStack()
        wkvd_sb = self.sb(s_pro, "wkvd_sb", [128, 8 * 256], BF16)
        wkr_sb = self.sb(s_pro, "wkr_sb", [128, 8 * DR], BF16)
        s_rtm = ExitStack()
        r_tm = self.sb(s_rtm, "r_tm", [128, NT * E], F32)
        s_xn1t = ExitStack()
        xn1_tm = self.sb(s_xn1t, "xn1_tm", [128, NT * E], BF16, side="right")

        with nc.Block() as blk:
            @blk.sync
            def _(s):
                for dst, srcd in [(self.idb, identb_d), (self.idf, identf_d),
                                  (self.iota_sb, iota_d)]:
                    self.mark(s.dma_start(out=dst[:], in_=srcd.ap()), "sD", 16)
                self.mark(s.dma_start(out=r_tm[:].rearrange("p (t e) -> p t e", e=E),
                                      in_=x_in.ap().rearrange("(t p) e -> p t e", p=128)), "sD", 16)
                self.mark(s.dma_start(out=cos_sb[:], in_=cosT_d.ap()), "sD", 16)
                self.mark(s.dma_start(out=self.sel_f[:], in_=sel_d.ap()), "sD", 16)
                self.mark(s.dma_start(out=self.cmask[:], in_=cmask_d.ap()), "sD", 16)
                self.mark(s.dma_start(out=sin_sb[:], in_=sinT_d.ap()), "sD", 16)
            @blk.gpsimd
            def _(g):
                for dst, srcd, ch in [(wkvd_sb, wkvd_d, 8), (wkr_sb, wkr_d, 8),
                                      (wvu_sb, wvu_d, 2)]:
                    self.mark(g.dma_start(out=dst[:].rearrange("p (c f) -> p c f", c=ch),
                                          in_=srcd.ap().rearrange("(c p) f -> p c f", p=128)),
                              "sG", 16)
                self.mark(g.dma_start(out=self.srow[0:1, 0:H],
                                      in_=lobo_d.ap().unsqueeze(0)), "sG", 16)
            for name in ["vector", "scalar", "tensor"]:
                def wtr(eng):
                    eng.wait_ge(self.H("sD"), self.V("sD"))
                    eng.wait_ge(self.H("sG"), self.V("sG"))
                getattr(blk, name)(wtr)
            @blk.vector
            def _(v):
                v.memset(self.c128_sb[:], 128.0)
                v.memset(self.ones_f[:], 1.0)
                v.memset(self.ones_b[:], 1.0)

        self.ln_stats(lambda t: r_tm[:, t * E:(t + 1) * E], NT)
        self.ln_finish(NT)
        with nc.Block() as blk:
            @blk.scalar
            def _(sc):
                for t in range(NT):
                    sc.activation(out=xn1_tm[:, t * E:(t + 1) * E],
                                  in_=r_tm[:, t * E:(t + 1) * E], func=AF.Identity,
                                  scale=self.lnab[:, 2 * t:2 * t + 1],
                                  bias=self.lnab[:, 2 * t + 1:2 * t + 2])
        with nc.Block() as blk:
            @blk.sync
            def _(s):
                self.mark(s.dma_start(out=r_spill.ap().rearrange("(t p) e -> p t e", p=128),
                                      in_=r_tm[:].rearrange("p (t e) -> p t e", e=E)), "sD", 16)
                s.wait_ge(self.H("sD"), self.V("sD"))
        s_rtm.close()

        pstr = ExitStack()
        pa = self.psum(pstr, "psA", [128, 1024], BF16)
        pb = self.psum(pstr, "psB", [128, 1024], BF16)
        self.transpose_tm_to_fm(xn1_tm, xn1_fm, pa, pb)
        pstr.close()
        s_xn1t.close()

        att_r = ExitStack()
        combQ = self.sb(att_r, "combQ", [128, 4 * T], BF16, side="right")
        combK = self.sb(att_r, "combK", [128, 4 * T], BF16, side="right")
        vaug = self.sb(att_r, "vaug", [128, NT * 4 * 65], BF16, side="right")
        E_sb = self.sb(att_r, "E_sb", [128, 2 * 512], BF16, side="right")
        runmax = self.sb(att_r, "runmax", [128, T], BF16, side="right")
        ystage = self.sb(att_r, "ystage", [64, T], BF16, side="right")
        drow = self.sb(att_r, "drow", [65, T], F32, side="right")

        php = ExitStack()
        ps_y = self.psum(php, "ps_y", [65, T], F32)
        ps_s0 = self.psum(php, "ps_s0", [128, 1024], BF16)
        ps_s1 = self.psum(php, "ps_s1", [128, 1024], BF16)
        ps_m = self.psum(php, "ps_m", [16, 128], F32)
        ps_r = self.psum(php, "ps_r", [64, 512], F32)
        ps_a = ps_s0.ap().bitcast(F32)
        ps_b = ps_s1.ap().bitcast(F32)

        with nc.Block() as blk:
            @blk.scalar
            def _(sc):
                sc.activation(out=self.srow[0:1, 0:H], in_=self.srow[0:1, 0:H],
                              func=AF.Exp, bias=0.0, scale=1.0)
        with nc.Block() as blk:
            @blk.vector
            def _(v):
                v.tensor_copy(out=self.crow_b[0:1, 0:H], in_=self.srow[0:1, 0:H])
        with nc.Block() as blk:
            @blk.tensor
            def _(t_):
                t_.matmul(ps_a[0:128, 0:16], self.ones_b[0:1, 0:128],
                          self.crow_b[0:1, 0:16], start=True, stop=True)
        with nc.Block() as blk:
            @blk.vector
            def _(v):
                v.tensor_copy(out=self.crep[:, 0:H], in_=ps_a[0:128, 0:16])
                v.memset(vaug[:].rearrange("p (t h c) -> p t h c", h=4, c=65)[:, :, :, 64:65], 1.0)

        def lat_mm(t_, it, buf):
            c, tc = it
            return self.acc_mm(t_, buf[0:128, 0:512],
                        [(wkvd_sb[:, ci * DKV + c * 128: ci * DKV + (c + 1) * 128],
                          xn1_fm[:, ci * T + tc * 512: ci * T + tc * 512 + 512])
                         for ci in range(8)])

        def lat_ev(v, it, buf):
            c, tc = it
            return v.tensor_copy(out=latent[:, c * T + tc * 512: c * T + tc * 512 + 512],
                                 in_=buf[0:128, 0:512])

        self.mm_evac([(c, tc) for c in range(2) for tc in range(n512)], lat_mm, lat_ev,
                     [ps_a, ps_b])

        def kr_mm(t_, tc, buf):
            self.acc_mm(t_, buf[0:16, 0:512],
                        [(wkr_sb[:, ci * DR: ci * DR + 16],
                          xn1_fm[:, ci * T + tc * 512: ci * T + tc * 512 + 512])
                         for ci in range(8)])
            return self.acc_mm(t_, buf[32:48, 0:512],
                        [(wkr_sb[:, ci * DR + 16: ci * DR + 32],
                          xn1_fm[:, ci * T + tc * 512: ci * T + tc * 512 + 512])
                         for ci in range(8)])

        def kr_ev(v, tc, buf):
            cs = cos_sb[0:16, tc * 512:(tc + 1) * 512]
            sn = sin_sb[0:16, tc * 512:(tc + 1) * 512]
            fh, sh = buf[0:16, 0:512], buf[32:48, 0:512]
            i1 = v.tensor_tensor(out=rtmp[0:16, 0:512], in0=sh, in1=sn, op=ALU.mult)
            i2 = v.tensor_tensor(out=rtmp[0:16, 512:1024], in0=fh, in1=sn, op=ALU.mult)
            i3 = v.tensor_tensor(out=rtmp[0:16, 1024:1536], in0=fh, in1=cs, op=ALU.mult)
            i4 = v.tensor_tensor(out=rtmp[0:16, 1536:2048], in0=sh, in1=cs, op=ALU.mult)
            self.edges(v, [i1, i2, i3, i4])
            v.tensor_sub(out=krr_a[0:16, tc * 512:(tc + 1) * 512],
                         in0=rtmp[0:16, 1024:1536], in1=rtmp[0:16, 0:512])
            return v.tensor_add(out=krr_b[0:16, tc * 512:(tc + 1) * 512],
                                in0=rtmp[0:16, 1536:2048], in1=rtmp[0:16, 512:1024])

        self.mm_evac(list(range(n512)), kr_mm, kr_ev, [ps_a, ps_b])
        s_pro.close()

        with nc.Block() as blk:
            @blk.vector
            def _(v):
                v.memset(runmax[:], 0.0)

        for grp in range(4):
            with nc.Block() as blk:
                @blk.gpsimd
                def _(g, grp=grp):
                    self.mark(g.dma_start(
                        out=wq_g[:].rearrange("p (c f) -> p c f", c=8),
                        in_=wq_d.ap().rearrange("(c p) f -> p c f", p=128)[:, :, grp * 256:(grp + 1) * 256]),
                        "sG", 16)
                    self.mark(g.dma_start(
                        out=wku_g[:].rearrange("p (c f) -> p c f", c=2),
                        in_=wku_d.ap().rearrange("(c p) f -> p c f", p=128)[:, :, grp * 256:(grp + 1) * 256]),
                        "sG", 16)
                    fh0 = (grp // 2) * 128 + (grp % 2) * 64
                    sh0 = (2 + grp // 2) * 128 + (grp % 2) * 64
                    self.mark(g.dma_start(
                        out=wqr_g[:].rearrange("p (c f) -> p c f", c=8)[:, :, 0:64],
                        in_=wqr_d.ap().rearrange("(c p) f -> p c f", p=128)[:, :, fh0:fh0 + 64]),
                        "sG", 16)
                    self.mark(g.dma_start(
                        out=wqr_g[:].rearrange("p (c f) -> p c f", c=8)[:, :, 64:128],
                        in_=wqr_d.ap().rearrange("(c p) f -> p c f", p=128)[:, :, sh0:sh0 + 64]),
                        "sG", 16)
                    g.wait_ge(self.H("sG"), self.V("sG"))
                @blk.tensor
                def _(t_):
                    t_.wait_ge(self.H("sG"), self.V("sG"))

            def qc_mm(t_, it, buf):
                j, tc = it
                return self.acc_mm(t_, buf[0:64, 0:512],
                            [(wq_g[:, ci * 256 + j * 64: ci * 256 + (j + 1) * 64],
                              xn1_fm[:, ci * T + tc * 512: ci * T + tc * 512 + 512])
                             for ci in range(8)])

            def qc_ev(v, it, buf):
                j, tc = it
                return v.tensor_copy(out=combQ[0:64, j * T + tc * 512: j * T + tc * 512 + 512],
                                     in_=buf[0:64, 0:512])

            self.mm_evac([(j, tc) for j in range(4) for tc in range(n512)], qc_mm, qc_ev,
                         [ps_a, ps_b])

            def kc_mm(t_, it, buf):
                j, tc = it
                return self.acc_mm(t_, buf[0:64, 0:512],
                            [(wku_g[:, c * 256 + j * 64: c * 256 + (j + 1) * 64],
                              latent[:, c * T + tc * 512: c * T + tc * 512 + 512])
                             for c in range(2)])

            def kc_ev(v, it, buf):
                j, tc = it
                return v.tensor_copy(out=combK[0:64, j * T + tc * 512: j * T + tc * 512 + 512],
                                     in_=buf[0:64, 0:512])

            self.mm_evac([(j, tc) for j in range(4) for tc in range(n512)], kc_mm, kc_ev,
                         [ps_a, ps_b])

            def qr_mm(t_, it, buf):
                tc, part = it
                return self.acc_mm(t_, buf[0:64, 0:512],
                            [(wqr_g[:, ci * 128 + part * 64: ci * 128 + part * 64 + 64],
                              xn1_fm[:, ci * T + tc * 512: ci * T + tc * 512 + 512])
                             for ci in range(8)])

            def qr_ev(v, it, buf):
                tc, part = it
                dst = rtmp[0:64, 0:512] if part == 0 else rtmp[0:64, 512:1024]
                i0 = v.tensor_copy(out=dst, in_=buf[0:64, 0:512])
                if part == 0:
                    return i0
                v.wait_ge(self.H("sEV"), self.V("sEV"))
                self.edge(v, i0)
                cs = cos_sb[0:64, tc * 512:(tc + 1) * 512]
                sn = sin_sb[0:64, tc * 512:(tc + 1) * 512]
                fh, sh = rtmp[0:64, 0:512], rtmp[0:64, 512:1024]
                i1 = v.tensor_tensor(out=rtmp[0:64, 1024:1536], in0=sh, in1=sn, op=ALU.mult)
                i2 = v.tensor_tensor(out=rtmp[0:64, 1536:2048], in0=fh, in1=sn, op=ALU.mult)
                self.edges(v, [i1, i2])
                i3 = v.tensor_tensor(out=fh, in0=fh, in1=cs, op=ALU.mult)
                i4 = v.tensor_tensor(out=sh, in0=sh, in1=cs, op=ALU.mult)
                self.edges(v, [i3, i4])
                v.tensor_sub(out=hp_q[0:64, tc * 512: tc * 512 + 512],
                             in0=fh, in1=rtmp[0:64, 1024:1536])
                return v.tensor_add(out=hp_q[0:64, T + tc * 512: T + tc * 512 + 512],
                                    in0=sh, in1=rtmp[0:64, 1536:2048])

            self.mm_evac([(tc, part) for tc in range(n512) for part in range(2)],
                         qr_mm, qr_ev, [ps_a, ps_b])

            with nc.Block() as blk:
                @blk.sync
                def _(s):
                    for j in range(4):
                        self.mark(s.dma_start(out=combQ[64:80, j * T:(j + 1) * T],
                                              in_=hp_q[j * 16:(j + 1) * 16, 0:T]), "sD", 16)
                        self.mark(s.dma_start(out=combQ[80:96, j * T:(j + 1) * T],
                                              in_=hp_q[j * 16:(j + 1) * 16, T:2 * T]), "sD", 16)
                        self.mark(s.dma_start(out=combK[64:80, j * T:(j + 1) * T],
                                              in_=krr_a[0:16, 0:T]), "sD", 16)
                        self.mark(s.dma_start(out=combK[80:96, j * T:(j + 1) * T],
                                              in_=krr_b[0:16, 0:T]), "sD", 16)
                    s.wait_ge(self.H("sD"), self.V("sD"))

            def v_mm(t_, tt, buf):
                return self.acc_mm(t_, buf[0:128, 0:256],
                            [(latent[:, c * T + tt * 128: c * T + (tt + 1) * 128],
                              wvu_sb[:, c * 1024 + grp * 256: c * 1024 + (grp + 1) * 256])
                             for c in range(2)])

            def v_ev(v, tt, buf):
                return v.tensor_copy(
                    out=vaug[:, tt * 260:(tt + 1) * 260].rearrange("p (h c) -> p h c", c=65)[:, :, 0:64],
                    in_=buf[0:128, 0:256].rearrange("p (h c) -> p h c", c=64))

            self.mm_evac(list(range(NT)), v_mm, v_ev, [ps_a, ps_b])

            for j in range(4):
                self.one_head(4 * grp + j, j, combQ, combK, vaug, E_sb, runmax,
                              ystage, drow, y_spill,
                              ps_y, [ps_s0, ps_s1], ps_m, ps_r)

        php.close()
        att_r.close()
        s_fm.close()
        att.close()

        s_heads = ExitStack()
        wo_sb = self.sb(s_heads, "wo_sb", [128, 8 * 1024], BF16)
        s_wo = ExitStack()
        r_tm2 = self.sb(s_wo, "r_tm2", [128, NT * E], F32, side="right")
        attn_tm = self.sb(s_wo, "attn_tm", [128, NT * E], BF16, side="right")
        s_afm = ExitStack()
        attn_fm = self.sb(s_afm, "attn_fm", [128, 8 * T], BF16, side="right")
        s_yfm = ExitStack()
        y_fm = self.sb(s_yfm, "y_fm", [128, 8 * T], BF16, side="right")

        with nc.Block() as blk:
            @blk.sync
            def _(s):
                self.mark(s.dma_start(out=r_tm2[:].rearrange("p (t e) -> p t e", e=E),
                                      in_=r_spill.ap().rearrange("(t p) e -> p t e", p=128)),
                          "sD", 16)
                self.mark(s.dma_start(out=y_fm[:].rearrange("p (c t) -> p c t", c=8),
                                      in_=y_spill.ap().rearrange("(c p) t -> p c t", p=128)),
                          "sD", 16)
            @blk.gpsimd
            def _(g):
                self.mark(g.dma_start(out=wo_sb[:].rearrange("p (c f) -> p c f", c=8),
                                      in_=wo_d.ap().rearrange("(c p) f -> p c f", p=128)), "sG", 16)
            @blk.tensor
            def _(t_):
                t_.wait_ge(self.H("sG"), self.V("sG"))
                t_.wait_ge(self.H("sD"), self.V("sD"))

        pwo = ExitStack()
        wps_a = self.psum(pwo, "wps_a", [128, 512], F32)
        wps_b = self.psum(pwo, "wps_b", [128, 512], F32)

        def wo_mm(t_, it, buf):
            oc, tc = it
            return self.acc_mm(t_, buf[0:128, 0:512],
                        [(wo_sb[:, ci * 1024 + oc * 128: ci * 1024 + (oc + 1) * 128],
                          y_fm[:, ci * T + tc * 512: ci * T + tc * 512 + 512])
                         for ci in range(8)])

        def wo_ev(v, it, buf):
            oc, tc = it
            return v.tensor_copy(out=attn_fm[:, oc * T + tc * 512: oc * T + tc * 512 + 512],
                          in_=buf[0:128, 0:512])

        self.mm_evac([(oc, tc) for oc in range(8) for tc in range(n512)], wo_mm, wo_ev,
                     [wps_a, wps_b])
        pwo.close()
        s_yfm.close()
        s_heads.close()

        ptw = ExitStack()
        pa2 = self.psum(ptw, "wpsA", [128, 1024], BF16)
        pb2 = self.psum(ptw, "wpsB", [128, 1024], BF16)

        def fm2tm_mm(t_, tt, buf):
            for fc in range(8):
                li = t_.transpose(buf[:, fc * 128:(fc + 1) * 128],
                                  attn_fm[:, fc * T + tt * 128: fc * T + (tt + 1) * 128],
                                  self.idb[:])
            return li

        def fm2tm_ev(v, tt, buf):
            return v.tensor_copy(out=attn_tm[:, tt * E:(tt + 1) * E], in_=buf[:, 0:1024])

        self.mm_evac(list(range(NT)), fm2tm_mm, fm2tm_ev, [pa2, pb2])
        ptw.close()
        s_afm.close()

        self.ln_stats(lambda t: attn_tm[:, t * E:(t + 1) * E], NT)
        self.ln_finish(NT)
        with nc.Block() as blk:
            @blk.vector
            def _(v):
                v.wait_ge(self.H("sD"), self.V("sD"))
                for t in range(NT):
                    i = v.scalar_tensor_tensor(out=r_tm2[:, t * E:(t + 1) * E],
                                               in0=attn_tm[:, t * E:(t + 1) * E],
                                               scalar=self.lnab[:, 2 * t:2 * t + 1],
                                               in1=r_tm2[:, t * E:(t + 1) * E],
                                               op0=ALU.mult, op1=ALU.add)
                    self.edge(v, i)
                    i = v.tensor_scalar(out=r_tm2[:, t * E:(t + 1) * E],
                                        in0=r_tm2[:, t * E:(t + 1) * E],
                                        scalar1=self.lnab[:, 2 * t + 1:2 * t + 2],
                                        scalar2=None, op0=ALU.add)
                    self.edge(v, i)

        self.ln_stats(lambda t: r_tm2[:, t * E:(t + 1) * E], NT)
        self.ln_finish(NT)
        s_xn3 = ExitStack()
        xn3 = self.sb(s_xn3, "xn3", [128, NT * E], F32)
        ybuf = self.sb(s_xn3, "ybuf", [128, E], F32)
        mbuf = self.sb(s_xn3, "mbuf", [128, E], F32)
        with nc.Block() as blk:
            @blk.scalar
            def _(sc):
                for t in range(NT):
                    sc.activation(out=xn3[:, t * E:(t + 1) * E],
                                  in_=r_tm2[:, t * E:(t + 1) * E], func=AF.Identity,
                                  scale=self.lnab[:, 2 * t:2 * t + 1],
                                  bias=self.lnab[:, 2 * t + 1:2 * t + 2])
        with nc.Block() as blk:
            @blk.sync
            def _(s):
                self.mark(s.dma_start(out=r_spill.ap().rearrange("(t p) e -> p t e", p=128),
                                      in_=r_tm2[:].rearrange("p (t e) -> p t e", e=E)), "sD", 16)
                s.wait_ge(self.H("sD"), self.V("sD"))
        with nc.Block() as blk:
            @blk.vector
            def _(v):
                for t in range(NT):
                    i = v.tensor_reduce(out=self.pstage[:, t:t + 1],
                                        in_=xn3[:, t * E:(t + 1) * E],
                                        axis=AX.X, op=ALU.max, apply_absolute_value=True)
                self.edge(v, i)
                v.tensor_reduce(out=self.pstage[:, 16:17], in_=self.pstage[:, 0:NT],
                                axis=AX.X, op=ALU.max)
        self.global_absmax(self.pstage[:, 16:17], S_X0)
        self.allreduce_max(1)
        with nc.Block() as blk:
            @blk.vector
            def _(v):
                i = v.tensor_scalar(out=self.srow[0:1, S_S1:S_S1 + 1],
                                    in0=self.srow[0:1, S_X0:S_X0 + 1],
                                    scalar1=INVQ, scalar2=1e-12, op0=ALU.mult, op1=ALU.max)
                self.edge(v, i)
                v.reciprocal(out=self.srow[0:1, S_I1:S_I1 + 1],
                             in_=self.srow[0:1, S_S1:S_S1 + 1])
        self.bcast_scalars([(S_I1, B_I1)])

        s_wo.close()

        s_xqf = ExitStack()
        xq_fm = self.sb(s_xqf, "xq_fm", [128, 8 * T], BF16, side="right")
        s_xqt = ExitStack()
        xq_tm = self.sb(s_xqt, "xq_tm", [128, NT * E], BF16, side="right")
        with nc.Block() as blk:
            @blk.vector
            def _(v):
                v.wait_ge(self.H("sG"), self.V("sG"))
                for t in range(NT):
                    i = self.emit_round(v, xn3[:, t * E:(t + 1) * E], self.scal[:, B_I1:B_I1 + 1],
                                        xq_tm[:, t * E:(t + 1) * E], ybuf[:], mbuf[:])
                    self.edge(v, i)
        s_xn3.close()

        ffs = ExitStack()
        spill_stage = self.sb(ffs, "spill_stage", [128, 2 * T], BF16)
        wdqT_down = self.sb(ffs, "wdqT_down", [128, 24 * 1024], BF16)
        s_dq = ExitStack()
        dqst = self.sb(s_dq, "dqst", [128, I3 + 2 * (I3 // 32) + 2 * I3], F32)
        o_nat, o_s, o_inv, o_t, o_m = (0, I3, I3 + I3 // 32, I3 + 2 * (I3 // 32),
                                       2 * I3 + 2 * (I3 // 32))
        wdq_nat = self.sb(s_dq, "wdq_nat", [128, I3], BF16)

        ptx = ExitStack()
        pa3 = self.psum(ptx, "fpsA", [128, 1024], BF16)
        pb3 = self.psum(ptx, "fpsB", [128, 1024], BF16)
        self.transpose_tm_to_fm(xq_tm, xq_fm, pa3, pb3)
        ptx.close()
        s_xqt.close()

        def dequant_weight(w_d, wdqT, O, nI, tag=[0]):
            nblk = nI // 32
            noc = O // 128
            tag[0] += 1
            pdq = ExitStack()
            pqa = self.psum(pdq, "dqA%d" % tag[0], [128, 1024], BF16)
            pqb = self.psum(pdq, "dqB%d" % tag[0], [128, 1024], BF16)
            for oc in range(noc):
                with nc.Block() as blk:
                    @blk.sync
                    def _(s, oc=oc):
                        self.mark(s.dma_start(out=dqst[:, o_nat:o_nat + nI],
                                              in_=w_d.ap()[oc * 128:(oc + 1) * 128, :]), "sD", 16)
                        s.wait_ge(self.H("sD"), self.V("sD"))
                with nc.Block() as blk:
                    @blk.vector
                    def _(v, oc=oc):
                        nat = dqst[:, o_nat:o_nat + nI]
                        nat3 = nat.rearrange("p (b i) -> p b i", i=32)
                        i = v.tensor_reduce(out=dqst[:, o_s:o_s + nblk].unsqueeze(2),
                                            in_=nat3, axis=AX.X, op=ALU.max,
                                            apply_absolute_value=True)
                        self.edge(v, i)
                        i = v.tensor_scalar(out=dqst[:, o_s:o_s + nblk],
                                            in0=dqst[:, o_s:o_s + nblk],
                                            scalar1=1.0 / 7.0, scalar2=1e-12,
                                            op0=ALU.mult, op1=ALU.max)
                        self.edge(v, i)
                        i = v.reciprocal(out=dqst[:, o_inv:o_inv + nblk],
                                         in_=dqst[:, o_s:o_s + nblk])
                        self.edge(v, i)
                        inv_bc = dqst[:, o_inv:o_inv + nblk].unsqueeze(2).broadcast_to((128, nblk, 32))
                        s_bc = dqst[:, o_s:o_s + nblk].unsqueeze(2).broadcast_to((128, nblk, 32))
                        tb = dqst[:, o_t:o_t + nI]
                        mb = dqst[:, o_m:o_m + nI]
                        i = v.tensor_tensor(out=tb.rearrange("p (b i) -> p b i", i=32),
                                            in0=nat3, in1=inv_bc, op=ALU.mult)
                        self.edge(v, i)
                        i = v.tensor_scalar(out=mb, in0=tb, scalar1=MAGIC, scalar2=MAGIC,
                                            op0=ALU.add, op1=ALU.subtract)
                        self.edge(v, i)
                        v.tensor_tensor(out=wdq_nat[:, 0:nI].rearrange("p (b i) -> p b i", i=32),
                                        in0=mb.rearrange("p (b i) -> p b i", i=32),
                                        in1=s_bc, op=ALU.mult)

                nfc = nI // 128

                def tmm(t_, fcg, buf, oc=oc):
                    for j, fc in enumerate(fcg):
                        li = t_.transpose(buf[:, j * 128:(j + 1) * 128],
                                          wdq_nat[:, fc * 128:(fc + 1) * 128], self.idb[:])
                    return li

                def tev(v, fcg, buf, oc=oc):
                    for j, fc in enumerate(fcg):
                        li = v.tensor_copy(out=wdqT[:, fc * O + oc * 128: fc * O + oc * 128 + 128],
                                           in_=buf[:, j * 128:(j + 1) * 128])
                    return li

                fgroups = [list(range(gg, min(gg + 8, nfc))) for gg in range(0, nfc, 8)]
                self.mm_evac(fgroups, tmm, tev, [pqa, pqb])
            pdq.close()

        def big_mms(wdqT, spill_d, srow_dst, tag=[0]):
            noc = I3 // 128
            tag[0] += 1
            pps = ExitStack()
            fa = self.psum(pps, "ffa%d" % tag[0], [128, 512], F32)
            fb = self.psum(pps, "ffb%d" % tag[0], [128, 512], F32)
            items = [(oc, tc) for oc in range(noc) for tc in range(n512)]
            with nc.Block() as blk:
                mm0 = self.V("sMM")
                ev0 = self.V("sEV")
                tm0 = self.V("sTM")
                dg0 = self.V("sG")

                @blk.tensor
                def _(t_):
                    for k, (oc, tc) in enumerate(items):
                        if k >= 2:
                            t_.wait_ge(self.H("sEV"), ev0 + k - 1)
                        buf = (fa, fb)[k % 2]
                        li = self.acc_mm(t_, buf[0:128, 0:512],
                                         [(wdqT[:, fc * I3 + oc * 128: fc * I3 + (oc + 1) * 128],
                                           xq_fm[:, fc * T + tc * 512: fc * T + tc * 512 + 512])
                                          for fc in range(8)])
                        li.then_inc(self.H("sMM"), 1)
                self.inc("sMM", len(items))

                @blk.scalar
                def _(sc):
                    for k, (oc, tc) in enumerate(items):
                        sc.wait_ge(self.H("sMM"), mm0 + k + 1)
                        if oc >= 2 and tc == 0:
                            sc.wait_ge(self.H("sG"), dg0 + 16 * (oc - 1))
                            sc.wait_ge(self.H("sTM"), tm0 + oc - 1)
                        sc.activation(out=spill_stage[:, (oc % 2) * T + tc * 512:
                                                      (oc % 2) * T + tc * 512 + 512],
                                      in_=(fa, fb)[k % 2][0:128, 0:512], func=AF.Identity,
                                      bias=0.0, scale=1.0).then_inc(self.H("sEV"), 1)
                        self.inc("sEV")

                @blk.vector
                def _(v):
                    for oc in range(noc):
                        v.wait_ge(self.H("sEV"), ev0 + (oc + 1) * n512)
                        v.tensor_reduce(out=self.pstage[:, oc:oc + 1],
                                        in_=spill_stage[:, (oc % 2) * T:(oc % 2) * T + T],
                                        axis=AX.X, op=ALU.max,
                                        apply_absolute_value=True).then_inc(self.H("sTM"), 1)
                self.inc("sTM", noc)

                @blk.gpsimd
                def _(g):
                    for oc in range(noc):
                        g.wait_ge(self.H("sTM"), tm0 + oc + 1)
                        self.mark(g.dma_start(out=spill_d.ap()[oc * 128:(oc + 1) * 128, :],
                                              in_=spill_stage[:, (oc % 2) * T:(oc % 2) * T + T]),
                                  "sG", 16)
                        g.wait_ge(self.H("sG"), self.V("sG"))
            pps.close()
            with nc.Block() as blk:
                @blk.vector
                def _(v):
                    v.tensor_reduce(out=self.pstage[:, 24:25], in_=self.pstage[:, 0:noc],
                                    axis=AX.X, op=ALU.max)
            self.global_absmax(self.pstage[:, 24:25], srow_dst)

        s_wu = ExitStack()
        wdqT_up = self.sb(s_wu, "wdqT_up", [128, 8 * I3], BF16)
        dequant_weight(wup_d, wdqT_up, I3, E)
        big_mms(wdqT_up, u_spill, S_LU)
        s_wu.close()
        s_wg = ExitStack()
        wdqT_g = self.sb(s_wg, "wdqT_g", [128, 8 * I3], BF16)
        dequant_weight(wgate_d, wdqT_g, I3, E)
        big_mms(wdqT_g, g_spill, S_LG)
        s_wg.close()
        s_xqf.close()

        dequant_weight(wdown_d, wdqT_down, 1024, I3)
        s_dq.close()

        with nc.Block() as blk:
            @blk.vector
            def _(v):
                i = v.tensor_copy(out=self.srow[0:1, S_X0:S_X0 + 1],
                                  in_=self.srow[0:1, S_LU:S_LU + 1])
                i = v.tensor_copy(out=self.srow[0:1, S_X1:S_X1 + 1],
                                  in_=self.srow[0:1, S_LG:S_LG + 1])
        self.allreduce_max(2)
        with nc.Block() as blk:
            @blk.vector
            def _(v):
                i = v.tensor_scalar(out=self.srow[0:1, S_S2:S_S2 + 1],
                                    in0=self.srow[0:1, S_X0:S_X0 + 1],
                                    scalar1=self.srow[0:1, S_S1:S_S1 + 1], scalar2=None,
                                    op0=ALU.mult)
                self.edge(v, i)
                i = v.tensor_scalar(out=self.srow[0:1, S_S2:S_S2 + 1],
                                    in0=self.srow[0:1, S_S2:S_S2 + 1],
                                    scalar1=INVQ, scalar2=1e-12, op0=ALU.mult, op1=ALU.max)
                self.edge(v, i)
                i = v.reciprocal(out=self.srow[0:1, S_C2:S_C2 + 1],
                                 in_=self.srow[0:1, S_S2:S_S2 + 1])
                self.edge(v, i)
                i = v.tensor_scalar(out=self.srow[0:1, S_C2:S_C2 + 1],
                                    in0=self.srow[0:1, S_C2:S_C2 + 1],
                                    scalar1=self.srow[0:1, S_S1:S_S1 + 1], scalar2=None,
                                    op0=ALU.mult)
                self.edge(v, i)
                i = v.tensor_scalar(out=self.srow[0:1, S_S3:S_S3 + 1],
                                    in0=self.srow[0:1, S_X1:S_X1 + 1],
                                    scalar1=self.srow[0:1, S_S1:S_S1 + 1], scalar2=None,
                                    op0=ALU.mult)
                self.edge(v, i)
                i = v.tensor_scalar(out=self.srow[0:1, S_S3:S_S3 + 1],
                                    in0=self.srow[0:1, S_S3:S_S3 + 1],
                                    scalar1=INVQ, scalar2=1e-12, op0=ALU.mult, op1=ALU.max)
                self.edge(v, i)
                i = v.reciprocal(out=self.srow[0:1, S_C3:S_C3 + 1],
                                 in_=self.srow[0:1, S_S3:S_S3 + 1])
                self.edge(v, i)
                i = v.tensor_scalar(out=self.srow[0:1, S_C3:S_C3 + 1],
                                    in0=self.srow[0:1, S_C3:S_C3 + 1],
                                    scalar1=self.srow[0:1, S_S1:S_S1 + 1], scalar2=None,
                                    op0=ALU.mult)
        self.bcast_scalars([(S_C2, B_C2), (S_C3, B_C3), (S_S3, B_S3)])

        s_gs = ExitStack()
        gsb = self.sb(s_gs, "gsb", [128, 24 * T], BF16)
        s_gsr = ExitStack()
        fy = self.sb(s_gsr, "fy", [128, T], F32, side="right")
        fm_ = self.sb(s_gsr, "fm_", [128, T], F32, side="right")
        fsig = self.sb(s_gsr, "fsig", [128, T], F32, side="right")

        def readback(spill_d, ic, dst):
            with nc.Block() as blk:
                @blk.sync
                def _(s):
                    self.mark(s.dma_start(out=dst,
                                          in_=spill_d.ap()[ic * 128:(ic + 1) * 128, :]), "sD", 16)
                    s.wait_ge(self.H("sD"), self.V("sD"))

        for ic in range(I3 // 128):
            readback(g_spill, ic, spill_stage[:, 0:T])
            with nc.Block() as blk:
                @blk.vector
                def _(v, ic=ic):
                    i = self.emit_round(v, spill_stage[:, 0:T], self.scal[:, B_C3:B_C3 + 1],
                                        gsb[:, ic * T:(ic + 1) * T], fy[:], fm_[:])
                    i.then_inc(self.H("sDQ"), 1)
                self.inc("sDQ")
                @blk.scalar
                def _(sc, ic=ic):
                    sc.wait_ge(self.H("sDQ"), self.V("sDQ"))
                    sc.activation(out=fsig[:], in_=gsb[:, ic * T:(ic + 1) * T],
                                  func=AF.Sigmoid, bias=0.0,
                                  scale=self.scal[:, B_S3:B_S3 + 1]).then_inc(self.H("sSo"), 1)
                    self.inc("sSo")
            with nc.Block() as blk:
                @blk.vector
                def _(v, ic=ic):
                    i = v.tensor_scalar(out=fy[:], in0=fsig[:], scalar1=65536.0,
                                        scalar2=65535.0, op0=ALU.mult, op1=ALU.min)
                    self.edge(v, i)
                    i = v.tensor_scalar(out=fm_[:], in0=fy[:], scalar1=MAGIC, scalar2=MAGIC,
                                        op0=ALU.add, op1=ALU.subtract)
                    self.edge(v, i)
                    i = v.tensor_copy(out=fy[:], in_=fm_[:])
                    self.edge(v, i)
                    i = v.tensor_tensor(out=gsb[:, ic * T:(ic + 1) * T],
                                        in0=gsb[:, ic * T:(ic + 1) * T], in1=fy[:], op=ALU.mult)
                    self.edge(v, i)
                    v.tensor_reduce(out=self.pstage[:, ic:ic + 1],
                                    in_=gsb[:, ic * T:(ic + 1) * T],
                                    axis=AX.X, op=ALU.max, apply_absolute_value=True)
        with nc.Block() as blk:
            @blk.vector
            def _(v):
                v.tensor_reduce(out=self.pstage[:, 24:25], in_=self.pstage[:, 0:24],
                                axis=AX.X, op=ALU.max)
        self.global_absmax(self.pstage[:, 24:25], S_X0)
        self.allreduce_max(1)
        with nc.Block() as blk:
            @blk.vector
            def _(v):
                i = v.tensor_scalar(out=self.srow[0:1, S_S4:S_S4 + 1],
                                    in0=self.srow[0:1, S_X0:S_X0 + 1],
                                    scalar1=self.srow[0:1, S_S3:S_S3 + 1], scalar2=None,
                                    op0=ALU.mult)
                self.edge(v, i)
                i = v.tensor_scalar(out=self.srow[0:1, S_S4:S_S4 + 1],
                                    in0=self.srow[0:1, S_S4:S_S4 + 1],
                                    scalar1=INVQ / 65536.0, scalar2=1e-12,
                                    op0=ALU.mult, op1=ALU.max)
                self.edge(v, i)
                i = v.reciprocal(out=self.srow[0:1, S_C4:S_C4 + 1],
                                 in_=self.srow[0:1, S_S4:S_S4 + 1])
                self.edge(v, i)
                i = v.tensor_scalar(out=self.srow[0:1, S_C4:S_C4 + 1],
                                    in0=self.srow[0:1, S_C4:S_C4 + 1],
                                    scalar1=self.srow[0:1, S_S3:S_S3 + 1], scalar2=1.0 / 65536.0,
                                    op0=ALU.mult, op1=ALU.mult)
        self.bcast_scalars([(S_C4, B_C4)])

        for ic in range(I3 // 128):
            readback(u_spill, ic, spill_stage[:, 0:T])
            with nc.Block() as blk:
                @blk.vector
                def _(v, ic=ic):
                    i = self.emit_round(v, gsb[:, ic * T:(ic + 1) * T], self.scal[:, B_C4:B_C4 + 1],
                                        gsb[:, ic * T:(ic + 1) * T], fy[:], fm_[:])
                    self.edge(v, i)
                    i = self.emit_round(v, spill_stage[:, 0:T], self.scal[:, B_C2:B_C2 + 1],
                                        spill_stage[:, T:2 * T], fy[:], fm_[:])
                    self.edge(v, i)
                    i = v.tensor_tensor(out=gsb[:, ic * T:(ic + 1) * T],
                                        in0=gsb[:, ic * T:(ic + 1) * T],
                                        in1=spill_stage[:, T:2 * T], op=ALU.mult)
                    self.edge(v, i)
                    v.tensor_reduce(out=self.pstage[:, ic:ic + 1],
                                    in_=gsb[:, ic * T:(ic + 1) * T],
                                    axis=AX.X, op=ALU.max, apply_absolute_value=True)
        with nc.Block() as blk:
            @blk.vector
            def _(v):
                v.tensor_reduce(out=self.pstage[:, 24:25], in_=self.pstage[:, 0:24],
                                axis=AX.X, op=ALU.max)
        self.global_absmax(self.pstage[:, 24:25], S_X0)
        self.allreduce_max(1)
        with nc.Block() as blk:
            @blk.vector
            def _(v):
                i = v.tensor_scalar(out=self.srow[0:1, S_S5:S_S5 + 1],
                                    in0=self.srow[0:1, S_X0:S_X0 + 1],
                                    scalar1=self.srow[0:1, S_S4:S_S4 + 1], scalar2=None,
                                    op0=ALU.mult)
                self.edge(v, i)
                i = v.tensor_scalar(out=self.srow[0:1, S_S5:S_S5 + 1],
                                    in0=self.srow[0:1, S_S5:S_S5 + 1],
                                    scalar1=self.srow[0:1, S_S2:S_S2 + 1], scalar2=None,
                                    op0=ALU.mult)
                self.edge(v, i)
                i = v.tensor_scalar(out=self.srow[0:1, S_S5:S_S5 + 1],
                                    in0=self.srow[0:1, S_S5:S_S5 + 1],
                                    scalar1=INVQ, scalar2=1e-12, op0=ALU.mult, op1=ALU.max)
                self.edge(v, i)
                i = v.reciprocal(out=self.srow[0:1, S_C5:S_C5 + 1],
                                 in_=self.srow[0:1, S_S5:S_S5 + 1])
                self.edge(v, i)
                i = v.tensor_scalar(out=self.srow[0:1, S_C5:S_C5 + 1],
                                    in0=self.srow[0:1, S_C5:S_C5 + 1],
                                    scalar1=self.srow[0:1, S_S4:S_S4 + 1], scalar2=None,
                                    op0=ALU.mult)
                self.edge(v, i)
                i = v.tensor_scalar(out=self.srow[0:1, S_C5:S_C5 + 1],
                                    in0=self.srow[0:1, S_C5:S_C5 + 1],
                                    scalar1=self.srow[0:1, S_S2:S_S2 + 1], scalar2=None,
                                    op0=ALU.mult)
        self.bcast_scalars([(S_C5, B_C5), (S_S5, B_S5)])

        with nc.Block() as blk:
            @blk.vector
            def _(v):
                v.wait_ge(self.H("sG"), self.V("sG"))
                for ic in range(I3 // 128):
                    i = self.emit_round(v, gsb[:, ic * T:(ic + 1) * T],
                                        self.scal[:, B_C5:B_C5 + 1],
                                        gsb[:, ic * T:(ic + 1) * T], fy[:], fm_[:])
                    self.edge(v, i)

        s_gsr.close()
        s_fout = ExitStack()
        ffn_fm = self.sb(s_fout, "ffn_fm", [128, 8 * T], BF16, side="right")
        pdn = ExitStack()
        da = self.psum(pdn, "dna", [128, 512], F32)
        db = self.psum(pdn, "dnb", [128, 512], F32)

        def dn_mm(t_, it, buf):
            oc, tc = it
            return self.acc_mm(t_, buf[0:128, 0:512],
                        [(wdqT_down[:, ic * 1024 + oc * 128: ic * 1024 + (oc + 1) * 128],
                          gsb[:, ic * T + tc * 512: ic * T + tc * 512 + 512])
                         for ic in range(I3 // 128)])

        def dn_ev(sc, it, buf):
            oc, tc = it
            return sc.activation(out=ffn_fm[:, oc * T + tc * 512: oc * T + tc * 512 + 512],
                          in_=buf[0:128, 0:512], func=AF.Identity, bias=0.0,
                          scale=self.scal[:, B_S5:B_S5 + 1])

        self.mm_evac([(oc, tc) for oc in range(8) for tc in range(n512)], dn_mm, dn_ev,
                     [da, db], evac_eng="scalar")
        pdn.close()
        s_gs.close()
        ffs.close()

        ffn_tm = self.sb(s_fout, "ffn_tm", [128, NT * E], BF16)
        ptf = ExitStack()
        fa2 = self.psum(ptf, "ftA", [128, 1024], BF16)
        fb2 = self.psum(ptf, "ftB", [128, 1024], BF16)

        def f2t_mm(t_, tt, buf):
            for fc in range(8):
                li = t_.transpose(buf[:, fc * 128:(fc + 1) * 128],
                                  ffn_fm[:, fc * T + tt * 128: fc * T + (tt + 1) * 128],
                                  self.idb[:])
            return li

        def f2t_ev(v, tt, buf):
            return v.tensor_copy(out=ffn_tm[:, tt * E:(tt + 1) * E], in_=buf[:, 0:1024])

        self.mm_evac(list(range(NT)), f2t_mm, f2t_ev, [fa2, fb2])
        ptf.close()

        s_r3 = ExitStack()
        r_tm3 = self.sb(s_r3, "r_tm3", [128, NT * E], F32)
        with nc.Block() as blk:
            @blk.sync
            def _(s):
                self.mark(s.dma_start(out=r_tm3[:].rearrange("p (t e) -> p t e", e=E),
                                      in_=r_spill.ap().rearrange("(t p) e -> p t e", p=128)),
                          "sD", 16)
        self.ln_stats(lambda t: ffn_tm[:, t * E:(t + 1) * E], NT)
        self.ln_finish(NT)
        with nc.Block() as blk:
            @blk.vector
            def _(v):
                v.wait_ge(self.H("sD"), self.V("sD"))
                for t in range(NT):
                    i = v.scalar_tensor_tensor(out=r_tm3[:, t * E:(t + 1) * E],
                                               in0=ffn_tm[:, t * E:(t + 1) * E],
                                               scalar=self.lnab[:, 2 * t:2 * t + 1],
                                               in1=r_tm3[:, t * E:(t + 1) * E],
                                               op0=ALU.mult, op1=ALU.add)
                    self.edge(v, i)
                    i = v.tensor_scalar(out=r_tm3[:, t * E:(t + 1) * E],
                                        in0=r_tm3[:, t * E:(t + 1) * E],
                                        scalar1=self.lnab[:, 2 * t + 1:2 * t + 2],
                                        scalar2=None, op0=ALU.add)
                    self.edge(v, i)
        with nc.Block() as blk:
            @blk.sync
            def _(s):
                self.mark(s.dma_start(out=out_d.ap().rearrange("(t p) e -> p t e", p=128),
                                      in_=r_tm3[:].rearrange("p (t e) -> p t e", e=E)), "sD", 16)
                s.wait_ge(self.H("sD"), self.V("sD"))
        s_r3.close()
        s_fout.close()
        self.root.close()
        return self.nc

    def one_head(self, h, j, combQ, combK, vaug, E_sb, runmax, ystage, drow,
                 y_spill, ps_y, ps_s, ps_m, ps_r):
        nc, T, NT = self.nc, self.T, self.NT
        items = []
        for kt in range(NT):
            off = kt * 128
            while off < T:
                n = min(512 - (off % 512), T - off)
                items.append((kt, off, n))
                off += n
        ns = len(items)
        last_kt_bank = {}
        for (kt, off, n) in items:
            b = off // 512
            last_kt_bank[b] = max(last_kt_bank.get(b, 0), kt)

        with nc.Block() as blk:
            mm0 = self.V("sMM")
            mk0 = self.V("sMK")
            ex0 = self.V("sEX")
            tm0 = self.V("sTM")
            av0 = self.V("sAV")

            def av_mm(t_, i):
                ktp, offp, np_ = items[i]
                t_.wait_ge(self.H("sEX"), ex0 + i + 1)
                t_.matmul(ps_y[0:65, offp:offp + np_],
                          vaug[:, ktp * 260 + j * 65: ktp * 260 + (j + 1) * 65],
                          E_sb[:, (i % 2) * 512: (i % 2) * 512 + np_],
                          start=(ktp == 0), stop=(ktp == last_kt_bank[offp // 512])
                          ).then_inc(self.H("sAV"), 1)

            @blk.tensor
            def _(t_):
                for i, (kt, off, n) in enumerate(items):
                    if i >= 2:
                        t_.wait_ge(self.H("sEX"), ex0 + i - 1)
                    sp = ps_s[i % 2].ap().bitcast(F32)
                    t_.matmul(sp[0:128, 0:n],
                              combK[0:96, j * T + kt * 128: j * T + (kt + 1) * 128],
                              combQ[0:96, j * T + off: j * T + off + n],
                              start=True, stop=True).then_inc(self.H("sMM"), 1)
                    if i > 0:
                        av_mm(t_, i - 1)
                av_mm(t_, ns - 1)
                for jq in range(NT):
                    i = ns + jq
                    t_.wait_ge(self.H("sEX"), ex0 + i - 1)
                    if jq == 0:
                        t_.wait_ge(self.H("sTM"), tm0 + ns)
                    t_.transpose(ps_s[i % 2][0:128, 0:128],
                                 runmax[:, jq * 128:(jq + 1) * 128],
                                 self.idb[:]).then_inc(self.H("sMM"), 1)
            self.inc("sMM", ns + NT)
            self.inc("sAV", ns)

            @blk.vector
            def _(v):
                def ttmax(i):
                    ktp, offp, np_ = items[i]
                    v.wait_ge(self.H("sEX"), ex0 + i + 1)
                    if i > 0:
                        v.wait_ge(self.H("sTM"), tm0 + i)
                    v.tensor_max(out=runmax[:, offp:offp + np_],
                                 in0=runmax[:, offp:offp + np_],
                                 in1=E_sb[:, (i % 2) * 512: (i % 2) * 512 + np_]
                                 ).then_inc(self.H("sTM"), 1)

                nd = 0
                for i, (kt, off, n) in enumerate(items):
                    if off == kt * 128:
                        v.wait_ge(self.H("sMM"), mm0 + i + 1)
                        nd += 1
                        sp = ps_s[i % 2].ap().bitcast(F32)
                        iv = v.tensor_tensor(
                            out=sp[0:128, 0:128], in0=sp[0:128, 0:128],
                            in1=self.cmask[:, 0:128], op=ALU.add)
                        iv.then_inc(self.H("sMK"), 1)
                        self.inc("sMK")
                    if i > 0:
                        ttmax(i - 1)
                ttmax(ns - 1)
                self.inc("sTM", ns)
                v.wait_ge(self.H("sEX"), ex0 + ns)
                for jq in range(NT):
                    i = ns + jq
                    v.wait_ge(self.H("sMM"), mm0 + i + 1)
                    v.tensor_reduce(out=self.mstage[:, jq:jq + 1], in_=ps_s[i % 2][0:128, 0:128],
                                    axis=AX.X, op=ALU.max).then_inc(self.H("sEX"), 1)

            @blk.scalar
            def _(sc):
                ndiag = 0
                for i, (kt, off, n) in enumerate(items):
                    if off == kt * 128:
                        ndiag += 1
                        sc.wait_ge(self.H("sMK"), mk0 + ndiag)
                    else:
                        sc.wait_ge(self.H("sMM"), mm0 + i + 1)
                    if i >= 2:
                        sc.wait_ge(self.H("sTM"), tm0 + i - 1)
                        sc.wait_ge(self.H("sAV"), av0 + i - 1)
                    sp = ps_s[i % 2].ap().bitcast(F32)
                    sc.activation(out=E_sb[:, (i % 2) * 512: (i % 2) * 512 + n],
                                  in_=sp[0:128, 0:n], func=AF.Exp, bias=0.0,
                                  scale=SCALE_ATT).then_inc(self.H("sEX"), 1)
                    self.inc("sEX")
            self.inc("sEX", NT)

        fz0 = self.V("sFZ")
        with nc.Block() as blk:
            @blk.tensor
            def _(t_):
                t_.transpose(ps_m[0:NT, 0:128], self.mstage[:, 0:NT],
                             self.idf[:]).then_inc(self.H("sFZ"), 1)
            self.inc("sFZ")
            @blk.scalar
            def _(sc):
                sc.activation(out=drow[64:65, 0:T], in_=ps_y[64:65, 0:T],
                              func=AF.Identity, bias=0.0, scale=1.0).then_inc(self.H("sFZ"), 1)
            self.inc("sFZ")
            @blk.vector
            def _(v):
                v.wait_ge(self.H("sFZ"), fz0 + 2)
                v.tensor_copy(out=self.mq[0:NT, 0:128], in_=ps_m[0:NT, 0:128])
            @blk.gpsimd
            def _(g):
                g.wait_ge(self.H("sFZ"), fz0 + 2)
                for qt in range(NT):
                    self.mark(g.dma_start(out=self.dq[qt:qt + 1, 0:128],
                                          in_=drow[64:65, qt * 128:(qt + 1) * 128]), "sG", 16)
                g.wait_ge(self.H("sG"), self.V("sG"))

        with nc.Block() as blk:
            @blk.vector
            def _(v):
                i = v.scalar_tensor_tensor(out=self.rq[0:NT, 0:128], in0=self.mq[0:NT, 0:128],
                                           scalar=self.crep[0:NT, h:h + 1], in1=self.dq[0:NT, 0:128],
                                           op0=ALU.mult, op1=ALU.add)
                self.edge(v, i)
                v.reciprocal(out=self.rq[0:NT, 0:128], in_=self.rq[0:NT, 0:128])

        n512 = T // 512
        nqt_grp = 4 if NT >= 4 else NT
        with nc.Block() as blk:
            mmb = self.V("sMM")
            fzb = self.V("sFZ")

            @blk.tensor
            def _(t_):
                for qg in range(n512):
                    if qg > 0:
                        t_.wait_ge(self.H("sFZ"), fzb + qg)
                    for jj in range(nqt_grp):
                        qt = nqt_grp * qg + jj
                        li = t_.matmul(ps_r[0:64, jj * 128:(jj + 1) * 128],
                                       self.sel_f[0:NT, qt * 64:(qt + 1) * 64],
                                       self.rq[0:NT, 0:128],
                                       start=(jj == 0), stop=(jj == nqt_grp - 1))
                    li.then_inc(self.H("sMM"), 1)
            self.inc("sMM", n512)

            @blk.vector
            def _(v):
                for qg in range(n512):
                    v.wait_ge(self.H("sMM"), mmb + qg + 1)
                    dst = ystage[0:64, qg * 512: qg * 512 + 512]
                    i = v.tensor_copy(out=dst, in_=ps_y[0:64, qg * 512: qg * 512 + 512])
                    self.edge(v, i)
                    v.tensor_tensor(out=dst, in0=dst, in1=ps_r[0:64, 0:512],
                                    op=ALU.mult).then_inc(self.H("sFZ"), 1)
            self.inc("sFZ", n512)

        with nc.Block() as blk:
            @blk.sync
            def _(s):
                self.mark(s.dma_start(out=y_spill.ap()[h * 64:(h + 1) * 64, :],
                                      in_=ystage[0:64, 0:T]), "sD", 16)
                s.wait_ge(self.H("sD"), self.V("sD"))
            if h < H - 1:
                @blk.vector
                def _(v):
                    v.memset(runmax[:], 0.0)


def _host_consts(T):
    half = DR // 2
    inv = (1.0 / 10000.0 ** (np.arange(half, dtype=np.float64) / half))
    ang = np.arange(T, dtype=np.float64)[None, :] * inv[:, None]
    cos16 = np.cos(ang)
    sin16 = np.sin(ang)
    cosT = np.tile(cos16, (8, 1)).astype(np.float32)
    sinT = np.tile(sin16, (8, 1)).astype(np.float32)
    identb = np.eye(128, dtype=np.float32).astype(ml_dtypes.bfloat16)
    identf = np.eye(128, dtype=np.float32)
    iota = np.arange(128, dtype=np.float32).reshape(128, 1)
    return cosT, sinT, identb, identf, iota


def _sel16():
    s = np.zeros((16, 16 * 64), dtype=np.float32)
    for qt in range(16):
        s[qt, qt * 64:(qt + 1) * 64] = 1.0
    return s


def _cmask():
    m = np.zeros((128, 128), dtype=np.float32)
    for k in range(128):
        m[k, :k] = -1e30
    return m


def _wqr_perm():
    perm = np.zeros(H * DR, dtype=np.int64)
    for hh in range(H):
        for halfsel in range(2):
            for i in range(16):
                tile = halfsel * 2 + hh // 8
                newc = tile * 128 + (hh % 8) * 16 + i
                perm[newc] = hh * 32 + halfsel * 16 + i
    return perm


_BUILD_CACHE = {}


def kernel(L0_attn_x, L0_attn_Wq, L0_attn_Wqr, L0_attn_Wkr, L0_attn_Wkvd,
           L0_attn_Wku, L0_attn_Wvu, L0_attn_Wo, L0_attn_lobo_log,
           L0_ffn_w_gate, L0_ffn_w_up, L0_ffn_w_down):
    T = 2048
    x = np.ascontiguousarray(np.asarray(L0_attn_x, dtype=np.float32))
    B = x.shape[0]
    cosT, sinT, identb, identf, iota = _host_consts(T)
    perm = _wqr_perm()
    wqr_p = np.ascontiguousarray(np.asarray(L0_attn_Wqr, dtype=np.float32)[:, perm])
    common = {
        "wq": np.ascontiguousarray(L0_attn_Wq, dtype=np.float32),
        "wqr": wqr_p,
        "wkr": np.ascontiguousarray(L0_attn_Wkr, dtype=np.float32),
        "wkvd": np.ascontiguousarray(L0_attn_Wkvd, dtype=np.float32),
        "wku": np.ascontiguousarray(L0_attn_Wku, dtype=np.float32),
        "wvu": np.ascontiguousarray(L0_attn_Wvu, dtype=np.float32),
        "wo": np.ascontiguousarray(L0_attn_Wo, dtype=np.float32),
        "lobo": np.ascontiguousarray(L0_attn_lobo_log, dtype=np.float32),
        "wup": np.ascontiguousarray(L0_ffn_w_up, dtype=np.float32),
        "wgate": np.ascontiguousarray(L0_ffn_w_gate, dtype=np.float32),
        "wdown": np.ascontiguousarray(L0_ffn_w_down, dtype=np.float32),
        "identb": identb, "identf": identf,
        "cosT": cosT, "sinT": sinT, "iota128": iota, "sel16": _sel16(),
        "cmask": _cmask(),
    }
    if T not in _BUILD_CACHE:
        _BUILD_CACHE[T] = KB(T).build()
    nc = _BUILD_CACHE[T]
    in_maps = [dict(common, x=np.ascontiguousarray(x[c])) for c in range(B)]
    res = run_bass_kernel_spmd(nc, in_maps, core_ids=list(range(B)))
    return np.stack([res.results[c]["out"] for c in range(B)], axis=0)


# version 06
# speedup vs baseline: 1.1570x; 1.1570x over previous
import math
from contextlib import ExitStack
import numpy as np
import ml_dtypes
import concourse.bass as bass
import concourse.mybir as mybir
from concourse.bass_utils import run_bass_kernel_spmd

F32 = mybir.dt.float32
BF16 = mybir.dt.bfloat16
AF = mybir.ActivationFunctionType
ALU = mybir.AluOpType
AX = mybir.AxisListType

NCORES = 8
E = 1024
H = 16
DH = 64
DR = 32
DKV = 256
I3 = 3072
EPS = 1e-5
SCALE_ATT = 1.0 / math.sqrt(DH + DR)
INVQ = 1.0 / 32767.0
MAGIC = 12582912.0

S_X0, S_X1 = 0, 1
S_S1, S_I1 = 2, 3
S_S2, S_C2 = 4, 5
S_S3, S_C3 = 6, 7
S_S4, S_C4 = 8, 9
S_S5, S_C5 = 10, 11
S_LU, S_LG, S_LGS, S_LOR = 12, 13, 14, 15
B_I1, B_C2, B_C3, B_S3, B_C4, B_C5, B_S5 = 0, 1, 2, 3, 4, 5, 6


def cdiv(a, b):
    return (a + b - 1) // b


class KB:
    def __init__(self, T):
        self.T = T
        self.NT = T // 128
        self.nc = bass.Bass("TRN2", target_bir_lowering=False)
        self.c = {}
        self.root = ExitStack()

    def sem(self, n):
        self.c[n] = [self.root.enter_context(self.nc.semaphore(n)), 0]

    def H(self, n):
        return self.c[n][0]

    def V(self, n):
        return self.c[n][1]

    def inc(self, n, k=1):
        self.c[n][1] += k
        return self.c[n][1]

    def mark(self, inst, n, k=1):
        inst.then_inc(self.H(n), k)
        return self.inc(n, k)

    def edge(self, eng, inst, n="sVo"):
        v = self.mark(inst, n)
        eng.wait_ge(self.H(n), v)

    def edges(self, eng, insts, n="sVo"):
        for i in insts:
            v = self.mark(i, n)
        eng.wait_ge(self.H(n), v)

    def sb(self, es, name, shape, dt, side="left"):
        return es.enter_context(self.nc.sbuf_tensor(name, shape, dt, side=side))

    def psum(self, es, name, shape, dt):
        return es.enter_context(self.nc.psum_tensor(name, shape, dt))

    def acc_mm(self, t_, out_ap, pairs):
        n = len(pairs)
        li = None
        for i, (l, r) in enumerate(pairs):
            li = t_.matmul(out_ap, l, r, start=(i == 0), stop=(i == n - 1))
        return li

    def mm_evac(self, items, emit_mms, emit_evac, bufs, evac_eng="vector"):
        nb = len(bufs)
        with self.nc.Block() as blk:
            tb = self.V("sMM")
            vb = self.V("sEV")

            @blk.tensor
            def _(t_):
                for k, it in enumerate(items):
                    if k >= nb:
                        t_.wait_ge(self.H("sEV"), vb + k - nb + 1)
                    li = emit_mms(t_, it, bufs[k % nb])
                    li.then_inc(self.H("sMM"), 1)
            self.inc("sMM", len(items))

            def body(eng):
                for k, it in enumerate(items):
                    eng.wait_ge(self.H("sMM"), tb + k + 1)
                    li = emit_evac(eng, it, bufs[k % nb])
                    li.then_inc(self.H("sEV"), 1)
            (blk.vector if evac_eng == "vector" else blk.scalar)(body)
            self.inc("sEV", len(items))

    def transpose_tm_to_fm(self, src_tm, dst_fm, pa, pb):
        T, NT = self.T, self.NT
        groups = []
        for fc in range(8):
            for half in range(cdiv(NT, 8)):
                groups.append((fc, half, list(range(half * 8, min((half + 1) * 8, NT)))))

        def mms(t_, it, buf):
            fc, half, tts = it
            li = None
            for j, tt in enumerate(tts):
                li = t_.transpose(buf[:, j * 128:(j + 1) * 128],
                                  src_tm[:, tt * E + fc * 128: tt * E + (fc + 1) * 128],
                                  self.idb[:])
            return li

        def ev(v, it, buf):
            fc, half, tts = it
            n = len(tts) * 128
            return v.tensor_copy(out=dst_fm[:, fc * T + half * 1024: fc * T + half * 1024 + n],
                                 in_=buf[:, 0:n])

        self.mm_evac(groups, mms, ev, [pa, pb])

    def ln_stats(self, src_fn, n_tiles):
        with self.nc.Block() as blk:
            @blk.vector
            def _(v):
                for t in range(n_tiles):
                    ap = src_fn(t)
                    v.bn_stats(out=self.lnstat[:, 12 * t:12 * t + 6], in_=ap[:, 0:512])
                    v.bn_stats(out=self.lnstat[:, 12 * t + 6:12 * t + 12], in_=ap[:, 512:1024])

    def ln_finish(self, n_tiles):
        nc = self.nc
        mv2 = self.lnmv[:].rearrange("p (t two) -> p t two", two=2)
        ab2 = self.lnab[:].rearrange("p (t two) -> p t two", two=2)
        with nc.Block() as blk:
            @blk.vector
            def _(v):
                for t in range(n_tiles):
                    i = v.bn_aggr(out=self.lnmv[:, 2 * t:2 * t + 2],
                                  in_=self.lnstat[:, 12 * t:12 * t + 12])
                self.edge(v, i)
                v.tensor_scalar(out=mv2[:, 0:n_tiles, 1:2], in0=mv2[:, 0:n_tiles, 1:2],
                                scalar1=EPS, scalar2=None, op0=ALU.add)
        with nc.Block() as blk:
            @blk.scalar
            def _(sc):
                i1 = sc.activation(out=ab2[:, 0:n_tiles, 0:1], in_=mv2[:, 0:n_tiles, 1:2],
                                   func=AF.Ln, bias=0.0, scale=1.0)
                self.edge(sc, i1, "sSo")
                sc.activation(out=ab2[:, 0:n_tiles, 0:1], in_=ab2[:, 0:n_tiles, 0:1],
                              func=AF.Exp, bias=0.0, scale=-0.5)
        with nc.Block() as blk:
            @blk.vector
            def _(v):
                i1 = v.tensor_tensor(out=ab2[:, 0:n_tiles, 1:2], in0=mv2[:, 0:n_tiles, 0:1],
                                     in1=ab2[:, 0:n_tiles, 0:1], op=ALU.mult)
                self.edge(v, i1)
                v.tensor_scalar(out=ab2[:, 0:n_tiles, 1:2], in0=ab2[:, 0:n_tiles, 1:2],
                                scalar1=-1.0, scalar2=None, op0=ALU.mult)

    def allreduce_max(self, n):
        nc = self.nc
        with nc.Block() as blk:
            @blk.gpsimd
            def _(g):
                self.mark(g.dma_start(out=self.cc_in.ap()[0:1, 0:8],
                                      in_=self.srow[0:1, 0:8]), "sG", 16)
                g.wait_ge(self.H("sG"), self.V("sG"))
                self.mark(g.collective_compute("AllReduce", ALU.max,
                                               replica_groups=[list(range(NCORES))],
                                               ins=[self.cc_in.ap().opt()],
                                               outs=[self.cc_out.ap().opt()]), "sC", 1)
                g.wait_ge(self.H("sC"), self.V("sC"))
                self.mark(g.dma_start(out=self.srow[0:1, 0:n],
                                      in_=self.cc_out.ap()[0:1, 0:n]), "sG", 16)
                g.wait_ge(self.H("sG"), self.V("sG"))

    def bcast_scalars(self, pairs):
        nc = self.nc
        with nc.Block() as blk:
            @blk.gpsimd
            def _(g):
                for i, (sc_, _dc) in enumerate(pairs):
                    self.mark(g.dma_start(out=self.bc_d.ap()[0:1, i:i + 1],
                                          in_=self.srow[0:1, sc_:sc_ + 1]), "sG", 16)
                g.wait_ge(self.H("sG"), self.V("sG"))
                for i, (_sc, dc) in enumerate(pairs):
                    self.mark(g.dma_start(out=self.scal[:, dc:dc + 1],
                                          in_=self.bc_d.ap()[0:1, i:i + 1].broadcast_to((128, 1))),
                              "sG", 16)
                g.wait_ge(self.H("sG"), self.V("sG"))

    def global_absmax(self, pcol_ap, srow_col):
        with self.nc.Block() as blk:
            @blk.gpsimd
            def _(g):
                g.tensor_reduce(out=self.srow[0:1, srow_col:srow_col + 1], in_=pcol_ap,
                                axis=AX.XYZWC, op=ALU.max, apply_absolute_value=True)

    def emit_round(self, v, in_ap, c_ap, out_ap, y_ap, m_ap=None):
        i1 = v.tensor_scalar(out=y_ap, in0=in_ap, scalar1=c_ap, scalar2=MAGIC,
                             op0=ALU.mult, op1=ALU.add)
        self.edge(v, i1)
        return v.tensor_scalar(out=out_ap, in0=y_ap, scalar1=MAGIC, scalar2=None,
                               op0=ALU.subtract)

    def build(self):
        nc, T, NT = self.nc, self.T, self.NT
        n512 = T // 512
        d = nc.dram_tensor
        x_in = d("x", [T, E], F32, kind="ExternalInput")
        wq_d = d("wq", [E, H * DH], F32, kind="ExternalInput")
        wqr_d = d("wqr", [E, H * DR], F32, kind="ExternalInput")
        wkr_d = d("wkr", [E, DR], F32, kind="ExternalInput")
        wkvd_d = d("wkvd", [E, DKV], F32, kind="ExternalInput")
        wku_d = d("wku", [DKV, H * DH], F32, kind="ExternalInput")
        wvu_d = d("wvu", [DKV, H * DH], F32, kind="ExternalInput")
        wo_d = d("wo", [E, E], F32, kind="ExternalInput")
        lobo_d = d("lobo", [H], F32, kind="ExternalInput")
        wup_d = d("wup", [I3, E], F32, kind="ExternalInput")
        wgate_d = d("wgate", [I3, E], F32, kind="ExternalInput")
        wdown_d = d("wdown", [E, I3], F32, kind="ExternalInput")
        identb_d = d("identb", [128, 128], BF16, kind="ExternalInput")
        identf_d = d("identf", [128, 128], F32, kind="ExternalInput")
        cosT_d = d("cosT", [128, T], F32, kind="ExternalInput")
        sinT_d = d("sinT", [128, T], F32, kind="ExternalInput")
        iota_d = d("iota128", [128, 1], F32, kind="ExternalInput")
        sel_d = d("sel16", [16, 16 * 64], F32, kind="ExternalInput")
        cmask_d = d("cmask", [128, 128], F32, kind="ExternalInput")
        out_d = d("out", [T, E], F32, kind="ExternalOutput")
        r_spill = d("r_spill", [T, E], F32)
        u_spill = d("u_spill", [I3, T], BF16)
        y_spill = d("y_spill", [H * 64, T], BF16)
        g_spill = d("g_spill", [I3, T], BF16)
        self.cc_in = d("cc_in", [1, 8], F32)
        self.cc_out = d("cc_out", [1, 8], F32)
        self.bc_d = d("bc_d", [1, 16], F32)

        for s in ["sD", "sG", "sC", "sMM", "sEV", "sEX", "sMK", "sTM", "sAV",
                  "sFZ", "sVo", "sSo", "sDQ"]:
            self.sem(s)

        rs = self.root
        self.idb = self.sb(rs, "idb", [128, 128], BF16)
        self.idf = self.sb(rs, "idf", [128, 128], F32)
        self.iota_sb = self.sb(rs, "iota", [128, 1], F32)
        self.c128_sb = self.sb(rs, "c128", [128, 1], F32)
        self.ones_b = self.sb(rs, "onesb", [128, 128], BF16)
        self.ones_f = self.sb(rs, "onesf", [128, 64], F32)
        self.lnstat = self.sb(rs, "lnstat", [128, 192], F32)
        self.lnmv = self.sb(rs, "lnmv", [128, 32], F32)
        self.lnab = self.sb(rs, "lnab", [128, 32], F32)
        self.scal = self.sb(rs, "scal", [128, 8], F32)
        self.srow = self.sb(rs, "srow", [1, 16], F32)
        self.crow_b = self.sb(rs, "crow_b", [1, 16], BF16)
        self.crep = self.sb(rs, "crep", [128, 16], F32)
        self.junk = self.sb(rs, "junk", [128, 1], F32)
        self.pstage = self.sb(rs, "pstage", [128, 32], F32)
        self.mstage = self.sb(rs, "mstage", [128, 16], F32)
        self.mq = self.sb(rs, "mq", [16, 128], F32)
        self.dq = self.sb(rs, "dq", [16, 128], F32)
        self.rq = self.sb(rs, "rq", [16, 128], F32)
        self.sel_f = self.sb(rs, "sel_f", [16, 16 * 64], F32)
        self.cmask = self.sb(rs, "cmask_sb", [128, 128], F32)

        att = ExitStack()
        cos_sb = self.sb(att, "cos_sb", [128, T], F32)
        sin_sb = self.sb(att, "sin_sb", [128, T], F32)
        wvu_sb = self.sb(att, "wvu_sb", [128, 2 * 1024], BF16)
        wq_g = self.sb(att, "wq_g", [128, 8 * 256], BF16)
        wku_g = self.sb(att, "wku_g", [128, 2 * 256], BF16)
        wqr_g = self.sb(att, "wqr_g", [128, 8 * 128], BF16)
        krr_a = self.sb(att, "krr_a", [16, T], BF16)
        krr_b = self.sb(att, "krr_b", [16, T], BF16)
        hp_q = self.sb(att, "hp_q", [64, 2 * T], BF16)
        rtmp = self.sb(att, "rtmp", [64, 2048], F32)
        s_fm = ExitStack()
        xn1_fm = self.sb(s_fm, "xn1_fm", [128, 8 * T], BF16)
        latent = self.sb(s_fm, "latent", [128, 2 * T], BF16)
        s_pro = ExitStack()
        wkvd_sb = self.sb(s_pro, "wkvd_sb", [128, 8 * 256], BF16)
        wkr_sb = self.sb(s_pro, "wkr_sb", [128, 8 * DR], BF16)
        s_rtm = ExitStack()
        r_tm = self.sb(s_rtm, "r_tm", [128, NT * E], F32)
        s_xn1t = ExitStack()
        xn1_tm = self.sb(s_xn1t, "xn1_tm", [128, NT * E], BF16, side="right")

        with nc.Block() as blk:
            @blk.sync
            def _(s):
                for dst, srcd in [(self.idb, identb_d), (self.idf, identf_d),
                                  (self.iota_sb, iota_d)]:
                    self.mark(s.dma_start(out=dst[:], in_=srcd.ap()), "sD", 16)
                self.mark(s.dma_start(out=r_tm[:].rearrange("p (t e) -> p t e", e=E),
                                      in_=x_in.ap().rearrange("(t p) e -> p t e", p=128)), "sD", 16)
                self.mark(s.dma_start(out=cos_sb[:], in_=cosT_d.ap()), "sD", 16)
                self.mark(s.dma_start(out=self.sel_f[:], in_=sel_d.ap()), "sD", 16)
                self.mark(s.dma_start(out=self.cmask[:], in_=cmask_d.ap()), "sD", 16)
                self.mark(s.dma_start(out=sin_sb[:], in_=sinT_d.ap()), "sD", 16)
            @blk.gpsimd
            def _(g):
                for dst, srcd, ch in [(wkvd_sb, wkvd_d, 8), (wkr_sb, wkr_d, 8),
                                      (wvu_sb, wvu_d, 2)]:
                    self.mark(g.dma_start(out=dst[:].rearrange("p (c f) -> p c f", c=ch),
                                          in_=srcd.ap().rearrange("(c p) f -> p c f", p=128)),
                              "sG", 16)
                self.mark(g.dma_start(out=self.srow[0:1, 0:H],
                                      in_=lobo_d.ap().unsqueeze(0)), "sG", 16)
            for name in ["vector", "scalar", "tensor"]:
                def wtr(eng):
                    eng.wait_ge(self.H("sD"), self.V("sD"))
                    eng.wait_ge(self.H("sG"), self.V("sG"))
                getattr(blk, name)(wtr)
            @blk.vector
            def _(v):
                v.memset(self.c128_sb[:], 128.0)
                v.memset(self.ones_f[:], 1.0)
                v.memset(self.ones_b[:], 1.0)
                v.memset(self.scal[:, 7:8], MAGIC)

        self.ln_stats(lambda t: r_tm[:, t * E:(t + 1) * E], NT)
        self.ln_finish(NT)
        with nc.Block() as blk:
            @blk.scalar
            def _(sc):
                for t in range(NT):
                    sc.activation(out=xn1_tm[:, t * E:(t + 1) * E],
                                  in_=r_tm[:, t * E:(t + 1) * E], func=AF.Identity,
                                  scale=self.lnab[:, 2 * t:2 * t + 1],
                                  bias=self.lnab[:, 2 * t + 1:2 * t + 2])
        with nc.Block() as blk:
            @blk.sync
            def _(s):
                self.mark(s.dma_start(out=r_spill.ap().rearrange("(t p) e -> p t e", p=128),
                                      in_=r_tm[:].rearrange("p (t e) -> p t e", e=E)), "sD", 16)
                s.wait_ge(self.H("sD"), self.V("sD"))
        s_rtm.close()

        pstr = ExitStack()
        pa = self.psum(pstr, "psA", [128, 1024], BF16)
        pb = self.psum(pstr, "psB", [128, 1024], BF16)
        self.transpose_tm_to_fm(xn1_tm, xn1_fm, pa, pb)
        pstr.close()
        s_xn1t.close()

        att_r = ExitStack()
        combQ = self.sb(att_r, "combQ", [128, 4 * T], BF16, side="right")
        combK = self.sb(att_r, "combK", [128, 4 * T], BF16, side="right")
        vaug = self.sb(att_r, "vaug", [128, NT * 4 * 65], BF16, side="right")
        E_sb = self.sb(att_r, "E_sb", [128, 2 * 512], BF16, side="right")
        runmax = self.sb(att_r, "runmax", [128, T], BF16, side="right")
        ystage = self.sb(att_r, "ystage", [64, T], BF16, side="right")
        drow = self.sb(att_r, "drow", [65, T], F32, side="right")

        php = ExitStack()
        ps_y = self.psum(php, "ps_y", [65, T], F32)
        ps_s0 = self.psum(php, "ps_s0", [128, 1024], BF16)
        ps_s1 = self.psum(php, "ps_s1", [128, 1024], BF16)
        ps_m = self.psum(php, "ps_m", [16, 128], F32)
        ps_r = self.psum(php, "ps_r", [64, 512], F32)
        ps_a = ps_s0.ap().bitcast(F32)
        ps_b = ps_s1.ap().bitcast(F32)

        with nc.Block() as blk:
            @blk.scalar
            def _(sc):
                sc.activation(out=self.srow[0:1, 0:H], in_=self.srow[0:1, 0:H],
                              func=AF.Exp, bias=0.0, scale=1.0)
        with nc.Block() as blk:
            @blk.vector
            def _(v):
                v.tensor_copy(out=self.crow_b[0:1, 0:H], in_=self.srow[0:1, 0:H])
        with nc.Block() as blk:
            @blk.tensor
            def _(t_):
                t_.matmul(ps_a[0:128, 0:16], self.ones_b[0:1, 0:128],
                          self.crow_b[0:1, 0:16], start=True, stop=True)
        with nc.Block() as blk:
            @blk.vector
            def _(v):
                v.tensor_copy(out=self.crep[:, 0:H], in_=ps_a[0:128, 0:16])
                v.memset(vaug[:].rearrange("p (t h c) -> p t h c", h=4, c=65)[:, :, :, 64:65], 1.0)

        def lat_mm(t_, it, buf):
            c, tc = it
            return self.acc_mm(t_, buf[0:128, 0:512],
                        [(wkvd_sb[:, ci * DKV + c * 128: ci * DKV + (c + 1) * 128],
                          xn1_fm[:, ci * T + tc * 512: ci * T + tc * 512 + 512])
                         for ci in range(8)])

        def lat_ev(v, it, buf):
            c, tc = it
            return v.tensor_copy(out=latent[:, c * T + tc * 512: c * T + tc * 512 + 512],
                                 in_=buf[0:128, 0:512])

        self.mm_evac([(c, tc) for c in range(2) for tc in range(n512)], lat_mm, lat_ev,
                     [ps_a, ps_b])

        def kr_mm(t_, tc, buf):
            self.acc_mm(t_, buf[0:16, 0:512],
                        [(wkr_sb[:, ci * DR: ci * DR + 16],
                          xn1_fm[:, ci * T + tc * 512: ci * T + tc * 512 + 512])
                         for ci in range(8)])
            return self.acc_mm(t_, buf[32:48, 0:512],
                        [(wkr_sb[:, ci * DR + 16: ci * DR + 32],
                          xn1_fm[:, ci * T + tc * 512: ci * T + tc * 512 + 512])
                         for ci in range(8)])

        def kr_ev(v, tc, buf):
            cs = cos_sb[0:16, tc * 512:(tc + 1) * 512]
            sn = sin_sb[0:16, tc * 512:(tc + 1) * 512]
            fh, sh = buf[0:16, 0:512], buf[32:48, 0:512]
            i1 = v.tensor_tensor(out=rtmp[0:16, 0:512], in0=sh, in1=sn, op=ALU.mult)
            i2 = v.tensor_tensor(out=rtmp[0:16, 512:1024], in0=fh, in1=sn, op=ALU.mult)
            i3 = v.tensor_tensor(out=rtmp[0:16, 1024:1536], in0=fh, in1=cs, op=ALU.mult)
            i4 = v.tensor_tensor(out=rtmp[0:16, 1536:2048], in0=sh, in1=cs, op=ALU.mult)
            self.edges(v, [i1, i2, i3, i4])
            v.tensor_sub(out=krr_a[0:16, tc * 512:(tc + 1) * 512],
                         in0=rtmp[0:16, 1024:1536], in1=rtmp[0:16, 0:512])
            return v.tensor_add(out=krr_b[0:16, tc * 512:(tc + 1) * 512],
                                in0=rtmp[0:16, 1536:2048], in1=rtmp[0:16, 512:1024])

        self.mm_evac(list(range(n512)), kr_mm, kr_ev, [ps_a, ps_b])
        s_pro.close()

        with nc.Block() as blk:
            @blk.vector
            def _(v):
                v.memset(runmax[:], 0.0)

        for grp in range(4):
            with nc.Block() as blk:
                @blk.gpsimd
                def _(g, grp=grp):
                    self.mark(g.dma_start(
                        out=wq_g[:].rearrange("p (c f) -> p c f", c=8),
                        in_=wq_d.ap().rearrange("(c p) f -> p c f", p=128)[:, :, grp * 256:(grp + 1) * 256]),
                        "sG", 16)
                    self.mark(g.dma_start(
                        out=wku_g[:].rearrange("p (c f) -> p c f", c=2),
                        in_=wku_d.ap().rearrange("(c p) f -> p c f", p=128)[:, :, grp * 256:(grp + 1) * 256]),
                        "sG", 16)
                    fh0 = (grp // 2) * 128 + (grp % 2) * 64
                    sh0 = (2 + grp // 2) * 128 + (grp % 2) * 64
                    self.mark(g.dma_start(
                        out=wqr_g[:].rearrange("p (c f) -> p c f", c=8)[:, :, 0:64],
                        in_=wqr_d.ap().rearrange("(c p) f -> p c f", p=128)[:, :, fh0:fh0 + 64]),
                        "sG", 16)
                    self.mark(g.dma_start(
                        out=wqr_g[:].rearrange("p (c f) -> p c f", c=8)[:, :, 64:128],
                        in_=wqr_d.ap().rearrange("(c p) f -> p c f", p=128)[:, :, sh0:sh0 + 64]),
                        "sG", 16)
                    g.wait_ge(self.H("sG"), self.V("sG"))
                @blk.tensor
                def _(t_):
                    t_.wait_ge(self.H("sG"), self.V("sG"))

            def qc_mm(t_, it, buf):
                j, tc = it
                return self.acc_mm(t_, buf[0:64, 0:512],
                            [(wq_g[:, ci * 256 + j * 64: ci * 256 + (j + 1) * 64],
                              xn1_fm[:, ci * T + tc * 512: ci * T + tc * 512 + 512])
                             for ci in range(8)])

            def qc_ev(v, it, buf):
                j, tc = it
                return v.tensor_copy(out=combQ[0:64, j * T + tc * 512: j * T + tc * 512 + 512],
                                     in_=buf[0:64, 0:512])

            self.mm_evac([(j, tc) for j in range(4) for tc in range(n512)], qc_mm, qc_ev,
                         [ps_a, ps_b])

            def kc_mm(t_, it, buf):
                j, tc = it
                return self.acc_mm(t_, buf[0:64, 0:512],
                            [(wku_g[:, c * 256 + j * 64: c * 256 + (j + 1) * 64],
                              latent[:, c * T + tc * 512: c * T + tc * 512 + 512])
                             for c in range(2)])

            def kc_ev(v, it, buf):
                j, tc = it
                return v.tensor_copy(out=combK[0:64, j * T + tc * 512: j * T + tc * 512 + 512],
                                     in_=buf[0:64, 0:512])

            self.mm_evac([(j, tc) for j in range(4) for tc in range(n512)], kc_mm, kc_ev,
                         [ps_a, ps_b])

            def qr_mm(t_, it, buf):
                tc, part = it
                return self.acc_mm(t_, buf[0:64, 0:512],
                            [(wqr_g[:, ci * 128 + part * 64: ci * 128 + part * 64 + 64],
                              xn1_fm[:, ci * T + tc * 512: ci * T + tc * 512 + 512])
                             for ci in range(8)])

            def qr_ev(v, it, buf):
                tc, part = it
                dst = rtmp[0:64, 0:512] if part == 0 else rtmp[0:64, 512:1024]
                i0 = v.tensor_copy(out=dst, in_=buf[0:64, 0:512])
                if part == 0:
                    return i0
                v.wait_ge(self.H("sEV"), self.V("sEV"))
                self.edge(v, i0)
                cs = cos_sb[0:64, tc * 512:(tc + 1) * 512]
                sn = sin_sb[0:64, tc * 512:(tc + 1) * 512]
                fh, sh = rtmp[0:64, 0:512], rtmp[0:64, 512:1024]
                i1 = v.tensor_tensor(out=rtmp[0:64, 1024:1536], in0=sh, in1=sn, op=ALU.mult)
                i2 = v.tensor_tensor(out=rtmp[0:64, 1536:2048], in0=fh, in1=sn, op=ALU.mult)
                self.edges(v, [i1, i2])
                i3 = v.tensor_tensor(out=fh, in0=fh, in1=cs, op=ALU.mult)
                i4 = v.tensor_tensor(out=sh, in0=sh, in1=cs, op=ALU.mult)
                self.edges(v, [i3, i4])
                v.tensor_sub(out=hp_q[0:64, tc * 512: tc * 512 + 512],
                             in0=fh, in1=rtmp[0:64, 1024:1536])
                return v.tensor_add(out=hp_q[0:64, T + tc * 512: T + tc * 512 + 512],
                                    in0=sh, in1=rtmp[0:64, 1536:2048])

            self.mm_evac([(tc, part) for tc in range(n512) for part in range(2)],
                         qr_mm, qr_ev, [ps_a, ps_b])

            with nc.Block() as blk:
                @blk.sync
                def _(s):
                    for j in range(4):
                        self.mark(s.dma_start(out=combQ[64:80, j * T:(j + 1) * T],
                                              in_=hp_q[j * 16:(j + 1) * 16, 0:T]), "sD", 16)
                        self.mark(s.dma_start(out=combQ[80:96, j * T:(j + 1) * T],
                                              in_=hp_q[j * 16:(j + 1) * 16, T:2 * T]), "sD", 16)
                        self.mark(s.dma_start(out=combK[64:80, j * T:(j + 1) * T],
                                              in_=krr_a[0:16, 0:T]), "sD", 16)
                        self.mark(s.dma_start(out=combK[80:96, j * T:(j + 1) * T],
                                              in_=krr_b[0:16, 0:T]), "sD", 16)
                    s.wait_ge(self.H("sD"), self.V("sD"))

            def v_mm(t_, tt, buf):
                return self.acc_mm(t_, buf[0:128, 0:256],
                            [(latent[:, c * T + tt * 128: c * T + (tt + 1) * 128],
                              wvu_sb[:, c * 1024 + grp * 256: c * 1024 + (grp + 1) * 256])
                             for c in range(2)])

            def v_ev(v, tt, buf):
                return v.tensor_copy(
                    out=vaug[:, tt * 260:(tt + 1) * 260].rearrange("p (h c) -> p h c", c=65)[:, :, 0:64],
                    in_=buf[0:128, 0:256].rearrange("p (h c) -> p h c", c=64))

            self.mm_evac(list(range(NT)), v_mm, v_ev, [ps_a, ps_b])

            for j in range(4):
                self.one_head(4 * grp + j, j, combQ, combK, vaug, E_sb, runmax,
                              ystage, drow, y_spill,
                              ps_y, [ps_s0, ps_s1], ps_m, ps_r)

        php.close()
        att_r.close()
        s_fm.close()
        att.close()

        s_heads = ExitStack()
        wo_sb = self.sb(s_heads, "wo_sb", [128, 8 * 1024], BF16)
        s_wo = ExitStack()
        r_tm2 = self.sb(s_wo, "r_tm2", [128, NT * E], F32, side="right")
        attn_tm = self.sb(s_wo, "attn_tm", [128, NT * E], BF16, side="right")
        s_afm = ExitStack()
        attn_fm = self.sb(s_afm, "attn_fm", [128, 8 * T], BF16, side="right")
        s_yfm = ExitStack()
        y_fm = self.sb(s_yfm, "y_fm", [128, 8 * T], BF16, side="right")

        with nc.Block() as blk:
            @blk.sync
            def _(s):
                self.mark(s.dma_start(out=r_tm2[:].rearrange("p (t e) -> p t e", e=E),
                                      in_=r_spill.ap().rearrange("(t p) e -> p t e", p=128)),
                          "sD", 16)
                self.mark(s.dma_start(out=y_fm[:].rearrange("p (c t) -> p c t", c=8),
                                      in_=y_spill.ap().rearrange("(c p) t -> p c t", p=128)),
                          "sD", 16)
            @blk.gpsimd
            def _(g):
                self.mark(g.dma_start(out=wo_sb[:].rearrange("p (c f) -> p c f", c=8),
                                      in_=wo_d.ap().rearrange("(c p) f -> p c f", p=128)), "sG", 16)
            @blk.tensor
            def _(t_):
                t_.wait_ge(self.H("sG"), self.V("sG"))
                t_.wait_ge(self.H("sD"), self.V("sD"))

        pwo = ExitStack()
        wps_a = self.psum(pwo, "wps_a", [128, 512], F32)
        wps_b = self.psum(pwo, "wps_b", [128, 512], F32)

        def wo_mm(t_, it, buf):
            oc, tc = it
            return self.acc_mm(t_, buf[0:128, 0:512],
                        [(wo_sb[:, ci * 1024 + oc * 128: ci * 1024 + (oc + 1) * 128],
                          y_fm[:, ci * T + tc * 512: ci * T + tc * 512 + 512])
                         for ci in range(8)])

        def wo_ev(v, it, buf):
            oc, tc = it
            return v.tensor_copy(out=attn_fm[:, oc * T + tc * 512: oc * T + tc * 512 + 512],
                          in_=buf[0:128, 0:512])

        self.mm_evac([(oc, tc) for oc in range(8) for tc in range(n512)], wo_mm, wo_ev,
                     [wps_a, wps_b])
        pwo.close()
        s_yfm.close()
        s_heads.close()

        ptw = ExitStack()
        pa2 = self.psum(ptw, "wpsA", [128, 1024], BF16)
        pb2 = self.psum(ptw, "wpsB", [128, 1024], BF16)

        def fm2tm_mm(t_, tt, buf):
            for fc in range(8):
                li = t_.transpose(buf[:, fc * 128:(fc + 1) * 128],
                                  attn_fm[:, fc * T + tt * 128: fc * T + (tt + 1) * 128],
                                  self.idb[:])
            return li

        def fm2tm_ev(v, tt, buf):
            return v.tensor_copy(out=attn_tm[:, tt * E:(tt + 1) * E], in_=buf[:, 0:1024])

        self.mm_evac(list(range(NT)), fm2tm_mm, fm2tm_ev, [pa2, pb2])
        ptw.close()
        s_afm.close()

        self.ln_stats(lambda t: attn_tm[:, t * E:(t + 1) * E], NT)
        self.ln_finish(NT)
        with nc.Block() as blk:
            @blk.vector
            def _(v):
                v.wait_ge(self.H("sD"), self.V("sD"))
                for t in range(NT):
                    i = v.scalar_tensor_tensor(out=r_tm2[:, t * E:(t + 1) * E],
                                               in0=attn_tm[:, t * E:(t + 1) * E],
                                               scalar=self.lnab[:, 2 * t:2 * t + 1],
                                               in1=r_tm2[:, t * E:(t + 1) * E],
                                               op0=ALU.mult, op1=ALU.add)
                    self.edge(v, i)
                    i = v.tensor_scalar(out=r_tm2[:, t * E:(t + 1) * E],
                                        in0=r_tm2[:, t * E:(t + 1) * E],
                                        scalar1=self.lnab[:, 2 * t + 1:2 * t + 2],
                                        scalar2=None, op0=ALU.add)
                    self.edge(v, i)

        self.ln_stats(lambda t: r_tm2[:, t * E:(t + 1) * E], NT)
        self.ln_finish(NT)
        s_xn3 = ExitStack()
        xn3 = self.sb(s_xn3, "xn3", [128, NT * E], F32)
        ybuf = self.sb(s_xn3, "ybuf", [128, E], F32)
        mbuf = self.sb(s_xn3, "mbuf", [128, E], F32)
        with nc.Block() as blk:
            @blk.scalar
            def _(sc):
                for t in range(NT):
                    sc.activation(out=xn3[:, t * E:(t + 1) * E],
                                  in_=r_tm2[:, t * E:(t + 1) * E], func=AF.Identity,
                                  scale=self.lnab[:, 2 * t:2 * t + 1],
                                  bias=self.lnab[:, 2 * t + 1:2 * t + 2])
        with nc.Block() as blk:
            @blk.sync
            def _(s):
                self.mark(s.dma_start(out=r_spill.ap().rearrange("(t p) e -> p t e", p=128),
                                      in_=r_tm2[:].rearrange("p (t e) -> p t e", e=E)), "sD", 16)
                s.wait_ge(self.H("sD"), self.V("sD"))
        with nc.Block() as blk:
            @blk.vector
            def _(v):
                for t in range(NT):
                    i = v.tensor_reduce(out=self.pstage[:, t:t + 1],
                                        in_=xn3[:, t * E:(t + 1) * E],
                                        axis=AX.X, op=ALU.max, apply_absolute_value=True)
                self.edge(v, i)
                v.tensor_reduce(out=self.pstage[:, 16:17], in_=self.pstage[:, 0:NT],
                                axis=AX.X, op=ALU.max)
        self.global_absmax(self.pstage[:, 16:17], S_X0)
        self.allreduce_max(1)
        with nc.Block() as blk:
            @blk.vector
            def _(v):
                i = v.tensor_scalar(out=self.srow[0:1, S_S1:S_S1 + 1],
                                    in0=self.srow[0:1, S_X0:S_X0 + 1],
                                    scalar1=INVQ, scalar2=1e-12, op0=ALU.mult, op1=ALU.max)
                self.edge(v, i)
                v.reciprocal(out=self.srow[0:1, S_I1:S_I1 + 1],
                             in_=self.srow[0:1, S_S1:S_S1 + 1])
        self.bcast_scalars([(S_I1, B_I1)])

        s_wo.close()

        s_xqf = ExitStack()
        xq_fm = self.sb(s_xqf, "xq_fm", [128, 8 * T], BF16, side="right")
        s_xqt = ExitStack()
        xq_tm = self.sb(s_xqt, "xq_tm", [128, NT * E], BF16, side="right")
        with nc.Block() as blk:
            @blk.vector
            def _(v):
                v.wait_ge(self.H("sG"), self.V("sG"))
                for t in range(NT):
                    i = self.emit_round(v, xn3[:, t * E:(t + 1) * E], self.scal[:, B_I1:B_I1 + 1],
                                        xq_tm[:, t * E:(t + 1) * E], ybuf[:], mbuf[:])
                    self.edge(v, i)
        s_xn3.close()

        ffs = ExitStack()
        spill_stage = self.sb(ffs, "spill_stage", [128, 2 * T], BF16)
        wdqT_down = self.sb(ffs, "wdqT_down", [128, 24 * 1024], BF16)
        s_dq = ExitStack()
        dqst = self.sb(s_dq, "dqst", [128, 2 * I3 + 2 * (I3 // 32) + 2 * I3], F32)
        o_nat, o_s, o_inv, o_t, o_m = (0, 2 * I3, 2 * I3 + I3 // 32,
                                       2 * I3 + 2 * (I3 // 32),
                                       3 * I3 + 2 * (I3 // 32))
        wdq_nat = self.sb(s_dq, "wdq_nat", [128, 2 * I3], BF16)

        ptx = ExitStack()
        pa3 = self.psum(ptx, "fpsA", [128, 1024], BF16)
        pb3 = self.psum(ptx, "fpsB", [128, 1024], BF16)
        self.transpose_tm_to_fm(xq_tm, xq_fm, pa3, pb3)
        ptx.close()
        s_xqt.close()

        def dequant_weight(w_d, wdqT, O, nI, tag=[0]):
            nblk = nI // 32
            noc = O // 128
            nfc = nI // 128
            fgroups = [list(range(gg, min(gg + 8, nfc))) for gg in range(0, nfc, 8)]
            ng = len(fgroups)
            tag[0] += 1
            pdq = ExitStack()
            pqa = self.psum(pdq, "dqA%d" % tag[0], [128, 1024], BF16)
            pqb = self.psum(pdq, "dqB%d" % tag[0], [128, 1024], BF16)
            pqs = [pqa, pqb]
            with nc.Block() as blk:
                d0 = self.V("sD")
                dq0 = self.V("sDQ")
                mm0 = self.V("sMM")
                ev0 = self.V("sEV")

                @blk.sync
                def _(s):
                    for oc in range(noc):
                        if oc >= 2:
                            s.wait_ge(self.H("sDQ"), dq0 + oc - 1)
                        self.mark(s.dma_start(out=dqst[:, o_nat + (oc % 2) * nI:
                                                       o_nat + (oc % 2) * nI + nI],
                                              in_=w_d.ap()[oc * 128:(oc + 1) * 128, :]),
                                  "sD", 16)
                        s.wait_ge(self.H("sD"), self.V("sD"))

                @blk.tensor
                def _(t_):
                    for oc in range(noc):
                        t_.wait_ge(self.H("sDQ"), dq0 + oc + 1)
                        for g, fcg in enumerate(fgroups):
                            k = oc * ng + g
                            if k >= 2:
                                t_.wait_ge(self.H("sEV"), ev0 + k - 1)
                            buf = pqs[k % 2]
                            for jj, fc in enumerate(fcg):
                                li = t_.transpose(
                                    buf[:, jj * 128:(jj + 1) * 128],
                                    wdq_nat[:, (oc % 2) * nI + fc * 128:
                                            (oc % 2) * nI + (fc + 1) * 128],
                                    self.idb[:])
                            li.then_inc(self.H("sMM"), 1)
                self.inc("sMM", noc * ng)

                @blk.vector
                def _(v):
                    for oc in range(noc):
                        v.wait_ge(self.H("sD"), d0 + 16 * (oc + 1))
                        if oc >= 1:
                            v.wait_ge(self.H("sDQ"), dq0 + oc)
                        if oc >= 2:
                            v.wait_ge(self.H("sMM"), mm0 + (oc - 1) * ng)
                        nat = dqst[:, o_nat + (oc % 2) * nI: o_nat + (oc % 2) * nI + nI]
                        nat3 = nat.rearrange("p (b i) -> p b i", i=32)
                        i = v.tensor_reduce(out=dqst[:, o_s:o_s + nblk].unsqueeze(2),
                                            in_=nat3, axis=AX.X, op=ALU.max,
                                            apply_absolute_value=True)
                        self.edge(v, i)
                        i = v.tensor_scalar(out=dqst[:, o_s:o_s + nblk],
                                            in0=dqst[:, o_s:o_s + nblk],
                                            scalar1=1.0 / 7.0, scalar2=1e-12,
                                            op0=ALU.mult, op1=ALU.max)
                        self.edge(v, i)
                        i = v.reciprocal(out=dqst[:, o_inv:o_inv + nblk],
                                         in_=dqst[:, o_s:o_s + nblk])
                        self.edge(v, i)
                        inv_bc = dqst[:, o_inv:o_inv + nblk].unsqueeze(2).broadcast_to((128, nblk, 32))
                        s_bc = dqst[:, o_s:o_s + nblk].unsqueeze(2).broadcast_to((128, nblk, 32))
                        tb = dqst[:, o_t:o_t + nI]
                        mb = dqst[:, o_m:o_m + nI]
                        i = v.tensor_tensor(out=tb.rearrange("p (b i) -> p b i", i=32),
                                            in0=nat3, in1=inv_bc, op=ALU.mult)
                        self.edge(v, i)
                        i = v.tensor_scalar(out=mb, in0=tb, scalar1=MAGIC, scalar2=MAGIC,
                                            op0=ALU.add, op1=ALU.subtract)
                        self.edge(v, i)
                        i = v.tensor_tensor(out=wdq_nat[:, (oc % 2) * nI:
                                                        (oc % 2) * nI + nI].rearrange(
                                                "p (b i) -> p b i", i=32),
                                            in0=mb.rearrange("p (b i) -> p b i", i=32),
                                            in1=s_bc, op=ALU.mult)
                        i.then_inc(self.H("sDQ"), 1)
                        self.inc("sDQ")
                        if oc > 0:
                            for g in range(ng):
                                k = (oc - 1) * ng + g
                                v.wait_ge(self.H("sMM"), mm0 + k + 1)
                                n = len(fgroups[g]) * 128
                                base = (oc - 1) * 128
                                li = v.tensor_copy(
                                    out=wdqT[:].rearrange("p (f o) -> p f o", o=O)[
                                        :, fgroups[g][0]:fgroups[g][0] + len(fgroups[g]),
                                        base:base + 128],
                                    in_=pqs[k % 2][:, 0:n].rearrange("p (f o) -> p f o", o=128))
                                li.then_inc(self.H("sEV"), 1)
                                self.inc("sEV")
                    for g in range(ng):
                        k = (noc - 1) * ng + g
                        v.wait_ge(self.H("sMM"), mm0 + k + 1)
                        n = len(fgroups[g]) * 128
                        base = (noc - 1) * 128
                        li = v.tensor_copy(
                            out=wdqT[:].rearrange("p (f o) -> p f o", o=O)[
                                :, fgroups[g][0]:fgroups[g][0] + len(fgroups[g]),
                                base:base + 128],
                            in_=pqs[k % 2][:, 0:n].rearrange("p (f o) -> p f o", o=128))
                        li.then_inc(self.H("sEV"), 1)
                        self.inc("sEV")
            pdq.close()

        def big_mms(wdqT, spill_d, srow_dst, tag=[0]):
            noc = I3 // 128
            tag[0] += 1
            pps = ExitStack()
            fa = self.psum(pps, "ffa%d" % tag[0], [128, 512], F32)
            fb = self.psum(pps, "ffb%d" % tag[0], [128, 512], F32)
            items = [(oc, tc) for oc in range(noc) for tc in range(n512)]
            with nc.Block() as blk:
                mm0 = self.V("sMM")
                ev0 = self.V("sEV")
                tm0 = self.V("sTM")
                dg0 = self.V("sG")

                @blk.tensor
                def _(t_):
                    for k, (oc, tc) in enumerate(items):
                        if k >= 2:
                            t_.wait_ge(self.H("sEV"), ev0 + k - 1)
                        buf = (fa, fb)[k % 2]
                        li = self.acc_mm(t_, buf[0:128, 0:512],
                                         [(wdqT[:, fc * I3 + oc * 128: fc * I3 + (oc + 1) * 128],
                                           xq_fm[:, fc * T + tc * 512: fc * T + tc * 512 + 512])
                                          for fc in range(8)])
                        li.then_inc(self.H("sMM"), 1)
                self.inc("sMM", len(items))

                @blk.scalar
                def _(sc):
                    for k, (oc, tc) in enumerate(items):
                        sc.wait_ge(self.H("sMM"), mm0 + k + 1)
                        if oc >= 2 and tc == 0:
                            sc.wait_ge(self.H("sG"), dg0 + 16 * (oc - 1))
                            sc.wait_ge(self.H("sTM"), tm0 + oc - 1)
                        sc.activation(out=spill_stage[:, (oc % 2) * T + tc * 512:
                                                      (oc % 2) * T + tc * 512 + 512],
                                      in_=(fa, fb)[k % 2][0:128, 0:512], func=AF.Identity,
                                      bias=0.0, scale=1.0).then_inc(self.H("sEV"), 1)
                        self.inc("sEV")

                @blk.vector
                def _(v):
                    for oc in range(noc):
                        v.wait_ge(self.H("sEV"), ev0 + (oc + 1) * n512)
                        v.tensor_reduce(out=self.pstage[:, oc:oc + 1],
                                        in_=spill_stage[:, (oc % 2) * T:(oc % 2) * T + T],
                                        axis=AX.X, op=ALU.max,
                                        apply_absolute_value=True).then_inc(self.H("sTM"), 1)
                self.inc("sTM", noc)

                @blk.gpsimd
                def _(g):
                    for oc in range(noc):
                        g.wait_ge(self.H("sTM"), tm0 + oc + 1)
                        self.mark(g.dma_start(out=spill_d.ap()[oc * 128:(oc + 1) * 128, :],
                                              in_=spill_stage[:, (oc % 2) * T:(oc % 2) * T + T]),
                                  "sG", 16)
                        g.wait_ge(self.H("sG"), self.V("sG"))
            pps.close()
            with nc.Block() as blk:
                @blk.vector
                def _(v):
                    v.tensor_reduce(out=self.pstage[:, 24:25], in_=self.pstage[:, 0:noc],
                                    axis=AX.X, op=ALU.max)
            self.global_absmax(self.pstage[:, 24:25], srow_dst)

        s_wu = ExitStack()
        wdqT_up = self.sb(s_wu, "wdqT_up", [128, 8 * I3], BF16)
        dequant_weight(wup_d, wdqT_up, I3, E)
        big_mms(wdqT_up, u_spill, S_LU)
        s_wu.close()
        s_wg = ExitStack()
        wdqT_g = self.sb(s_wg, "wdqT_g", [128, 8 * I3], BF16)
        dequant_weight(wgate_d, wdqT_g, I3, E)
        big_mms(wdqT_g, g_spill, S_LG)
        s_wg.close()
        s_xqf.close()

        dequant_weight(wdown_d, wdqT_down, 1024, I3)
        s_dq.close()

        with nc.Block() as blk:
            @blk.vector
            def _(v):
                i = v.tensor_copy(out=self.srow[0:1, S_X0:S_X0 + 1],
                                  in_=self.srow[0:1, S_LU:S_LU + 1])
                i = v.tensor_copy(out=self.srow[0:1, S_X1:S_X1 + 1],
                                  in_=self.srow[0:1, S_LG:S_LG + 1])
        self.allreduce_max(2)
        with nc.Block() as blk:
            @blk.vector
            def _(v):
                i = v.tensor_scalar(out=self.srow[0:1, S_S2:S_S2 + 1],
                                    in0=self.srow[0:1, S_X0:S_X0 + 1],
                                    scalar1=self.srow[0:1, S_S1:S_S1 + 1], scalar2=None,
                                    op0=ALU.mult)
                self.edge(v, i)
                i = v.tensor_scalar(out=self.srow[0:1, S_S2:S_S2 + 1],
                                    in0=self.srow[0:1, S_S2:S_S2 + 1],
                                    scalar1=INVQ, scalar2=1e-12, op0=ALU.mult, op1=ALU.max)
                self.edge(v, i)
                i = v.reciprocal(out=self.srow[0:1, S_C2:S_C2 + 1],
                                 in_=self.srow[0:1, S_S2:S_S2 + 1])
                self.edge(v, i)
                i = v.tensor_scalar(out=self.srow[0:1, S_C2:S_C2 + 1],
                                    in0=self.srow[0:1, S_C2:S_C2 + 1],
                                    scalar1=self.srow[0:1, S_S1:S_S1 + 1], scalar2=None,
                                    op0=ALU.mult)
                self.edge(v, i)
                i = v.tensor_scalar(out=self.srow[0:1, S_S3:S_S3 + 1],
                                    in0=self.srow[0:1, S_X1:S_X1 + 1],
                                    scalar1=self.srow[0:1, S_S1:S_S1 + 1], scalar2=None,
                                    op0=ALU.mult)
                self.edge(v, i)
                i = v.tensor_scalar(out=self.srow[0:1, S_S3:S_S3 + 1],
                                    in0=self.srow[0:1, S_S3:S_S3 + 1],
                                    scalar1=INVQ, scalar2=1e-12, op0=ALU.mult, op1=ALU.max)
                self.edge(v, i)
                i = v.reciprocal(out=self.srow[0:1, S_C3:S_C3 + 1],
                                 in_=self.srow[0:1, S_S3:S_S3 + 1])
                self.edge(v, i)
                i = v.tensor_scalar(out=self.srow[0:1, S_C3:S_C3 + 1],
                                    in0=self.srow[0:1, S_C3:S_C3 + 1],
                                    scalar1=self.srow[0:1, S_S1:S_S1 + 1], scalar2=None,
                                    op0=ALU.mult)
        self.bcast_scalars([(S_C2, B_C2), (S_C3, B_C3), (S_S3, B_S3)])

        s_gs = ExitStack()
        gsb = self.sb(s_gs, "gsb", [128, 24 * T], BF16)
        s_gsr = ExitStack()
        fy = self.sb(s_gsr, "fy", [128, T], F32, side="right")
        fm_ = self.sb(s_gsr, "fm_", [128, T], F32, side="right")
        fsig = self.sb(s_gsr, "fsig", [128, T], F32, side="right")

        NIC = I3 // 128
        MCOL = self.scal[:, 7:8]

        with nc.Block() as blk:
            d0 = self.V("sD")
            so0 = self.V("sSo")
            dq0 = self.V("sDQ")
            ev0 = self.V("sEV")

            @blk.sync
            def _(s):
                for ic in range(NIC):
                    if ic >= 2:
                        s.wait_ge(self.H("sSo"), so0 + 2 * (ic - 2) + 1)
                    self.mark(s.dma_start(out=spill_stage[:, (ic % 2) * T:(ic % 2) * T + T],
                                          in_=g_spill.ap()[ic * 128:(ic + 1) * 128, :]),
                              "sD", 16)
                    s.wait_ge(self.H("sD"), self.V("sD"))

            @blk.scalar
            def _(sc):
                for ic in range(NIC):
                    sc.wait_ge(self.H("sD"), d0 + 16 * (ic + 1))
                    if ic > 0:
                        sc.wait_ge(self.H("sEV"), ev0 + ic)
                    sc.activation(out=fy[:], in_=spill_stage[:, (ic % 2) * T:(ic % 2) * T + T],
                                  func=AF.Identity, scale=self.scal[:, B_C3:B_C3 + 1],
                                  bias=MCOL).then_inc(self.H("sSo"), 1)
                    self.inc("sSo")
                    sc.wait_ge(self.H("sDQ"), dq0 + ic + 1)
                    sc.activation(out=fsig[:], in_=gsb[:, ic * T:(ic + 1) * T],
                                  func=AF.Sigmoid, bias=0.0,
                                  scale=self.scal[:, B_S3:B_S3 + 1]).then_inc(self.H("sSo"), 1)
                    self.inc("sSo")

            @blk.vector
            def _(v):
                for ic in range(NIC):
                    v.wait_ge(self.H("sSo"), so0 + 2 * ic + 1)
                    i = v.tensor_scalar(out=gsb[:, ic * T:(ic + 1) * T], in0=fy[:],
                                        scalar1=MAGIC, scalar2=None, op0=ALU.subtract)
                    i.then_inc(self.H("sDQ"), 1)
                    self.inc("sDQ")
                    v.wait_ge(self.H("sSo"), so0 + 2 * ic + 2)
                    i = v.tensor_scalar(out=fy[:], in0=fsig[:], scalar1=65536.0,
                                        scalar2=65535.0, op0=ALU.mult, op1=ALU.min)
                    self.edge(v, i)
                    i = v.tensor_scalar(out=fm_[:], in0=fy[:], scalar1=MAGIC, scalar2=MAGIC,
                                        op0=ALU.add, op1=ALU.subtract)
                    self.edge(v, i)
                    i = v.tensor_tensor(out=gsb[:, ic * T:(ic + 1) * T],
                                        in0=gsb[:, ic * T:(ic + 1) * T], in1=fm_[:], op=ALU.mult)
                    self.edge(v, i)
                    i = v.tensor_reduce(out=self.pstage[:, ic:ic + 1],
                                        in_=gsb[:, ic * T:(ic + 1) * T],
                                        axis=AX.X, op=ALU.max, apply_absolute_value=True)
                    i.then_inc(self.H("sEV"), 1)
                    self.inc("sEV")
        with nc.Block() as blk:
            @blk.vector
            def _(v):
                v.tensor_reduce(out=self.pstage[:, 24:25], in_=self.pstage[:, 0:24],
                                axis=AX.X, op=ALU.max)
        self.global_absmax(self.pstage[:, 24:25], S_X0)
        self.allreduce_max(1)
        with nc.Block() as blk:
            @blk.vector
            def _(v):
                i = v.tensor_scalar(out=self.srow[0:1, S_S4:S_S4 + 1],
                                    in0=self.srow[0:1, S_X0:S_X0 + 1],
                                    scalar1=self.srow[0:1, S_S3:S_S3 + 1], scalar2=None,
                                    op0=ALU.mult)
                self.edge(v, i)
                i = v.tensor_scalar(out=self.srow[0:1, S_S4:S_S4 + 1],
                                    in0=self.srow[0:1, S_S4:S_S4 + 1],
                                    scalar1=INVQ / 65536.0, scalar2=1e-12,
                                    op0=ALU.mult, op1=ALU.max)
                self.edge(v, i)
                i = v.reciprocal(out=self.srow[0:1, S_C4:S_C4 + 1],
                                 in_=self.srow[0:1, S_S4:S_S4 + 1])
                self.edge(v, i)
                i = v.tensor_scalar(out=self.srow[0:1, S_C4:S_C4 + 1],
                                    in0=self.srow[0:1, S_C4:S_C4 + 1],
                                    scalar1=self.srow[0:1, S_S3:S_S3 + 1], scalar2=1.0 / 65536.0,
                                    op0=ALU.mult, op1=ALU.mult)
        self.bcast_scalars([(S_C4, B_C4)])

        with nc.Block() as blk:
            d0 = self.V("sD")
            so0 = self.V("sSo")
            ev0 = self.V("sEV")

            @blk.sync
            def _(s):
                for ic in range(NIC):
                    if ic >= 2:
                        s.wait_ge(self.H("sSo"), so0 + 2 * (ic - 2) + 2)
                    self.mark(s.dma_start(out=spill_stage[:, (ic % 2) * T:(ic % 2) * T + T],
                                          in_=u_spill.ap()[ic * 128:(ic + 1) * 128, :]),
                              "sD", 16)
                    s.wait_ge(self.H("sD"), self.V("sD"))

            @blk.scalar
            def _(sc):
                for ic in range(NIC):
                    if ic > 0:
                        sc.wait_ge(self.H("sEV"), ev0 + ic)
                    sc.activation(out=fy[:], in_=gsb[:, ic * T:(ic + 1) * T],
                                  func=AF.Identity, scale=self.scal[:, B_C4:B_C4 + 1],
                                  bias=MCOL).then_inc(self.H("sSo"), 1)
                    self.inc("sSo")
                    sc.wait_ge(self.H("sD"), d0 + 16 * (ic + 1))
                    sc.activation(out=fsig[:], in_=spill_stage[:, (ic % 2) * T:(ic % 2) * T + T],
                                  func=AF.Identity, scale=self.scal[:, B_C2:B_C2 + 1],
                                  bias=MCOL).then_inc(self.H("sSo"), 1)
                    self.inc("sSo")

            @blk.vector
            def _(v):
                for ic in range(NIC):
                    v.wait_ge(self.H("sSo"), so0 + 2 * ic + 1)
                    i = v.tensor_scalar(out=gsb[:, ic * T:(ic + 1) * T], in0=fy[:],
                                        scalar1=MAGIC, scalar2=None, op0=ALU.subtract)
                    self.edge(v, i)
                    v.wait_ge(self.H("sSo"), so0 + 2 * ic + 2)
                    i = v.tensor_scalar(out=fm_[:], in0=fsig[:],
                                        scalar1=MAGIC, scalar2=None, op0=ALU.subtract)
                    self.edge(v, i)
                    i = v.tensor_tensor(out=gsb[:, ic * T:(ic + 1) * T],
                                        in0=gsb[:, ic * T:(ic + 1) * T],
                                        in1=fm_[:], op=ALU.mult)
                    self.edge(v, i)
                    i = v.tensor_reduce(out=self.pstage[:, ic:ic + 1],
                                        in_=gsb[:, ic * T:(ic + 1) * T],
                                        axis=AX.X, op=ALU.max, apply_absolute_value=True)
                    i.then_inc(self.H("sEV"), 1)
                    self.inc("sEV")
        with nc.Block() as blk:
            @blk.vector
            def _(v):
                v.tensor_reduce(out=self.pstage[:, 24:25], in_=self.pstage[:, 0:24],
                                axis=AX.X, op=ALU.max)
        self.global_absmax(self.pstage[:, 24:25], S_X0)
        self.allreduce_max(1)
        with nc.Block() as blk:
            @blk.vector
            def _(v):
                i = v.tensor_scalar(out=self.srow[0:1, S_S5:S_S5 + 1],
                                    in0=self.srow[0:1, S_X0:S_X0 + 1],
                                    scalar1=self.srow[0:1, S_S4:S_S4 + 1], scalar2=None,
                                    op0=ALU.mult)
                self.edge(v, i)
                i = v.tensor_scalar(out=self.srow[0:1, S_S5:S_S5 + 1],
                                    in0=self.srow[0:1, S_S5:S_S5 + 1],
                                    scalar1=self.srow[0:1, S_S2:S_S2 + 1], scalar2=None,
                                    op0=ALU.mult)
                self.edge(v, i)
                i = v.tensor_scalar(out=self.srow[0:1, S_S5:S_S5 + 1],
                                    in0=self.srow[0:1, S_S5:S_S5 + 1],
                                    scalar1=INVQ, scalar2=1e-12, op0=ALU.mult, op1=ALU.max)
                self.edge(v, i)
                i = v.reciprocal(out=self.srow[0:1, S_C5:S_C5 + 1],
                                 in_=self.srow[0:1, S_S5:S_S5 + 1])
                self.edge(v, i)
                i = v.tensor_scalar(out=self.srow[0:1, S_C5:S_C5 + 1],
                                    in0=self.srow[0:1, S_C5:S_C5 + 1],
                                    scalar1=self.srow[0:1, S_S4:S_S4 + 1], scalar2=None,
                                    op0=ALU.mult)
                self.edge(v, i)
                i = v.tensor_scalar(out=self.srow[0:1, S_C5:S_C5 + 1],
                                    in0=self.srow[0:1, S_C5:S_C5 + 1],
                                    scalar1=self.srow[0:1, S_S2:S_S2 + 1], scalar2=None,
                                    op0=ALU.mult)
        self.bcast_scalars([(S_C5, B_C5), (S_S5, B_S5)])

        with nc.Block() as blk:
            so0 = self.V("sSo")
            ev0 = self.V("sEV")

            @blk.scalar
            def _(sc):
                sc.wait_ge(self.H("sG"), self.V("sG"))
                for ic in range(NIC):
                    if ic >= 2:
                        sc.wait_ge(self.H("sEV"), ev0 + ic - 1)
                    sc.activation(out=(fy, fsig)[ic % 2][:], in_=gsb[:, ic * T:(ic + 1) * T],
                                  func=AF.Identity, scale=self.scal[:, B_C5:B_C5 + 1],
                                  bias=MCOL).then_inc(self.H("sSo"), 1)
                    self.inc("sSo")

            @blk.vector
            def _(v):
                for ic in range(NIC):
                    v.wait_ge(self.H("sSo"), so0 + ic + 1)
                    i = v.tensor_scalar(out=gsb[:, ic * T:(ic + 1) * T],
                                        in0=(fy, fsig)[ic % 2][:],
                                        scalar1=MAGIC, scalar2=None, op0=ALU.subtract)
                    i.then_inc(self.H("sEV"), 1)
                    self.inc("sEV")

        s_gsr.close()
        s_fout = ExitStack()
        ffn_fm = self.sb(s_fout, "ffn_fm", [128, 8 * T], BF16, side="right")
        pdn = ExitStack()
        da = self.psum(pdn, "dna", [128, 512], F32)
        db = self.psum(pdn, "dnb", [128, 512], F32)

        def dn_mm(t_, it, buf):
            oc, tc = it
            return self.acc_mm(t_, buf[0:128, 0:512],
                        [(wdqT_down[:, ic * 1024 + oc * 128: ic * 1024 + (oc + 1) * 128],
                          gsb[:, ic * T + tc * 512: ic * T + tc * 512 + 512])
                         for ic in range(I3 // 128)])

        def dn_ev(sc, it, buf):
            oc, tc = it
            return sc.activation(out=ffn_fm[:, oc * T + tc * 512: oc * T + tc * 512 + 512],
                          in_=buf[0:128, 0:512], func=AF.Identity, bias=0.0,
                          scale=self.scal[:, B_S5:B_S5 + 1])

        self.mm_evac([(oc, tc) for oc in range(8) for tc in range(n512)], dn_mm, dn_ev,
                     [da, db], evac_eng="scalar")
        pdn.close()
        s_gs.close()
        ffs.close()

        ffn_tm = self.sb(s_fout, "ffn_tm", [128, NT * E], BF16)
        ptf = ExitStack()
        fa2 = self.psum(ptf, "ftA", [128, 1024], BF16)
        fb2 = self.psum(ptf, "ftB", [128, 1024], BF16)

        def f2t_mm(t_, tt, buf):
            for fc in range(8):
                li = t_.transpose(buf[:, fc * 128:(fc + 1) * 128],
                                  ffn_fm[:, fc * T + tt * 128: fc * T + (tt + 1) * 128],
                                  self.idb[:])
            return li

        def f2t_ev(v, tt, buf):
            return v.tensor_copy(out=ffn_tm[:, tt * E:(tt + 1) * E], in_=buf[:, 0:1024])

        self.mm_evac(list(range(NT)), f2t_mm, f2t_ev, [fa2, fb2])
        ptf.close()

        s_r3 = ExitStack()
        r_tm3 = self.sb(s_r3, "r_tm3", [128, NT * E], F32)
        with nc.Block() as blk:
            @blk.sync
            def _(s):
                self.mark(s.dma_start(out=r_tm3[:].rearrange("p (t e) -> p t e", e=E),
                                      in_=r_spill.ap().rearrange("(t p) e -> p t e", p=128)),
                          "sD", 16)
        self.ln_stats(lambda t: ffn_tm[:, t * E:(t + 1) * E], NT)
        self.ln_finish(NT)
        with nc.Block() as blk:
            @blk.vector
            def _(v):
                v.wait_ge(self.H("sD"), self.V("sD"))
                for t in range(NT):
                    i = v.scalar_tensor_tensor(out=r_tm3[:, t * E:(t + 1) * E],
                                               in0=ffn_tm[:, t * E:(t + 1) * E],
                                               scalar=self.lnab[:, 2 * t:2 * t + 1],
                                               in1=r_tm3[:, t * E:(t + 1) * E],
                                               op0=ALU.mult, op1=ALU.add)
                    self.edge(v, i)
                    i = v.tensor_scalar(out=r_tm3[:, t * E:(t + 1) * E],
                                        in0=r_tm3[:, t * E:(t + 1) * E],
                                        scalar1=self.lnab[:, 2 * t + 1:2 * t + 2],
                                        scalar2=None, op0=ALU.add)
                    self.edge(v, i)
        with nc.Block() as blk:
            @blk.sync
            def _(s):
                self.mark(s.dma_start(out=out_d.ap().rearrange("(t p) e -> p t e", p=128),
                                      in_=r_tm3[:].rearrange("p (t e) -> p t e", e=E)), "sD", 16)
                s.wait_ge(self.H("sD"), self.V("sD"))
        s_r3.close()
        s_fout.close()
        self.root.close()
        return self.nc

    def one_head(self, h, j, combQ, combK, vaug, E_sb, runmax, ystage, drow,
                 y_spill, ps_y, ps_s, ps_m, ps_r):
        nc, T, NT = self.nc, self.T, self.NT
        items = []
        for kt in range(NT):
            off = kt * 128
            while off < T:
                n = min(512 - (off % 512), T - off)
                items.append((kt, off, n))
                off += n
        ns = len(items)
        last_kt_bank = {}
        for (kt, off, n) in items:
            b = off // 512
            last_kt_bank[b] = max(last_kt_bank.get(b, 0), kt)

        with nc.Block() as blk:
            mm0 = self.V("sMM")
            mk0 = self.V("sMK")
            ex0 = self.V("sEX")
            tm0 = self.V("sTM")
            av0 = self.V("sAV")

            def av_mm(t_, i):
                ktp, offp, np_ = items[i]
                t_.wait_ge(self.H("sEX"), ex0 + i + 1)
                t_.matmul(ps_y[0:65, offp:offp + np_],
                          vaug[:, ktp * 260 + j * 65: ktp * 260 + (j + 1) * 65],
                          E_sb[:, (i % 2) * 512: (i % 2) * 512 + np_],
                          start=(ktp == 0), stop=(ktp == last_kt_bank[offp // 512])
                          ).then_inc(self.H("sAV"), 1)

            @blk.tensor
            def _(t_):
                for i, (kt, off, n) in enumerate(items):
                    if i >= 2:
                        t_.wait_ge(self.H("sEX"), ex0 + i - 1)
                    sp = ps_s[i % 2].ap().bitcast(F32)
                    t_.matmul(sp[0:128, 0:n],
                              combK[0:96, j * T + kt * 128: j * T + (kt + 1) * 128],
                              combQ[0:96, j * T + off: j * T + off + n],
                              start=True, stop=True).then_inc(self.H("sMM"), 1)
                    if i > 0:
                        av_mm(t_, i - 1)
                av_mm(t_, ns - 1)
                for jq in range(NT):
                    i = ns + jq
                    t_.wait_ge(self.H("sEX"), ex0 + i - 1)
                    if jq == 0:
                        t_.wait_ge(self.H("sTM"), tm0 + ns)
                    t_.transpose(ps_s[i % 2][0:128, 0:128],
                                 runmax[:, jq * 128:(jq + 1) * 128],
                                 self.idb[:]).then_inc(self.H("sMM"), 1)
            self.inc("sMM", ns + NT)
            self.inc("sAV", ns)

            @blk.vector
            def _(v):
                def ttmax(i):
                    ktp, offp, np_ = items[i]
                    v.wait_ge(self.H("sEX"), ex0 + i + 1)
                    if i > 0:
                        v.wait_ge(self.H("sTM"), tm0 + i)
                    v.tensor_max(out=runmax[:, offp:offp + np_],
                                 in0=runmax[:, offp:offp + np_],
                                 in1=E_sb[:, (i % 2) * 512: (i % 2) * 512 + np_]
                                 ).then_inc(self.H("sTM"), 1)

                nd = 0
                for i, (kt, off, n) in enumerate(items):
                    if off == kt * 128:
                        v.wait_ge(self.H("sMM"), mm0 + i + 1)
                        nd += 1
                        sp = ps_s[i % 2].ap().bitcast(F32)
                        iv = v.tensor_tensor(
                            out=sp[0:128, 0:128], in0=sp[0:128, 0:128],
                            in1=self.cmask[:, 0:128], op=ALU.add)
                        iv.then_inc(self.H("sMK"), 1)
                        self.inc("sMK")
                    if i > 0:
                        ttmax(i - 1)
                ttmax(ns - 1)
                self.inc("sTM", ns)
                v.wait_ge(self.H("sEX"), ex0 + ns)
                for jq in range(NT):
                    i = ns + jq
                    v.wait_ge(self.H("sMM"), mm0 + i + 1)
                    v.tensor_reduce(out=self.mstage[:, jq:jq + 1], in_=ps_s[i % 2][0:128, 0:128],
                                    axis=AX.X, op=ALU.max).then_inc(self.H("sEX"), 1)

            @blk.scalar
            def _(sc):
                ndiag = 0
                for i, (kt, off, n) in enumerate(items):
                    if off == kt * 128:
                        ndiag += 1
                        sc.wait_ge(self.H("sMK"), mk0 + ndiag)
                    else:
                        sc.wait_ge(self.H("sMM"), mm0 + i + 1)
                    if i >= 2:
                        sc.wait_ge(self.H("sTM"), tm0 + i - 1)
                        sc.wait_ge(self.H("sAV"), av0 + i - 1)
                    sp = ps_s[i % 2].ap().bitcast(F32)
                    sc.activation(out=E_sb[:, (i % 2) * 512: (i % 2) * 512 + n],
                                  in_=sp[0:128, 0:n], func=AF.Exp, bias=0.0,
                                  scale=SCALE_ATT).then_inc(self.H("sEX"), 1)
                    self.inc("sEX")
            self.inc("sEX", NT)

        fz0 = self.V("sFZ")
        with nc.Block() as blk:
            @blk.tensor
            def _(t_):
                t_.transpose(ps_m[0:NT, 0:128], self.mstage[:, 0:NT],
                             self.idf[:]).then_inc(self.H("sFZ"), 1)
            self.inc("sFZ")
            @blk.scalar
            def _(sc):
                sc.activation(out=drow[64:65, 0:T], in_=ps_y[64:65, 0:T],
                              func=AF.Identity, bias=0.0, scale=1.0).then_inc(self.H("sFZ"), 1)
            self.inc("sFZ")
            @blk.vector
            def _(v):
                v.wait_ge(self.H("sFZ"), fz0 + 2)
                v.tensor_copy(out=self.mq[0:NT, 0:128], in_=ps_m[0:NT, 0:128])
            @blk.gpsimd
            def _(g):
                g.wait_ge(self.H("sFZ"), fz0 + 2)
                for qt in range(NT):
                    self.mark(g.dma_start(out=self.dq[qt:qt + 1, 0:128],
                                          in_=drow[64:65, qt * 128:(qt + 1) * 128]), "sG", 16)
                g.wait_ge(self.H("sG"), self.V("sG"))

        with nc.Block() as blk:
            @blk.vector
            def _(v):
                i = v.scalar_tensor_tensor(out=self.rq[0:NT, 0:128], in0=self.mq[0:NT, 0:128],
                                           scalar=self.crep[0:NT, h:h + 1], in1=self.dq[0:NT, 0:128],
                                           op0=ALU.mult, op1=ALU.add)
                self.edge(v, i)
                v.reciprocal(out=self.rq[0:NT, 0:128], in_=self.rq[0:NT, 0:128])

        n512 = T // 512
        nqt_grp = 4 if NT >= 4 else NT
        with nc.Block() as blk:
            mmb = self.V("sMM")
            fzb = self.V("sFZ")

            @blk.tensor
            def _(t_):
                for qg in range(n512):
                    if qg > 0:
                        t_.wait_ge(self.H("sFZ"), fzb + qg)
                    for jj in range(nqt_grp):
                        qt = nqt_grp * qg + jj
                        li = t_.matmul(ps_r[0:64, jj * 128:(jj + 1) * 128],
                                       self.sel_f[0:NT, qt * 64:(qt + 1) * 64],
                                       self.rq[0:NT, 0:128],
                                       start=(jj == 0), stop=(jj == nqt_grp - 1))
                    li.then_inc(self.H("sMM"), 1)
            self.inc("sMM", n512)

            @blk.vector
            def _(v):
                for qg in range(n512):
                    v.wait_ge(self.H("sMM"), mmb + qg + 1)
                    dst = ystage[0:64, qg * 512: qg * 512 + 512]
                    i = v.tensor_copy(out=dst, in_=ps_y[0:64, qg * 512: qg * 512 + 512])
                    self.edge(v, i)
                    v.tensor_tensor(out=dst, in0=dst, in1=ps_r[0:64, 0:512],
                                    op=ALU.mult).then_inc(self.H("sFZ"), 1)
            self.inc("sFZ", n512)

        with nc.Block() as blk:
            @blk.sync
            def _(s):
                self.mark(s.dma_start(out=y_spill.ap()[h * 64:(h + 1) * 64, :],
                                      in_=ystage[0:64, 0:T]), "sD", 16)
                s.wait_ge(self.H("sD"), self.V("sD"))
            if h < H - 1:
                @blk.vector
                def _(v):
                    v.memset(runmax[:], 0.0)


def _host_consts(T):
    half = DR // 2
    inv = (1.0 / 10000.0 ** (np.arange(half, dtype=np.float64) / half))
    ang = np.arange(T, dtype=np.float64)[None, :] * inv[:, None]
    cos16 = np.cos(ang)
    sin16 = np.sin(ang)
    cosT = np.tile(cos16, (8, 1)).astype(np.float32)
    sinT = np.tile(sin16, (8, 1)).astype(np.float32)
    identb = np.eye(128, dtype=np.float32).astype(ml_dtypes.bfloat16)
    identf = np.eye(128, dtype=np.float32)
    iota = np.arange(128, dtype=np.float32).reshape(128, 1)
    return cosT, sinT, identb, identf, iota


def _sel16():
    s = np.zeros((16, 16 * 64), dtype=np.float32)
    for qt in range(16):
        s[qt, qt * 64:(qt + 1) * 64] = 1.0
    return s


def _cmask():
    m = np.zeros((128, 128), dtype=np.float32)
    for k in range(128):
        m[k, :k] = -1e30
    return m


def _wqr_perm():
    perm = np.zeros(H * DR, dtype=np.int64)
    for hh in range(H):
        for halfsel in range(2):
            for i in range(16):
                tile = halfsel * 2 + hh // 8
                newc = tile * 128 + (hh % 8) * 16 + i
                perm[newc] = hh * 32 + halfsel * 16 + i
    return perm


_BUILD_CACHE = {}


def kernel(L0_attn_x, L0_attn_Wq, L0_attn_Wqr, L0_attn_Wkr, L0_attn_Wkvd,
           L0_attn_Wku, L0_attn_Wvu, L0_attn_Wo, L0_attn_lobo_log,
           L0_ffn_w_gate, L0_ffn_w_up, L0_ffn_w_down):
    T = 2048
    x = np.ascontiguousarray(np.asarray(L0_attn_x, dtype=np.float32))
    B = x.shape[0]
    cosT, sinT, identb, identf, iota = _host_consts(T)
    perm = _wqr_perm()
    wqr_p = np.ascontiguousarray(np.asarray(L0_attn_Wqr, dtype=np.float32)[:, perm])
    common = {
        "wq": np.ascontiguousarray(L0_attn_Wq, dtype=np.float32),
        "wqr": wqr_p,
        "wkr": np.ascontiguousarray(L0_attn_Wkr, dtype=np.float32),
        "wkvd": np.ascontiguousarray(L0_attn_Wkvd, dtype=np.float32),
        "wku": np.ascontiguousarray(L0_attn_Wku, dtype=np.float32),
        "wvu": np.ascontiguousarray(L0_attn_Wvu, dtype=np.float32),
        "wo": np.ascontiguousarray(L0_attn_Wo, dtype=np.float32),
        "lobo": np.ascontiguousarray(L0_attn_lobo_log, dtype=np.float32),
        "wup": np.ascontiguousarray(L0_ffn_w_up, dtype=np.float32),
        "wgate": np.ascontiguousarray(L0_ffn_w_gate, dtype=np.float32),
        "wdown": np.ascontiguousarray(L0_ffn_w_down, dtype=np.float32),
        "identb": identb, "identf": identf,
        "cosT": cosT, "sinT": sinT, "iota128": iota, "sel16": _sel16(),
        "cmask": _cmask(),
    }
    if T not in _BUILD_CACHE:
        _BUILD_CACHE[T] = KB(T).build()
    nc = _BUILD_CACHE[T]
    in_maps = [dict(common, x=np.ascontiguousarray(x[c])) for c in range(B)]
    res = run_bass_kernel_spmd(nc, in_maps, core_ids=list(range(B)))
    return np.stack([res.results[c]["out"] for c in range(B)], axis=0)


# version 10
# speedup vs baseline: 1.1632x; 1.0053x over previous
import math
from contextlib import ExitStack
import numpy as np
import ml_dtypes
import concourse.bass as bass
import concourse.mybir as mybir
from concourse.bass_utils import run_bass_kernel_spmd

F32 = mybir.dt.float32
BF16 = mybir.dt.bfloat16
AF = mybir.ActivationFunctionType
ALU = mybir.AluOpType
AX = mybir.AxisListType

NCORES = 8
E = 1024
H = 16
DH = 64
DR = 32
DKV = 256
I3 = 3072
EPS = 1e-5
SCALE_ATT = 1.0 / math.sqrt(DH + DR)
INVQ = 1.0 / 32767.0
MAGIC = 12582912.0

S_X0, S_X1 = 0, 1
S_S1, S_I1 = 2, 3
S_S2, S_C2 = 4, 5
S_S3, S_C3 = 6, 7
S_S4, S_C4 = 8, 9
S_S5, S_C5 = 10, 11
S_LU, S_LG, S_LGS, S_LOR = 12, 13, 14, 15
B_I1, B_C2, B_C3, B_S3, B_C4, B_C5, B_S5 = 0, 1, 2, 3, 4, 5, 6


def cdiv(a, b):
    return (a + b - 1) // b


class KB:
    def __init__(self, T):
        self.T = T
        self.NT = T // 128
        self.nc = bass.Bass("TRN2", target_bir_lowering=False)
        self.c = {}
        self.root = ExitStack()

    def sem(self, n):
        self.c[n] = [self.root.enter_context(self.nc.semaphore(n)), 0]

    def H(self, n):
        return self.c[n][0]

    def V(self, n):
        return self.c[n][1]

    def inc(self, n, k=1):
        self.c[n][1] += k
        return self.c[n][1]

    def mark(self, inst, n, k=1):
        inst.then_inc(self.H(n), k)
        return self.inc(n, k)

    def edge(self, eng, inst, n="sVo"):
        v = self.mark(inst, n)
        eng.wait_ge(self.H(n), v)

    def edges(self, eng, insts, n="sVo"):
        for i in insts:
            v = self.mark(i, n)
        eng.wait_ge(self.H(n), v)

    def sb(self, es, name, shape, dt, side="left"):
        return es.enter_context(self.nc.sbuf_tensor(name, shape, dt, side=side))

    def psum(self, es, name, shape, dt):
        return es.enter_context(self.nc.psum_tensor(name, shape, dt))

    def acc_mm(self, t_, out_ap, pairs):
        n = len(pairs)
        li = None
        for i, (l, r) in enumerate(pairs):
            li = t_.matmul(out_ap, l, r, start=(i == 0), stop=(i == n - 1))
        return li

    def mm_evac(self, items, emit_mms, emit_evac, bufs, evac_eng="vector"):
        nb = len(bufs)
        with self.nc.Block() as blk:
            tb = self.V("sMM")
            vb = self.V("sEV")

            @blk.tensor
            def _(t_):
                for k, it in enumerate(items):
                    if k >= nb:
                        t_.wait_ge(self.H("sEV"), vb + k - nb + 1)
                    li = emit_mms(t_, it, bufs[k % nb])
                    li.then_inc(self.H("sMM"), 1)
            self.inc("sMM", len(items))

            def body(eng):
                for k, it in enumerate(items):
                    eng.wait_ge(self.H("sMM"), tb + k + 1)
                    li = emit_evac(eng, it, bufs[k % nb])
                    li.then_inc(self.H("sEV"), 1)
            (blk.vector if evac_eng == "vector" else blk.scalar)(body)
            self.inc("sEV", len(items))

    def transpose_tm_to_fm(self, src_tm, dst_fm, pa, pb):
        T, NT = self.T, self.NT
        groups = []
        for fc in range(8):
            for half in range(cdiv(NT, 8)):
                groups.append((fc, half, list(range(half * 8, min((half + 1) * 8, NT)))))

        def mms(t_, it, buf):
            fc, half, tts = it
            li = None
            for j, tt in enumerate(tts):
                li = t_.transpose(buf[:, j * 128:(j + 1) * 128],
                                  src_tm[:, tt * E + fc * 128: tt * E + (fc + 1) * 128],
                                  self.idb[:])
            return li

        def ev(v, it, buf):
            fc, half, tts = it
            n = len(tts) * 128
            return v.tensor_copy(out=dst_fm[:, fc * T + half * 1024: fc * T + half * 1024 + n],
                                 in_=buf[:, 0:n])

        self.mm_evac(groups, mms, ev, [pa, pb])

    def ln_stats(self, src_fn, n_tiles):
        with self.nc.Block() as blk:
            @blk.vector
            def _(v):
                for t in range(n_tiles):
                    ap = src_fn(t)
                    v.bn_stats(out=self.lnstat[:, 12 * t:12 * t + 6], in_=ap[:, 0:512])
                    v.bn_stats(out=self.lnstat[:, 12 * t + 6:12 * t + 12], in_=ap[:, 512:1024])

    def ln_finish(self, n_tiles):
        nc = self.nc
        mv2 = self.lnmv[:].rearrange("p (t two) -> p t two", two=2)
        ab2 = self.lnab[:].rearrange("p (t two) -> p t two", two=2)
        with nc.Block() as blk:
            @blk.vector
            def _(v):
                for t in range(n_tiles):
                    i = v.bn_aggr(out=self.lnmv[:, 2 * t:2 * t + 2],
                                  in_=self.lnstat[:, 12 * t:12 * t + 12])
                self.edge(v, i)
                v.tensor_scalar(out=mv2[:, 0:n_tiles, 1:2], in0=mv2[:, 0:n_tiles, 1:2],
                                scalar1=EPS, scalar2=None, op0=ALU.add)
        with nc.Block() as blk:
            @blk.scalar
            def _(sc):
                i1 = sc.activation(out=ab2[:, 0:n_tiles, 0:1], in_=mv2[:, 0:n_tiles, 1:2],
                                   func=AF.Ln, bias=0.0, scale=1.0)
                self.edge(sc, i1, "sSo")
                sc.activation(out=ab2[:, 0:n_tiles, 0:1], in_=ab2[:, 0:n_tiles, 0:1],
                              func=AF.Exp, bias=0.0, scale=-0.5)
        with nc.Block() as blk:
            @blk.vector
            def _(v):
                i1 = v.tensor_tensor(out=ab2[:, 0:n_tiles, 1:2], in0=mv2[:, 0:n_tiles, 0:1],
                                     in1=ab2[:, 0:n_tiles, 0:1], op=ALU.mult)
                self.edge(v, i1)
                v.tensor_scalar(out=ab2[:, 0:n_tiles, 1:2], in0=ab2[:, 0:n_tiles, 1:2],
                                scalar1=-1.0, scalar2=None, op0=ALU.mult)

    def allreduce_max(self, n):
        nc = self.nc
        with nc.Block() as blk:
            @blk.gpsimd
            def _(g):
                self.mark(g.dma_start(out=self.cc_in.ap()[0:1, 0:8],
                                      in_=self.srow[0:1, 0:8]), "sG", 16)
                g.wait_ge(self.H("sG"), self.V("sG"))
                self.mark(g.collective_compute("AllReduce", ALU.max,
                                               replica_groups=[list(range(NCORES))],
                                               ins=[self.cc_in.ap().opt()],
                                               outs=[self.cc_out.ap().opt()]), "sC", 1)
                g.wait_ge(self.H("sC"), self.V("sC"))
                self.mark(g.dma_start(out=self.srow[0:1, 0:n],
                                      in_=self.cc_out.ap()[0:1, 0:n]), "sG", 16)
                g.wait_ge(self.H("sG"), self.V("sG"))

    def bcast_scalars(self, pairs):
        nc = self.nc
        with nc.Block() as blk:
            @blk.gpsimd
            def _(g):
                for i, (sc_, _dc) in enumerate(pairs):
                    self.mark(g.dma_start(out=self.bc_d.ap()[0:1, i:i + 1],
                                          in_=self.srow[0:1, sc_:sc_ + 1]), "sG", 16)
                g.wait_ge(self.H("sG"), self.V("sG"))
                for i, (_sc, dc) in enumerate(pairs):
                    self.mark(g.dma_start(out=self.scal[:, dc:dc + 1],
                                          in_=self.bc_d.ap()[0:1, i:i + 1].broadcast_to((128, 1))),
                              "sG", 16)
                g.wait_ge(self.H("sG"), self.V("sG"))

    def global_absmax(self, pcol_ap, srow_col):
        with self.nc.Block() as blk:
            @blk.gpsimd
            def _(g):
                g.tensor_reduce(out=self.srow[0:1, srow_col:srow_col + 1], in_=pcol_ap,
                                axis=AX.XYZWC, op=ALU.max, apply_absolute_value=True)

    def emit_round(self, v, in_ap, c_ap, out_ap, y_ap, m_ap=None):
        i1 = v.tensor_scalar(out=y_ap, in0=in_ap, scalar1=c_ap, scalar2=MAGIC,
                             op0=ALU.mult, op1=ALU.add)
        self.edge(v, i1)
        return v.tensor_scalar(out=out_ap, in0=y_ap, scalar1=MAGIC, scalar2=None,
                               op0=ALU.subtract)

    def build(self):
        nc, T, NT = self.nc, self.T, self.NT
        n512 = T // 512
        d = nc.dram_tensor
        x_in = d("x", [T, E], F32, kind="ExternalInput")
        wq_d = d("wq", [E, H * DH], F32, kind="ExternalInput")
        wqr_d = d("wqr", [E, H * DR], F32, kind="ExternalInput")
        wkr_d = d("wkr", [E, DR], F32, kind="ExternalInput")
        wkvd_d = d("wkvd", [E, DKV], F32, kind="ExternalInput")
        wku_d = d("wku", [DKV, H * DH], F32, kind="ExternalInput")
        wvu_d = d("wvu", [DKV, H * DH], F32, kind="ExternalInput")
        wo_d = d("wo", [E, E], F32, kind="ExternalInput")
        lobo_d = d("lobo", [H], F32, kind="ExternalInput")
        wup_d = d("wup", [I3, E], F32, kind="ExternalInput")
        wgate_d = d("wgate", [I3, E], F32, kind="ExternalInput")
        wdown_d = d("wdown", [E, I3], F32, kind="ExternalInput")
        identb_d = d("identb", [128, 128], BF16, kind="ExternalInput")
        identf_d = d("identf", [128, 128], F32, kind="ExternalInput")
        cosT_d = d("cosT", [128, T], F32, kind="ExternalInput")
        sinT_d = d("sinT", [128, T], F32, kind="ExternalInput")
        iota_d = d("iota128", [128, 1], F32, kind="ExternalInput")
        cmask_d = d("cmask", [128, 128], F32, kind="ExternalInput")
        out_d = d("out", [T, E], F32, kind="ExternalOutput")
        r_spill = d("r_spill", [T, E], F32)
        r_d = d("r_bounce", [4, 128], F32)
        u_spill = d("u_spill", [I3, T], BF16)
        y_spill = d("y_spill", [H * 64, T], BF16)
        g_spill = d("g_spill", [I3, T], BF16)
        self.cc_in = d("cc_in", [1, 8], F32)
        self.cc_out = d("cc_out", [1, 8], F32)
        self.bc_d = d("bc_d", [1, 16], F32)

        for s in ["sD", "sG", "sC", "sMM", "sEV", "sEX", "sMK", "sTM", "sAV",
                  "sFZ", "sVo", "sSo", "sDQ", "sMX", "sDR"]:
            self.sem(s)

        rs = self.root
        self.idb = self.sb(rs, "idb", [128, 128], BF16)
        self.idf = self.sb(rs, "idf", [128, 128], F32)
        self.iota_sb = self.sb(rs, "iota", [128, 1], F32)
        self.c128_sb = self.sb(rs, "c128", [128, 1], F32)
        self.ones_b = self.sb(rs, "onesb", [128, 128], BF16)
        self.ones_f = self.sb(rs, "onesf", [128, 64], F32)
        self.lnstat = self.sb(rs, "lnstat", [128, 192], F32)
        self.lnmv = self.sb(rs, "lnmv", [128, 32], F32)
        self.lnab = self.sb(rs, "lnab", [128, 32], F32)
        self.scal = self.sb(rs, "scal", [128, 8], F32)
        self.srow = self.sb(rs, "srow", [1, 16], F32)
        self.crow_b = self.sb(rs, "crow_b", [1, 16], BF16)
        self.crep = self.sb(rs, "crep", [128, 16], F32)
        self.junk = self.sb(rs, "junk", [128, 1], F32)
        self.pstage = self.sb(rs, "pstage", [128, 32], F32)
        self.mstage = self.sb(rs, "mstage", [128, 16], BF16)
        self.cmask = self.sb(rs, "cmask_sb", [128, 128], F32)

        att = ExitStack()
        cos_sb = self.sb(att, "cos_sb", [128, T], F32)
        sin_sb = self.sb(att, "sin_sb", [128, T], F32)
        wvu_sb = self.sb(att, "wvu_sb", [128, 2 * 1024], BF16)
        wq_gs = [self.sb(att, "wq_g%d" % p, [128, 8 * 256], BF16) for p in range(2)]
        wku_gs = [self.sb(att, "wku_g%d" % p, [128, 2 * 256], BF16) for p in range(2)]
        wqr_gs = [self.sb(att, "wqr_g%d" % p, [128, 8 * 128], BF16) for p in range(2)]
        krr_a = self.sb(att, "krr_a", [16, T], BF16)
        krr_b = self.sb(att, "krr_b", [16, T], BF16)
        hp_q = self.sb(att, "hp_q", [64, 2 * T], BF16)
        rtmp = self.sb(att, "rtmp", [64, 2048], F32)
        s_fm = ExitStack()
        xn1_fm = self.sb(s_fm, "xn1_fm", [128, 8 * T], BF16)
        latent = self.sb(s_fm, "latent", [128, 2 * T], BF16)
        s_pro = ExitStack()
        wkvd_sb = self.sb(s_pro, "wkvd_sb", [128, 8 * 256], BF16)
        wkr_sb = self.sb(s_pro, "wkr_sb", [128, 8 * DR], BF16)
        s_rtm = ExitStack()
        r_tm = self.sb(s_rtm, "r_tm", [128, NT * E], F32)
        s_xn1t = ExitStack()
        xn1_tm = self.sb(s_xn1t, "xn1_tm", [128, NT * E], BF16, side="right")

        with nc.Block() as blk:
            @blk.sync
            def _(s):
                for dst, srcd in [(self.idb, identb_d), (self.idf, identf_d),
                                  (self.iota_sb, iota_d)]:
                    self.mark(s.dma_start(out=dst[:], in_=srcd.ap()), "sD", 16)
                self.mark(s.dma_start(out=r_tm[:].rearrange("p (t e) -> p t e", e=E),
                                      in_=x_in.ap().rearrange("(t p) e -> p t e", p=128)), "sD", 16)
                self.mark(s.dma_start(out=cos_sb[:], in_=cosT_d.ap()), "sD", 16)
                self.mark(s.dma_start(out=self.cmask[:], in_=cmask_d.ap()), "sD", 16)
                self.mark(s.dma_start(out=sin_sb[:], in_=sinT_d.ap()), "sD", 16)
            @blk.gpsimd
            def _(g):
                for dst, srcd, ch in [(wkvd_sb, wkvd_d, 8), (wkr_sb, wkr_d, 8),
                                      (wvu_sb, wvu_d, 2)]:
                    self.mark(g.dma_start(out=dst[:].rearrange("p (c f) -> p c f", c=ch),
                                          in_=srcd.ap().rearrange("(c p) f -> p c f", p=128)),
                              "sG", 16)
                self.mark(g.dma_start(out=self.srow[0:1, 0:H],
                                      in_=lobo_d.ap().unsqueeze(0)), "sG", 16)
            for name in ["vector", "scalar", "tensor"]:
                def wtr(eng):
                    eng.wait_ge(self.H("sD"), self.V("sD"))
                    eng.wait_ge(self.H("sG"), self.V("sG"))
                getattr(blk, name)(wtr)
            @blk.vector
            def _(v):
                v.memset(self.c128_sb[:], 128.0)
                v.memset(self.ones_f[:], 1.0)
                v.memset(self.ones_b[:], 1.0)
                v.memset(self.scal[:, 7:8], MAGIC)

        self.ln_stats(lambda t: r_tm[:, t * E:(t + 1) * E], NT)
        self.ln_finish(NT)
        with nc.Block() as blk:
            @blk.scalar
            def _(sc):
                for t in range(NT):
                    sc.activation(out=xn1_tm[:, t * E:(t + 1) * E],
                                  in_=r_tm[:, t * E:(t + 1) * E], func=AF.Identity,
                                  scale=self.lnab[:, 2 * t:2 * t + 1],
                                  bias=self.lnab[:, 2 * t + 1:2 * t + 2])
        with nc.Block() as blk:
            @blk.sync
            def _(s):
                self.mark(s.dma_start(out=r_spill.ap().rearrange("(t p) e -> p t e", p=128),
                                      in_=r_tm[:].rearrange("p (t e) -> p t e", e=E)), "sD", 16)
                s.wait_ge(self.H("sD"), self.V("sD"))
        s_rtm.close()

        pstr = ExitStack()
        pa = self.psum(pstr, "psA", [128, 1024], BF16)
        pb = self.psum(pstr, "psB", [128, 1024], BF16)
        self.transpose_tm_to_fm(xn1_tm, xn1_fm, pa, pb)
        pstr.close()
        s_xn1t.close()

        att_r = ExitStack()
        combQs = [self.sb(att_r, "combQ%d" % p, [128, 4 * T], BF16, side="right")
                  for p in range(2)]
        combKs = [self.sb(att_r, "combK%d" % p, [128, 4 * T], BF16, side="right")
                  for p in range(2)]
        vaugs = [self.sb(att_r, "vaug%d" % p, [128, NT * 4 * 65], BF16, side="right")
                 for p in range(2)]
        E_sb = self.sb(att_r, "E_sb", [128, 2 * 512], BF16, side="right")
        runmax = self.sb(att_r, "runmax", [128, T], BF16, side="right")
        ystage = self.sb(att_r, "ystage", [64, T], BF16, side="right")
        drow = self.sb(att_r, "drow", [65, 512], F32, side="right")
        dq_sb = self.sb(att_r, "dq_sb", [4, 128], F32, side="right")
        mq_sb = self.sb(att_r, "mq_sb", [4, 128], F32, side="right")
        rq_sb = self.sb(att_r, "rq_sb", [4, 128], F32, side="right")
        rbc = self.sb(att_r, "rbc", [64, 512], F32, side="right")

        php = ExitStack()
        ps_y = self.psum(php, "ps_y", [65, T], F32)
        ps_s0 = self.psum(php, "ps_s0", [128, 512], F32)
        ps_s1 = self.psum(php, "ps_s1", [128, 512], F32)
        psA = self.psum(php, "psA_att", [128, 1024], BF16)
        psB = self.psum(php, "psB_att", [128, 512], F32)
        ps_s = [ps_s0, ps_s1]
        ps_a = ps_s0.ap()
        ps_b = ps_s1.ap()
        psA_bf = psA.ap()

        with nc.Block() as blk:
            @blk.scalar
            def _(sc):
                sc.activation(out=self.srow[0:1, 0:H], in_=self.srow[0:1, 0:H],
                              func=AF.Exp, bias=0.0, scale=1.0)
        with nc.Block() as blk:
            @blk.vector
            def _(v):
                v.tensor_copy(out=self.crow_b[0:1, 0:H], in_=self.srow[0:1, 0:H])
        with nc.Block() as blk:
            @blk.tensor
            def _(t_):
                t_.matmul(ps_a[0:128, 0:16], self.ones_b[0:1, 0:128],
                          self.crow_b[0:1, 0:16], start=True, stop=True)
        with nc.Block() as blk:
            @blk.vector
            def _(v):
                i = v.tensor_copy(out=self.crep[:, 0:H], in_=ps_a[0:128, 0:16])
                i.then_inc(self.H("sEV"), 1)
                self.inc("sEV")
                for p in range(2):
                    v.memset(vaugs[p][:].rearrange("p (t h c) -> p t h c",
                                                   h=4, c=65)[:, :, :, 64:65], 1.0)

        def lat_mm(t_, it, buf):
            c, tc = it
            return self.acc_mm(t_, buf[0:128, 0:512],
                        [(wkvd_sb[:, ci * DKV + c * 128: ci * DKV + (c + 1) * 128],
                          xn1_fm[:, ci * T + tc * 512: ci * T + tc * 512 + 512])
                         for ci in range(8)])

        def lat_ev(v, it, buf):
            c, tc = it
            return v.tensor_copy(out=latent[:, c * T + tc * 512: c * T + tc * 512 + 512],
                                 in_=buf[0:128, 0:512])

        self.mm_evac([(c, tc) for c in range(2) for tc in range(n512)], lat_mm, lat_ev,
                     [ps_a, ps_b])

        def kr_mm(t_, tc, buf):
            self.acc_mm(t_, buf[0:16, 0:512],
                        [(wkr_sb[:, ci * DR: ci * DR + 16],
                          xn1_fm[:, ci * T + tc * 512: ci * T + tc * 512 + 512])
                         for ci in range(8)])
            return self.acc_mm(t_, buf[32:48, 0:512],
                        [(wkr_sb[:, ci * DR + 16: ci * DR + 32],
                          xn1_fm[:, ci * T + tc * 512: ci * T + tc * 512 + 512])
                         for ci in range(8)])

        def kr_ev(v, tc, buf):
            if tc >= 1:
                v.wait_ge(self.H("sEV"), self.V("sEV") + tc)
                v.wait_ge(self.H("sVo"), self.V("sVo"))
            cs = cos_sb[0:16, tc * 512:(tc + 1) * 512]
            sn = sin_sb[0:16, tc * 512:(tc + 1) * 512]
            fh, sh = buf[0:16, 0:512], buf[32:48, 0:512]
            i1 = v.tensor_tensor(out=rtmp[0:16, 0:512], in0=sh, in1=sn, op=ALU.mult)
            i2 = v.tensor_tensor(out=rtmp[0:16, 512:1024], in0=fh, in1=sn, op=ALU.mult)
            i3 = v.tensor_tensor(out=rtmp[0:16, 1024:1536], in0=fh, in1=cs, op=ALU.mult)
            i4 = v.tensor_tensor(out=rtmp[0:16, 1536:2048], in0=sh, in1=cs, op=ALU.mult)
            self.edges(v, [i1, i2, i3, i4])
            i5 = v.tensor_sub(out=krr_a[0:16, tc * 512:(tc + 1) * 512],
                              in0=rtmp[0:16, 1024:1536], in1=rtmp[0:16, 0:512])
            self.mark(i5, "sVo")
            return v.tensor_add(out=krr_b[0:16, tc * 512:(tc + 1) * 512],
                                in0=rtmp[0:16, 1536:2048], in1=rtmp[0:16, 512:1024])

        self.mm_evac(list(range(n512)), kr_mm, kr_ev, [ps_a, ps_b])
        s_pro.close()

        def grp_loads(eng, g):
            par = g % 2
            self.mark(eng.dma_start(
                out=wq_gs[par][:].rearrange("p (c f) -> p c f", c=8),
                in_=wq_d.ap().rearrange("(c p) f -> p c f", p=128)[:, :, g * 256:(g + 1) * 256]),
                "sG", 16)
            self.mark(eng.dma_start(
                out=wku_gs[par][:].rearrange("p (c f) -> p c f", c=2),
                in_=wku_d.ap().rearrange("(c p) f -> p c f", p=128)[:, :, g * 256:(g + 1) * 256]),
                "sG", 16)
            fh0 = (g // 2) * 128 + (g % 2) * 64
            sh0 = (2 + g // 2) * 128 + (g % 2) * 64
            self.mark(eng.dma_start(
                out=wqr_gs[par][:].rearrange("p (c f) -> p c f", c=8)[:, :, 0:64],
                in_=wqr_d.ap().rearrange("(c p) f -> p c f", p=128)[:, :, fh0:fh0 + 64]),
                "sG", 16)
            self.mark(eng.dma_start(
                out=wqr_gs[par][:].rearrange("p (c f) -> p c f", c=8)[:, :, 64:128],
                in_=wqr_d.ap().rearrange("(c p) f -> p c f", p=128)[:, :, sh0:sh0 + 64]),
                "sG", 16)

        with nc.Block() as blk:
            @blk.gpsimd
            def _(g):
                grp_loads(g, 0)
                g.wait_ge(self.H("sG"), self.V("sG"))
            @blk.tensor
            def _(t_):
                t_.wait_ge(self.H("sG"), self.V("sG"))

        def qc_mm(t_, it, buf):
            j, tc = it
            return self.acc_mm(t_, buf[0:64, 0:512],
                        [(wq_gs[0][:, ci * 256 + j * 64: ci * 256 + (j + 1) * 64],
                          xn1_fm[:, ci * T + tc * 512: ci * T + tc * 512 + 512])
                         for ci in range(8)])

        def qc_ev(v, it, buf):
            j, tc = it
            return v.tensor_copy(out=combQs[0][0:64, j * T + tc * 512: j * T + tc * 512 + 512],
                                 in_=buf[0:64, 0:512])

        self.mm_evac([(j, tc) for j in range(4) for tc in range(n512)], qc_mm, qc_ev,
                     [ps_a, ps_b])

        def kc_mm(t_, it, buf):
            j, tc = it
            return self.acc_mm(t_, buf[0:64, 0:512],
                        [(wku_gs[0][:, c * 256 + j * 64: c * 256 + (j + 1) * 64],
                          latent[:, c * T + tc * 512: c * T + tc * 512 + 512])
                         for c in range(2)])

        def kc_ev(v, it, buf):
            j, tc = it
            return v.tensor_copy(out=combKs[0][0:64, j * T + tc * 512: j * T + tc * 512 + 512],
                                 in_=buf[0:64, 0:512])

        self.mm_evac([(j, tc) for j in range(4) for tc in range(n512)], kc_mm, kc_ev,
                     [ps_a, ps_b])

        def qr_mm(t_, it, buf):
            tc, part = it
            return self.acc_mm(t_, buf[0:64, 0:512],
                        [(wqr_gs[0][:, ci * 128 + part * 64: ci * 128 + part * 64 + 64],
                          xn1_fm[:, ci * T + tc * 512: ci * T + tc * 512 + 512])
                         for ci in range(8)])

        def qr_ev(v, it, buf):
            tc, part = it
            if part == 0 and tc >= 1:
                v.wait_ge(self.H("sEV"), self.V("sEV") + 2 * tc)
                v.wait_ge(self.H("sVo"), self.V("sVo"))
            dst = rtmp[0:64, 0:512] if part == 0 else rtmp[0:64, 512:1024]
            i0 = v.tensor_copy(out=dst, in_=buf[0:64, 0:512])
            if part == 0:
                return i0
            v.wait_ge(self.H("sEV"), self.V("sEV") + 2 * tc + 1)
            self.edge(v, i0)
            cs = cos_sb[0:64, tc * 512:(tc + 1) * 512]
            sn = sin_sb[0:64, tc * 512:(tc + 1) * 512]
            fh, sh = rtmp[0:64, 0:512], rtmp[0:64, 512:1024]
            i1 = v.tensor_tensor(out=rtmp[0:64, 1024:1536], in0=sh, in1=sn, op=ALU.mult)
            i2 = v.tensor_tensor(out=rtmp[0:64, 1536:2048], in0=fh, in1=sn, op=ALU.mult)
            self.edges(v, [i1, i2])
            i3 = v.tensor_tensor(out=fh, in0=fh, in1=cs, op=ALU.mult)
            i4 = v.tensor_tensor(out=sh, in0=sh, in1=cs, op=ALU.mult)
            self.edges(v, [i3, i4])
            i5 = v.tensor_sub(out=hp_q[0:64, tc * 512: tc * 512 + 512],
                              in0=fh, in1=rtmp[0:64, 1024:1536])
            self.mark(i5, "sVo")
            return v.tensor_add(out=hp_q[0:64, T + tc * 512: T + tc * 512 + 512],
                                in0=sh, in1=rtmp[0:64, 1536:2048])

        self.mm_evac([(tc, part) for tc in range(n512) for part in range(2)],
                     qr_mm, qr_ev, [ps_a, ps_b])

        with nc.Block() as blk:
            @blk.sync
            def _(s):
                for j in range(4):
                    self.mark(s.dma_start(out=combQs[0][64:80, j * T:(j + 1) * T],
                                          in_=hp_q[j * 16:(j + 1) * 16, 0:T]), "sD", 16)
                    self.mark(s.dma_start(out=combQs[0][80:96, j * T:(j + 1) * T],
                                          in_=hp_q[j * 16:(j + 1) * 16, T:2 * T]), "sD", 16)
                    self.mark(s.dma_start(out=combKs[0][64:80, j * T:(j + 1) * T],
                                          in_=krr_a[0:16, 0:T]), "sD", 16)
                    self.mark(s.dma_start(out=combKs[0][80:96, j * T:(j + 1) * T],
                                          in_=krr_b[0:16, 0:T]), "sD", 16)
                s.wait_ge(self.H("sD"), self.V("sD"))

        def v_mm(t_, tt, buf):
            return self.acc_mm(t_, buf[0:128, 0:256],
                        [(latent[:, c * T + tt * 128: c * T + (tt + 1) * 128],
                          wvu_sb[:, c * 1024 + 0 * 256: c * 1024 + 1 * 256])
                         for c in range(2)])

        def v_ev(v, tt, buf):
            return v.tensor_copy(
                out=vaugs[0][:, tt * 260:(tt + 1) * 260].rearrange("p (h c) -> p h c", c=65)[:, :, 0:64],
                in_=buf[0:128, 0:256].rearrange("p (h c) -> p h c", c=64))

        self.mm_evac(list(range(NT)), v_mm, v_ev, [ps_a, ps_b])

        NB = n512
        PPH = sum(4 * b + 4 for b in range(NB))
        pieces = []
        for h in range(H):
            for b in range(NB - 1, -1, -1):
                for kt in range(4 * b + 4):
                    offp = b * 512 if kt < 4 * b else kt * 128
                    nn = 512 if kt < 4 * b else (b + 1) * 512 - kt * 128
                    pieces.append((h, b, kt, offp, nn))
        NPIE = len(pieces)
        geidx = {}
        for gi, pc in enumerate(pieces):
            geidx[(pc[0], pc[1])] = gi

        import os as _os
        pipe = bool(_os.environ.get("K_PIPE"))
        TRIG = ([("drow", 3), ("mT", 3), ("dq", 4), ("mred", 5), ("mqT", 6),
                 ("mqc", 7), ("rq", 8), ("rdma", 9), ("ys0", 12),
                 ("ys1", 13), ("spill", 14)] if pipe else
                [(nm, 1) for nm in ("drow", "mT", "dq", "mred", "mqT", "mqc",
                                    "rq", "rdma", "ys0", "ys1", "spill")])
        _tallow = _os.environ.get("K_TRIG")
        trig_at = {}
        for h in range(H):
            for b in reversed(range(NB)):
                for nm, o in TRIG:
                    if _tallow is not None and nm not in _tallow.split(","):
                        continue
                    trig_at.setdefault(geidx[(h, b)] + o, []).append((nm, h, b))

        tiles_at, repack_at, loads_at = {}, {}, {}
        grp_ntiles = {}
        for g in range(1, 4):
            tl = []
            for tc in range(n512):
                for part in range(2):
                    for hf in range(2):
                        tl.append(("qr", g, tc, part, hf))
                tl.append(("rope", g, tc))
            for j in range(4):
                for tc in range(n512):
                    for hf in range(2):
                        tl.append(("qc", g, j, tc, hf))
            for j in range(4):
                for tc in range(n512):
                    for hf in range(2):
                        tl.append(("kc", g, j, tc, hf))
            for tt in range(NT):
                tl.append(("va", g, tt))
            hh = 4 * (g - 1)
            base = 0 if hh == 0 else geidx[(hh - 1, 0)] + 1
            span = (7 * PPH // 2) if pipe else 3 * PPH
            ntl = len(tl)
            grp_ntiles[g] = ntl
            for s_, t_ in enumerate(tl):
                tiles_at.setdefault(base + s_ * span // ntl, []).append(t_)
            last_rope = 5 * n512 - 1
            repack_at.setdefault(base + last_rope * span // ntl + 2, []).append(g)
            loads_at.setdefault(base, []).append(g)

        DBG = int(_os.environ.get("K_DBG", "0"))
        EVs = {k: [] for k in ("t", "v", "c", "y", "g")}
        cnt = {s: self.V(s) for s in ("sMM", "sEV", "sEX", "sTM", "sAV",
                                      "sD", "sG", "sMX", "sDR", "sVo")}
        post = {("evg", 0): cnt["sEV"], ("rpk", 0): cnt["sD"],
                ("pmmlast", 0): ("sMM", cnt["sMM"])}

        def ADD(engk, waits, fn):
            w = {}
            for s_, v_ in waits:
                if v_ is not None and v_ > 0:
                    w[s_] = max(w.get(s_, 0), v_)
            EVs[engk].append((sorted(w.items()), fn))

        def prevbank(h, b):
            return (h, b + 1) if b + 1 < NB else (h - 1, 0)

        ptile_k = {1: 0, 2: 0, 3: 0}

        maxg = NPIE + 16 if DBG == 0 else (0 if DBG == 1 else NPIE + 16)
        if DBG == 2:
            trig_at = {}
        if DBG in (2, 3):
            tiles_at, repack_at, loads_at = {}, {}, {}
        for g in range(maxg):
            for gg in loads_at.get(g, []):
                pm = post[("pmmlast", gg - 2)] if gg >= 2 else None
                pv = cnt["sG"] + 64

                def ld_fn(eng, gg=gg, pv=pv):
                    grp_loads(eng, gg)
                    eng.wait_ge(self.H("sG"), pv)
                ADD("g", ([(pm[0], pm[1])] if pm else []), ld_fn)
                post[("loads", gg)] = pv
                cnt["sG"] += 64

            for gg in repack_at.get(g, []):
                par = gg % 2
                w = [("sEV", post[("rope", gg, n512 - 1)])]
                if gg >= 2:
                    w.append(("sMM", post[("mm", geidx[(4 * (gg - 2) + 3, 0)])]))

                pv = cnt["sD"] + 256

                def rpk_fn(s, par=par, pv=pv):
                    for j in range(4):
                        for dst, src in [(combQs[par][64:80, j * T:(j + 1) * T],
                                          hp_q[j * 16:(j + 1) * 16, 0:T]),
                                         (combQs[par][80:96, j * T:(j + 1) * T],
                                          hp_q[j * 16:(j + 1) * 16, T:2 * T]),
                                         (combKs[par][64:80, j * T:(j + 1) * T],
                                          krr_a[0:16, 0:T]),
                                         (combKs[par][80:96, j * T:(j + 1) * T],
                                          krr_b[0:16, 0:T])]:
                            s.dma_start(out=dst, in_=src).then_inc(self.H("sD"), 16)
                    s.wait_ge(self.H("sD"), pv)
                ADD("y", w, rpk_fn)
                cnt["sD"] += 256
                post[("rpk", gg)] = cnt["sD"]

            def emit_av(gp):
                h, b, kt, offp, nn = pieces[gp]
                par, j = (h // 4) % 2, h % 4
                slot = gp % 2
                w = [("sEX", post[("exp", gp)])]
                if gp < 2:
                    w.append(("sEV", post[("evg", 0)]))
                if kt == 0 and h > 0:
                    w += [("sTM", post.get(("ysTT", h - 1, b, 1))),
                          ("sDR", post.get(("drow", h - 1, b)))]

                def fn(t_, h=h, b=b, kt=kt, offp=offp, nn=nn, par=par, j=j, slot=slot):
                    t_.matmul(ps_y[0:65, offp:offp + nn],
                              vaugs[par][:, kt * 260 + j * 65: kt * 260 + (j + 1) * 65],
                              E_sb[:, slot * 512: slot * 512 + nn],
                              start=(kt == 0), stop=(kt == 4 * b + 3)
                              ).then_inc(self.H("sAV"), 1)
                ADD("t", w, fn)
                cnt["sAV"] += 1
                post[("av", gp)] = cnt["sAV"]

            if g < NPIE:
                h, b, kt, offp, nn = pieces[g]
                par, j = (h // 4) % 2, h % 4
                slot = g % 2
                diag = (offp == kt * 128)
                first_piece = (b == NB - 1 and kt == 0)
                w = []
                if g >= 2:
                    w.append(("sEX", post[("exp", g - 2)]))
                if first_piece and h % 4 == 0:
                    grp = h // 4
                    if grp == 0:
                        w += [("sEV", post[("evg", 0)]), ("sD", post[("rpk", 0)])]
                    else:
                        w += [("sEV", post.get(("pev", grp, grp_ntiles[grp] - 1 - n512))),
                              ("sD", post.get(("rpk", grp)))]

                def sc_fn(t_, nn=nn, par=par, j=j, kt=kt, offp=offp, diag=diag, slot=slot):
                    i1 = t_.matmul(ps_s[slot].ap()[0:128, 0:nn],
                                   combKs[par][0:96, j * T + kt * 128: j * T + (kt + 1) * 128],
                                   combQs[par][0:96, j * T + offp: j * T + offp + nn],
                                   start=True, stop=not diag)
                    i1.then_inc(self.H("sMM"), 1)
                    if diag:
                        t_.matmul(ps_s[slot].ap()[0:128, 0:128], self.idf[:],
                                  self.cmask[:, 0:128], start=False, stop=True
                                  ).then_inc(self.H("sMM"), 1)
                ADD("t", w, sc_fn)
                cnt["sMM"] += 2 if diag else 1
                post[("mm", g)] = cnt["sMM"]
                if g >= 1:
                    emit_av(g - 1)
            if g == NPIE:
                emit_av(NPIE - 1)

            for (nm, h, b) in trig_at.get(g, []):
                pb = prevbank(h, b)
                if nm == "drow":
                    w = [("sAV", post[("av", geidx[(h, b)])])]
                    if ("dq",) + pb in post:
                        w.append(("sD", post[("dq",) + pb]))

                    def fn(sc, b=b):
                        sc.activation(out=drow[64:65, 0:512],
                                      in_=ps_y[64:65, b * 512:(b + 1) * 512],
                                      func=AF.Identity, bias=0.0, scale=1.0
                                      ).then_inc(self.H("sDR"), 1)
                    ADD("c", w, fn)
                    cnt["sDR"] += 1
                    post[("drow", h, b)] = cnt["sDR"]
                elif nm == "mT":
                    w = [("sTM", post[("tm", geidx[(h, b)])])]
                    if ("mred",) + pb in post:
                        w.append(("sTM", post[("mred",) + pb]))

                    def fn(t_, b=b):
                        for qi in range(4):
                            t_.transpose(psA_bf[0:128, qi * 128:(qi + 1) * 128],
                                         runmax[:, (4 * b + qi) * 128:(4 * b + qi + 1) * 128],
                                         self.idb[:]).then_inc(self.H("sMX"), 1)
                    ADD("t", w, fn)
                    cnt["sMX"] += 4
                    post[("mT", h, b)] = cnt["sMX"]
                elif nm == "dq":
                    w = [("sDR", post[("drow", h, b)])]
                    if ("recip",) + pb in post:
                        w.append(("sEV", post[("recip",) + pb]))
                        w.append(("sVo", post[("rqstt",) + pb]))
                    pv = cnt["sD"] + 64

                    def fn(s, pv=pv):
                        for qi in range(4):
                            s.dma_start(out=dq_sb[qi:qi + 1, 0:128],
                                        in_=drow[64:65, qi * 128:(qi + 1) * 128]
                                        ).then_inc(self.H("sD"), 16)
                        s.wait_ge(self.H("sD"), pv)
                    ADD("y", w, fn)
                    cnt["sD"] += 64
                    post[("dq", h, b)] = cnt["sD"]
                elif nm == "mred":
                    w = [("sMX", post[("mT", h, b)])]

                    def fn(v, b=b):
                        for qi in range(4):
                            v.tensor_reduce(out=self.mstage[:, 4 * b + qi: 4 * b + qi + 1],
                                            in_=psA_bf[0:128, qi * 128:(qi + 1) * 128],
                                            axis=AX.X, op=ALU.max).then_inc(self.H("sTM"), 1)
                    ADD("v", w, fn)
                    cnt["sTM"] += 4
                    post[("mred", h, b)] = cnt["sTM"]
                elif nm == "mqT":
                    w = [("sTM", post[("mred", h, b)])]
                    if ("mqc",) + pb in post:
                        w.append(("sEV", post[("mqc",) + pb]))

                    def fn(t_, b=b):
                        t_.transpose(psA_bf[0:4, 512:640], self.mstage[:, 4 * b:4 * b + 4],
                                     self.idb[:]).then_inc(self.H("sMX"), 1)
                    ADD("t", w, fn)
                    cnt["sMX"] += 1
                    post[("mqT", h, b)] = cnt["sMX"]
                elif nm == "mqc":
                    w = [("sMX", post[("mqT", h, b)])]
                    if ("recip",) + pb in post:
                        w.append(("sEV", post[("recip",) + pb]))
                        w.append(("sVo", post[("rqstt",) + pb]))

                    def fn(v):
                        v.tensor_copy(out=mq_sb[0:4, 0:128],
                                      in_=psA_bf[0:4, 512:640]).then_inc(self.H("sEV"), 1)
                    ADD("v", w, fn)
                    cnt["sEV"] += 1
                    post[("mqc", h, b)] = cnt["sEV"]
                elif nm == "rq":
                    w = [("sD", post[("dq", h, b)]), ("sEV", post[("mqc", h, b)])]
                    if ("rdma",) + pb in post:
                        w.append(("sD", post[("rdma",) + pb]))
                    vo = cnt["sVo"] + 1

                    def fn(v, h=h, vo=vo):
                        i1 = v.scalar_tensor_tensor(out=rq_sb[0:4, 0:128],
                                                    in0=mq_sb[0:4, 0:128],
                                                    scalar=self.crep[0:4, h:h + 1],
                                                    in1=dq_sb[0:4, 0:128],
                                                    op0=ALU.mult, op1=ALU.add)
                        i1.then_inc(self.H("sVo"), 1)
                        v.wait_ge(self.H("sVo"), vo)
                        v.reciprocal(out=rq_sb[0:4, 0:128],
                                     in_=rq_sb[0:4, 0:128]).then_inc(self.H("sEV"), 1)
                    ADD("v", w, fn)
                    cnt["sVo"] += 1
                    post[("rqstt", h, b)] = cnt["sVo"]
                    cnt["sEV"] += 1
                    post[("recip", h, b)] = cnt["sEV"]
                elif nm == "rdma":
                    w = [("sEV", post[("recip", h, b)])]
                    if ("ysTT", pb[0], pb[1], 1) in post:
                        w.append(("sTM", post[("ysTT", pb[0], pb[1], 1)]))
                    pv1 = cnt["sD"] + 16
                    pv2 = cnt["sD"] + 80

                    def fn(s, pv1=pv1, pv2=pv2):
                        s.dma_start(out=r_d.ap(), in_=rq_sb[0:4, 0:128]
                                    ).then_inc(self.H("sD"), 16)
                        s.wait_ge(self.H("sD"), pv1)
                        for q in range(4):
                            s.dma_start(out=rbc[0:64, q * 128:(q + 1) * 128],
                                        in_=r_d.ap()[q:q + 1, :].broadcast_to((64, 128))
                                        ).then_inc(self.H("sD"), 16)
                        s.wait_ge(self.H("sD"), pv2)
                    ADD("y", w, fn)
                    cnt["sD"] += 80
                    post[("rdma", h, b)] = cnt["sD"]
                elif nm in ("ys0", "ys1"):
                    hf = int(nm[-1])
                    w = [("sD", post[("rdma", h, b)]),
                         ("sAV", post[("av", geidx[(h, b)])])]
                    if h > 0 and ("spill", h - 1, b) in post:
                        w.append(("sD", post[("spill", h - 1, b)]))
                    vo = cnt["sVo"] + 1

                    def fn(v, b=b, hf=hf, vo=vo):
                        cc = b * 512 + hf * 256
                        i1 = v.tensor_copy(out=ystage[0:64, cc:cc + 256],
                                           in_=ps_y[0:64, cc:cc + 256])
                        i1.then_inc(self.H("sVo"), 1)
                        v.wait_ge(self.H("sVo"), vo)
                        v.tensor_tensor(out=ystage[0:64, cc:cc + 256],
                                        in0=ystage[0:64, cc:cc + 256],
                                        in1=rbc[0:64, hf * 256:(hf + 1) * 256],
                                        op=ALU.mult).then_inc(self.H("sTM"), 1)
                    ADD("v", w, fn)
                    cnt["sVo"] += 1
                    cnt["sTM"] += 1
                    post[("ysTT", h, b, hf)] = cnt["sTM"]
                elif nm == "spill":
                    w = [("sTM", post[("ysTT", h, b, 1)])]
                    pv = cnt["sD"] + 16

                    def fn(s, h=h, b=b, pv=pv):
                        s.dma_start(out=y_spill.ap()[h * 64:(h + 1) * 64, b * 512:(b + 1) * 512],
                                    in_=ystage[0:64, b * 512:(b + 1) * 512]
                                    ).then_inc(self.H("sD"), 16)
                        s.wait_ge(self.H("sD"), pv)
                    ADD("y", w, fn)
                    cnt["sD"] += 16
                    post[("spill", h, b)] = cnt["sD"]

            if g < NPIE:
                h, b, kt, offp, nn = pieces[g]
                par, j = (h // 4) % 2, h % 4
                slot = g % 2
                diag = (offp == kt * 128)
                w = [("sMM", post[("mm", g)])]
                if g >= 2:
                    w += [("sTM", post[("tm", g - 2)]), ("sAV", post[("av", g - 2)])]

                def ex_fn(sc, nn=nn, slot=slot):
                    sc.activation(out=E_sb[:, slot * 512: slot * 512 + nn],
                                  in_=ps_s[slot].ap()[0:128, 0:nn], func=AF.Exp,
                                  bias=0.0, scale=SCALE_ATT).then_inc(self.H("sEX"), 1)
                ADD("c", w, ex_fn)
                cnt["sEX"] += 1
                post[("exp", g)] = cnt["sEX"]
                w = [("sEX", post[("exp", g)])]
                if kt == 0:
                    if h > 0:
                        w.append(("sMX", post.get(("mT", h - 1, b))))
                        w.append(("sTM", post[("tm", geidx[(h - 1, b)])]))
                else:
                    w.append(("sTM", cnt["sTM"]))

                def tm_fn(v, nn=nn, offp=offp, slot=slot, kt=kt):
                    if kt == 0:
                        i = v.tensor_copy(out=runmax[:, offp:offp + nn],
                                          in_=E_sb[:, slot * 512: slot * 512 + nn])
                    else:
                        i = v.tensor_max(out=runmax[:, offp:offp + nn],
                                         in0=runmax[:, offp:offp + nn],
                                         in1=E_sb[:, slot * 512: slot * 512 + nn])
                    i.then_inc(self.H("sTM"), 1)
                ADD("v", w, tm_fn)
                cnt["sTM"] += 1
                post[("tm", g)] = cnt["sTM"]
            for t in tiles_at.get(g, []):
                kind, gg = t[0], t[1]
                par = gg % 2
                if kind == "rope":
                    tc = t[2]
                    w = [("sD", post[("rpk", gg - 1)]), ("sEV", cnt["sEV"])]
                    vo = cnt["sVo"]

                    def fn(v, tc=tc, vo=vo):
                        cs = cos_sb[0:64, tc * 512:(tc + 1) * 512]
                        sn = sin_sb[0:64, tc * 512:(tc + 1) * 512]
                        fh, sh = rtmp[0:64, 0:512], rtmp[0:64, 512:1024]
                        i1 = v.tensor_tensor(out=rtmp[0:64, 1024:1536], in0=sh, in1=sn, op=ALU.mult)
                        i2 = v.tensor_tensor(out=rtmp[0:64, 1536:2048], in0=fh, in1=sn, op=ALU.mult)
                        i1.then_inc(self.H("sVo"), 1)
                        i2.then_inc(self.H("sVo"), 1)
                        v.wait_ge(self.H("sVo"), vo + 2)
                        i3 = v.tensor_tensor(out=fh, in0=fh, in1=cs, op=ALU.mult)
                        i4 = v.tensor_tensor(out=sh, in0=sh, in1=cs, op=ALU.mult)
                        i3.then_inc(self.H("sVo"), 1)
                        i4.then_inc(self.H("sVo"), 1)
                        v.wait_ge(self.H("sVo"), vo + 4)
                        v.tensor_sub(out=hp_q[0:64, tc * 512: tc * 512 + 512],
                                     in0=fh, in1=rtmp[0:64, 1024:1536]
                                     ).then_inc(self.H("sVo"), 1)
                        v.tensor_add(out=hp_q[0:64, T + tc * 512: T + tc * 512 + 512],
                                     in0=sh, in1=rtmp[0:64, 1536:2048]
                                     ).then_inc(self.H("sEV"), 1)
                    ADD("v", w, fn)
                    cnt["sVo"] += 5
                    cnt["sEV"] += 1
                    post[("rope", gg, tc)] = cnt["sEV"]
                    continue
                k = ptile_k[gg]
                ptile_k[gg] += 1
                w = [("sG", post[("loads", gg)])]
                if k >= 1:
                    w.append(("sEV", post[("pev", gg, k - 1)]))
                elif gg >= 2:
                    w.append(("sEV", post[("pev", gg - 1, grp_ntiles[gg - 1] - 1 - n512)]))
                if kind == "qc":
                    _, _, j, tc, hf = t

                    def mmfn(t_, par=par, j=j, tc=tc, hf=hf):
                        li = None
                        for ci in range(8):
                            li = t_.matmul(psB.ap()[0:64, 0:256],
                                           wq_gs[par][:, ci * 256 + j * 64: ci * 256 + (j + 1) * 64],
                                           xn1_fm[:, ci * T + tc * 512 + hf * 256:
                                                  ci * T + tc * 512 + hf * 256 + 256],
                                           start=(ci == 0), stop=(ci == 7))
                        li.then_inc(self.H("sMX"), 1)

                    def evfn(v, par=par, j=j, tc=tc, hf=hf):
                        v.tensor_copy(out=combQs[par][0:64, j * T + tc * 512 + hf * 256:
                                                      j * T + tc * 512 + hf * 256 + 256],
                                      in_=psB.ap()[0:64, 0:256]).then_inc(self.H("sEV"), 1)
                elif kind == "kc":
                    _, _, j, tc, hf = t

                    def mmfn(t_, par=par, j=j, tc=tc, hf=hf):
                        li = None
                        for c in range(2):
                            li = t_.matmul(psB.ap()[0:64, 0:256],
                                           wku_gs[par][:, c * 256 + j * 64: c * 256 + (j + 1) * 64],
                                           latent[:, c * T + tc * 512 + hf * 256:
                                                  c * T + tc * 512 + hf * 256 + 256],
                                           start=(c == 0), stop=(c == 1))
                        li.then_inc(self.H("sMX"), 1)

                    def evfn(v, par=par, j=j, tc=tc, hf=hf):
                        v.tensor_copy(out=combKs[par][0:64, j * T + tc * 512 + hf * 256:
                                                      j * T + tc * 512 + hf * 256 + 256],
                                      in_=psB.ap()[0:64, 0:256]).then_inc(self.H("sEV"), 1)
                elif kind == "qr":
                    _, _, tc, part, hf = t

                    def mmfn(t_, par=par, tc=tc, part=part, hf=hf):
                        li = None
                        for ci in range(8):
                            li = t_.matmul(psB.ap()[0:64, 0:256],
                                           wqr_gs[par][:, ci * 128 + part * 64: ci * 128 + part * 64 + 64],
                                           xn1_fm[:, ci * T + tc * 512 + hf * 256:
                                                  ci * T + tc * 512 + hf * 256 + 256],
                                           start=(ci == 0), stop=(ci == 7))
                        li.then_inc(self.H("sMX"), 1)

                    def evfn(v, tc=tc, part=part, hf=hf):
                        v.tensor_copy(out=rtmp[0:64, part * 512 + hf * 256:
                                              part * 512 + hf * 256 + 256],
                                      in_=psB.ap()[0:64, 0:256]).then_inc(self.H("sEV"), 1)
                else:
                    _, _, tt = t

                    def mmfn(t_, par=par, gg=gg, tt=tt):
                        li = None
                        for c in range(2):
                            li = t_.matmul(psB.ap()[0:128, 0:256],
                                           latent[:, c * T + tt * 128: c * T + (tt + 1) * 128],
                                           wvu_sb[:, c * 1024 + gg * 256: c * 1024 + (gg + 1) * 256],
                                           start=(c == 0), stop=(c == 1))
                        li.then_inc(self.H("sMX"), 1)

                    def evfn(v, par=par, tt=tt):
                        v.tensor_copy(
                            out=vaugs[par][:, tt * 260:(tt + 1) * 260]
                            .rearrange("p (h c) -> p h c", c=65)[:, :, 0:64],
                            in_=psB.ap()[0:128, 0:256].rearrange("p (h c) -> p h c", c=64)
                        ).then_inc(self.H("sEV"), 1)
                ADD("t", w, mmfn)
                cnt["sMX"] += 1
                post[("pmm", gg, k)] = cnt["sMX"]
                post[("pmmlast", gg)] = ("sMX", cnt["sMX"])
                ADD("v", [("sMX", post[("pmm", gg, k)]), ("sEV", cnt["sEV"]),
                          ("sVo", cnt["sVo"])], evfn)
                cnt["sEV"] += 1
                post[("pev", gg, k)] = cnt["sEV"]

        with nc.Block() as blk:
            @blk.tensor
            def _(t_):
                for w, fn in EVs["t"]:
                    for s_, v_ in w:
                        t_.wait_ge(self.H(s_), v_)
                    fn(t_)
            @blk.vector
            def _(v):
                for w, fn in EVs["v"]:
                    for s_, v_ in w:
                        v.wait_ge(self.H(s_), v_)
                    fn(v)
            @blk.scalar
            def _(sc):
                for w, fn in EVs["c"]:
                    for s_, v_ in w:
                        sc.wait_ge(self.H(s_), v_)
                    fn(sc)
            @blk.sync
            def _(s):
                for w, fn in EVs["y"]:
                    for s_, v_ in w:
                        s.wait_ge(self.H(s_), v_)
                    fn(s)
            @blk.gpsimd
            def _(eng):
                for w, fn in EVs["g"]:
                    for s_, v_ in w:
                        eng.wait_ge(self.H(s_), v_)
                    fn(eng)

        for s_ in cnt:
            self.c[s_][1] = cnt[s_]

        php.close()
        att_r.close()
        s_fm.close()
        att.close()

        s_heads = ExitStack()
        wo_sb = self.sb(s_heads, "wo_sb", [128, 8 * 1024], BF16)
        s_wo = ExitStack()
        r_tm2 = self.sb(s_wo, "r_tm2", [128, NT * E], F32, side="right")
        attn_tm = self.sb(s_wo, "attn_tm", [128, NT * E], BF16, side="right")
        s_afm = ExitStack()
        attn_fm = self.sb(s_afm, "attn_fm", [128, 8 * T], BF16, side="right")
        s_yfm = ExitStack()
        y_fm = self.sb(s_yfm, "y_fm", [128, 8 * T], BF16, side="right")

        with nc.Block() as blk:
            @blk.sync
            def _(s):
                self.mark(s.dma_start(out=r_tm2[:].rearrange("p (t e) -> p t e", e=E),
                                      in_=r_spill.ap().rearrange("(t p) e -> p t e", p=128)),
                          "sD", 16)
                self.mark(s.dma_start(out=y_fm[:].rearrange("p (c t) -> p c t", c=8),
                                      in_=y_spill.ap().rearrange("(c p) t -> p c t", p=128)),
                          "sD", 16)
            @blk.gpsimd
            def _(g):
                self.mark(g.dma_start(out=wo_sb[:].rearrange("p (c f) -> p c f", c=8),
                                      in_=wo_d.ap().rearrange("(c p) f -> p c f", p=128)), "sG", 16)
            @blk.tensor
            def _(t_):
                t_.wait_ge(self.H("sG"), self.V("sG"))
                t_.wait_ge(self.H("sD"), self.V("sD"))

        pwo = ExitStack()
        wps_a = self.psum(pwo, "wps_a", [128, 512], F32)
        wps_b = self.psum(pwo, "wps_b", [128, 512], F32)

        def wo_mm(t_, it, buf):
            oc, tc = it
            return self.acc_mm(t_, buf[0:128, 0:512],
                        [(wo_sb[:, ci * 1024 + oc * 128: ci * 1024 + (oc + 1) * 128],
                          y_fm[:, ci * T + tc * 512: ci * T + tc * 512 + 512])
                         for ci in range(8)])

        def wo_ev(v, it, buf):
            oc, tc = it
            return v.tensor_copy(out=attn_fm[:, oc * T + tc * 512: oc * T + tc * 512 + 512],
                          in_=buf[0:128, 0:512])

        self.mm_evac([(oc, tc) for oc in range(8) for tc in range(n512)], wo_mm, wo_ev,
                     [wps_a, wps_b])
        pwo.close()
        s_yfm.close()
        s_heads.close()

        ptw = ExitStack()
        pa2 = self.psum(ptw, "wpsA", [128, 1024], BF16)
        pb2 = self.psum(ptw, "wpsB", [128, 1024], BF16)

        def fm2tm_mm(t_, tt, buf):
            for fc in range(8):
                li = t_.transpose(buf[:, fc * 128:(fc + 1) * 128],
                                  attn_fm[:, fc * T + tt * 128: fc * T + (tt + 1) * 128],
                                  self.idb[:])
            return li

        def fm2tm_ev(v, tt, buf):
            return v.tensor_copy(out=attn_tm[:, tt * E:(tt + 1) * E], in_=buf[:, 0:1024])

        self.mm_evac(list(range(NT)), fm2tm_mm, fm2tm_ev, [pa2, pb2])
        ptw.close()
        s_afm.close()

        self.ln_stats(lambda t: attn_tm[:, t * E:(t + 1) * E], NT)
        self.ln_finish(NT)
        with nc.Block() as blk:
            @blk.vector
            def _(v):
                v.wait_ge(self.H("sD"), self.V("sD"))
                for t in range(NT):
                    i = v.scalar_tensor_tensor(out=r_tm2[:, t * E:(t + 1) * E],
                                               in0=attn_tm[:, t * E:(t + 1) * E],
                                               scalar=self.lnab[:, 2 * t:2 * t + 1],
                                               in1=r_tm2[:, t * E:(t + 1) * E],
                                               op0=ALU.mult, op1=ALU.add)
                    self.edge(v, i)
                    i = v.tensor_scalar(out=r_tm2[:, t * E:(t + 1) * E],
                                        in0=r_tm2[:, t * E:(t + 1) * E],
                                        scalar1=self.lnab[:, 2 * t + 1:2 * t + 2],
                                        scalar2=None, op0=ALU.add)
                    self.edge(v, i)

        self.ln_stats(lambda t: r_tm2[:, t * E:(t + 1) * E], NT)
        self.ln_finish(NT)
        s_xn3 = ExitStack()
        xn3 = self.sb(s_xn3, "xn3", [128, NT * E], F32)
        ybuf = self.sb(s_xn3, "ybuf", [128, E], F32)
        mbuf = self.sb(s_xn3, "mbuf", [128, E], F32)
        with nc.Block() as blk:
            @blk.scalar
            def _(sc):
                for t in range(NT):
                    sc.activation(out=xn3[:, t * E:(t + 1) * E],
                                  in_=r_tm2[:, t * E:(t + 1) * E], func=AF.Identity,
                                  scale=self.lnab[:, 2 * t:2 * t + 1],
                                  bias=self.lnab[:, 2 * t + 1:2 * t + 2])
        with nc.Block() as blk:
            @blk.sync
            def _(s):
                self.mark(s.dma_start(out=r_spill.ap().rearrange("(t p) e -> p t e", p=128),
                                      in_=r_tm2[:].rearrange("p (t e) -> p t e", e=E)), "sD", 16)
                s.wait_ge(self.H("sD"), self.V("sD"))
        with nc.Block() as blk:
            @blk.vector
            def _(v):
                for t in range(NT):
                    i = v.tensor_reduce(out=self.pstage[:, t:t + 1],
                                        in_=xn3[:, t * E:(t + 1) * E],
                                        axis=AX.X, op=ALU.max, apply_absolute_value=True)
                self.edge(v, i)
                v.tensor_reduce(out=self.pstage[:, 16:17], in_=self.pstage[:, 0:NT],
                                axis=AX.X, op=ALU.max)
        self.global_absmax(self.pstage[:, 16:17], S_X0)
        self.allreduce_max(1)
        with nc.Block() as blk:
            @blk.vector
            def _(v):
                i = v.tensor_scalar(out=self.srow[0:1, S_S1:S_S1 + 1],
                                    in0=self.srow[0:1, S_X0:S_X0 + 1],
                                    scalar1=INVQ, scalar2=1e-12, op0=ALU.mult, op1=ALU.max)
                self.edge(v, i)
                v.reciprocal(out=self.srow[0:1, S_I1:S_I1 + 1],
                             in_=self.srow[0:1, S_S1:S_S1 + 1])
        self.bcast_scalars([(S_I1, B_I1)])

        s_wo.close()

        s_xqf = ExitStack()
        xq_fm = self.sb(s_xqf, "xq_fm", [128, 8 * T], BF16, side="right")
        s_xqt = ExitStack()
        xq_tm = self.sb(s_xqt, "xq_tm", [128, NT * E], BF16, side="right")
        with nc.Block() as blk:
            @blk.vector
            def _(v):
                v.wait_ge(self.H("sG"), self.V("sG"))
                for t in range(NT):
                    i = self.emit_round(v, xn3[:, t * E:(t + 1) * E], self.scal[:, B_I1:B_I1 + 1],
                                        xq_tm[:, t * E:(t + 1) * E], ybuf[:], mbuf[:])
                    self.edge(v, i)
        s_xn3.close()

        ffs = ExitStack()
        spill_stage = self.sb(ffs, "spill_stage", [128, 2 * T], BF16)
        wdqT_down = self.sb(ffs, "wdqT_down", [128, 24 * 1024], BF16)
        s_dq = ExitStack()
        dqst = self.sb(s_dq, "dqst", [128, 2 * I3 + 2 * (I3 // 32) + 2 * I3], F32)
        o_nat, o_s, o_inv, o_t, o_m = (0, 2 * I3, 2 * I3 + I3 // 32,
                                       2 * I3 + 2 * (I3 // 32),
                                       3 * I3 + 2 * (I3 // 32))
        wdq_nat = self.sb(s_dq, "wdq_nat", [128, 2 * I3], BF16)

        ptx = ExitStack()
        pa3 = self.psum(ptx, "fpsA", [128, 1024], BF16)
        pb3 = self.psum(ptx, "fpsB", [128, 1024], BF16)
        self.transpose_tm_to_fm(xq_tm, xq_fm, pa3, pb3)
        ptx.close()
        s_xqt.close()

        def dequant_weight(w_d, wdqT, O, nI, tag=[0]):
            nblk = nI // 32
            noc = O // 128
            nfc = nI // 128
            fgroups = [list(range(gg, min(gg + 8, nfc))) for gg in range(0, nfc, 8)]
            ng = len(fgroups)
            tag[0] += 1
            pdq = ExitStack()
            pqa = self.psum(pdq, "dqA%d" % tag[0], [128, 1024], BF16)
            pqb = self.psum(pdq, "dqB%d" % tag[0], [128, 1024], BF16)
            pqs = [pqa, pqb]
            with nc.Block() as blk:
                d0 = self.V("sD")
                dq0 = self.V("sDQ")
                mm0 = self.V("sMM")
                ev0 = self.V("sEV")

                @blk.sync
                def _(s):
                    for oc in range(noc):
                        if oc >= 2:
                            s.wait_ge(self.H("sDQ"), dq0 + oc - 1)
                        self.mark(s.dma_start(out=dqst[:, o_nat + (oc % 2) * nI:
                                                       o_nat + (oc % 2) * nI + nI],
                                              in_=w_d.ap()[oc * 128:(oc + 1) * 128, :]),
                                  "sD", 16)
                        s.wait_ge(self.H("sD"), self.V("sD"))

                @blk.tensor
                def _(t_):
                    for oc in range(noc):
                        t_.wait_ge(self.H("sDQ"), dq0 + oc + 1)
                        for g, fcg in enumerate(fgroups):
                            k = oc * ng + g
                            if k >= 2:
                                t_.wait_ge(self.H("sEV"), ev0 + k - 1)
                            buf = pqs[k % 2]
                            for jj, fc in enumerate(fcg):
                                li = t_.transpose(
                                    buf[:, jj * 128:(jj + 1) * 128],
                                    wdq_nat[:, (oc % 2) * nI + fc * 128:
                                            (oc % 2) * nI + (fc + 1) * 128],
                                    self.idb[:])
                            li.then_inc(self.H("sMM"), 1)
                self.inc("sMM", noc * ng)

                @blk.vector
                def _(v):
                    for oc in range(noc):
                        v.wait_ge(self.H("sD"), d0 + 16 * (oc + 1))
                        if oc >= 1:
                            v.wait_ge(self.H("sDQ"), dq0 + oc)
                        if oc >= 2:
                            v.wait_ge(self.H("sMM"), mm0 + (oc - 1) * ng)
                        nat = dqst[:, o_nat + (oc % 2) * nI: o_nat + (oc % 2) * nI + nI]
                        nat3 = nat.rearrange("p (b i) -> p b i", i=32)
                        i = v.tensor_reduce(out=dqst[:, o_s:o_s + nblk].unsqueeze(2),
                                            in_=nat3, axis=AX.X, op=ALU.max,
                                            apply_absolute_value=True)
                        self.edge(v, i)
                        i = v.tensor_scalar(out=dqst[:, o_s:o_s + nblk],
                                            in0=dqst[:, o_s:o_s + nblk],
                                            scalar1=1.0 / 7.0, scalar2=1e-12,
                                            op0=ALU.mult, op1=ALU.max)
                        self.edge(v, i)
                        i = v.reciprocal(out=dqst[:, o_inv:o_inv + nblk],
                                         in_=dqst[:, o_s:o_s + nblk])
                        self.edge(v, i)
                        inv_bc = dqst[:, o_inv:o_inv + nblk].unsqueeze(2).broadcast_to((128, nblk, 32))
                        s_bc = dqst[:, o_s:o_s + nblk].unsqueeze(2).broadcast_to((128, nblk, 32))
                        tb = dqst[:, o_t:o_t + nI]
                        mb = dqst[:, o_m:o_m + nI]
                        i = v.tensor_tensor(out=tb.rearrange("p (b i) -> p b i", i=32),
                                            in0=nat3, in1=inv_bc, op=ALU.mult)
                        self.edge(v, i)
                        i = v.tensor_scalar(out=mb, in0=tb, scalar1=MAGIC, scalar2=MAGIC,
                                            op0=ALU.add, op1=ALU.subtract)
                        self.edge(v, i)
                        i = v.tensor_tensor(out=wdq_nat[:, (oc % 2) * nI:
                                                        (oc % 2) * nI + nI].rearrange(
                                                "p (b i) -> p b i", i=32),
                                            in0=mb.rearrange("p (b i) -> p b i", i=32),
                                            in1=s_bc, op=ALU.mult)
                        i.then_inc(self.H("sDQ"), 1)
                        self.inc("sDQ")
                        if oc > 0:
                            for g in range(ng):
                                k = (oc - 1) * ng + g
                                v.wait_ge(self.H("sMM"), mm0 + k + 1)
                                n = len(fgroups[g]) * 128
                                base = (oc - 1) * 128
                                li = v.tensor_copy(
                                    out=wdqT[:].rearrange("p (f o) -> p f o", o=O)[
                                        :, fgroups[g][0]:fgroups[g][0] + len(fgroups[g]),
                                        base:base + 128],
                                    in_=pqs[k % 2][:, 0:n].rearrange("p (f o) -> p f o", o=128))
                                li.then_inc(self.H("sEV"), 1)
                                self.inc("sEV")
                    for g in range(ng):
                        k = (noc - 1) * ng + g
                        v.wait_ge(self.H("sMM"), mm0 + k + 1)
                        n = len(fgroups[g]) * 128
                        base = (noc - 1) * 128
                        li = v.tensor_copy(
                            out=wdqT[:].rearrange("p (f o) -> p f o", o=O)[
                                :, fgroups[g][0]:fgroups[g][0] + len(fgroups[g]),
                                base:base + 128],
                            in_=pqs[k % 2][:, 0:n].rearrange("p (f o) -> p f o", o=128))
                        li.then_inc(self.H("sEV"), 1)
                        self.inc("sEV")
            pdq.close()

        def big_mms(wdqT, spill_d, srow_dst, tag=[0]):
            noc = I3 // 128
            tag[0] += 1
            pps = ExitStack()
            fa = self.psum(pps, "ffa%d" % tag[0], [128, 512], F32)
            fb = self.psum(pps, "ffb%d" % tag[0], [128, 512], F32)
            items = [(oc, tc) for oc in range(noc) for tc in range(n512)]
            with nc.Block() as blk:
                mm0 = self.V("sMM")
                ev0 = self.V("sEV")
                tm0 = self.V("sTM")
                dg0 = self.V("sG")

                @blk.tensor
                def _(t_):
                    for k, (oc, tc) in enumerate(items):
                        if k >= 2:
                            t_.wait_ge(self.H("sEV"), ev0 + k - 1)
                        buf = (fa, fb)[k % 2]
                        li = self.acc_mm(t_, buf[0:128, 0:512],
                                         [(wdqT[:, fc * I3 + oc * 128: fc * I3 + (oc + 1) * 128],
                                           xq_fm[:, fc * T + tc * 512: fc * T + tc * 512 + 512])
                                          for fc in range(8)])
                        li.then_inc(self.H("sMM"), 1)
                self.inc("sMM", len(items))

                @blk.scalar
                def _(sc):
                    for k, (oc, tc) in enumerate(items):
                        sc.wait_ge(self.H("sMM"), mm0 + k + 1)
                        if oc >= 2 and tc == 0:
                            sc.wait_ge(self.H("sG"), dg0 + 16 * (oc - 1))
                            sc.wait_ge(self.H("sTM"), tm0 + oc - 1)
                        sc.activation(out=spill_stage[:, (oc % 2) * T + tc * 512:
                                                      (oc % 2) * T + tc * 512 + 512],
                                      in_=(fa, fb)[k % 2][0:128, 0:512], func=AF.Identity,
                                      bias=0.0, scale=1.0).then_inc(self.H("sEV"), 1)
                        self.inc("sEV")

                @blk.vector
                def _(v):
                    for oc in range(noc):
                        v.wait_ge(self.H("sEV"), ev0 + (oc + 1) * n512)
                        v.tensor_reduce(out=self.pstage[:, oc:oc + 1],
                                        in_=spill_stage[:, (oc % 2) * T:(oc % 2) * T + T],
                                        axis=AX.X, op=ALU.max,
                                        apply_absolute_value=True).then_inc(self.H("sTM"), 1)
                self.inc("sTM", noc)

                @blk.gpsimd
                def _(g):
                    for oc in range(noc):
                        g.wait_ge(self.H("sTM"), tm0 + oc + 1)
                        self.mark(g.dma_start(out=spill_d.ap()[oc * 128:(oc + 1) * 128, :],
                                              in_=spill_stage[:, (oc % 2) * T:(oc % 2) * T + T]),
                                  "sG", 16)
                        g.wait_ge(self.H("sG"), self.V("sG"))
            pps.close()
            with nc.Block() as blk:
                @blk.vector
                def _(v):
                    v.tensor_reduce(out=self.pstage[:, 24:25], in_=self.pstage[:, 0:noc],
                                    axis=AX.X, op=ALU.max)
            self.global_absmax(self.pstage[:, 24:25], srow_dst)

        s_wu = ExitStack()
        wdqT_up = self.sb(s_wu, "wdqT_up", [128, 8 * I3], BF16)
        dequant_weight(wup_d, wdqT_up, I3, E)
        big_mms(wdqT_up, u_spill, S_LU)
        s_wu.close()
        s_wg = ExitStack()
        wdqT_g = self.sb(s_wg, "wdqT_g", [128, 8 * I3], BF16)
        dequant_weight(wgate_d, wdqT_g, I3, E)
        big_mms(wdqT_g, g_spill, S_LG)
        s_wg.close()
        s_xqf.close()

        dequant_weight(wdown_d, wdqT_down, 1024, I3)
        s_dq.close()

        with nc.Block() as blk:
            @blk.vector
            def _(v):
                i = v.tensor_copy(out=self.srow[0:1, S_X0:S_X0 + 1],
                                  in_=self.srow[0:1, S_LU:S_LU + 1])
                i = v.tensor_copy(out=self.srow[0:1, S_X1:S_X1 + 1],
                                  in_=self.srow[0:1, S_LG:S_LG + 1])
        self.allreduce_max(2)
        with nc.Block() as blk:
            @blk.vector
            def _(v):
                i = v.tensor_scalar(out=self.srow[0:1, S_S2:S_S2 + 1],
                                    in0=self.srow[0:1, S_X0:S_X0 + 1],
                                    scalar1=self.srow[0:1, S_S1:S_S1 + 1], scalar2=None,
                                    op0=ALU.mult)
                self.edge(v, i)
                i = v.tensor_scalar(out=self.srow[0:1, S_S2:S_S2 + 1],
                                    in0=self.srow[0:1, S_S2:S_S2 + 1],
                                    scalar1=INVQ, scalar2=1e-12, op0=ALU.mult, op1=ALU.max)
                self.edge(v, i)
                i = v.reciprocal(out=self.srow[0:1, S_C2:S_C2 + 1],
                                 in_=self.srow[0:1, S_S2:S_S2 + 1])
                self.edge(v, i)
                i = v.tensor_scalar(out=self.srow[0:1, S_C2:S_C2 + 1],
                                    in0=self.srow[0:1, S_C2:S_C2 + 1],
                                    scalar1=self.srow[0:1, S_S1:S_S1 + 1], scalar2=None,
                                    op0=ALU.mult)
                self.edge(v, i)
                i = v.tensor_scalar(out=self.srow[0:1, S_S3:S_S3 + 1],
                                    in0=self.srow[0:1, S_X1:S_X1 + 1],
                                    scalar1=self.srow[0:1, S_S1:S_S1 + 1], scalar2=None,
                                    op0=ALU.mult)
                self.edge(v, i)
                i = v.tensor_scalar(out=self.srow[0:1, S_S3:S_S3 + 1],
                                    in0=self.srow[0:1, S_S3:S_S3 + 1],
                                    scalar1=INVQ, scalar2=1e-12, op0=ALU.mult, op1=ALU.max)
                self.edge(v, i)
                i = v.reciprocal(out=self.srow[0:1, S_C3:S_C3 + 1],
                                 in_=self.srow[0:1, S_S3:S_S3 + 1])
                self.edge(v, i)
                i = v.tensor_scalar(out=self.srow[0:1, S_C3:S_C3 + 1],
                                    in0=self.srow[0:1, S_C3:S_C3 + 1],
                                    scalar1=self.srow[0:1, S_S1:S_S1 + 1], scalar2=None,
                                    op0=ALU.mult)
        self.bcast_scalars([(S_C2, B_C2), (S_C3, B_C3), (S_S3, B_S3)])

        s_gs = ExitStack()
        gsb = self.sb(s_gs, "gsb", [128, 24 * T], BF16)
        s_gsr = ExitStack()
        fy = self.sb(s_gsr, "fy", [128, T], F32, side="right")
        fm_ = self.sb(s_gsr, "fm_", [128, T], F32, side="right")
        fsig = self.sb(s_gsr, "fsig", [128, T], F32, side="right")

        NIC = I3 // 128
        MCOL = self.scal[:, 7:8]

        with nc.Block() as blk:
            d0 = self.V("sD")
            so0 = self.V("sSo")
            dq0 = self.V("sDQ")
            ev0 = self.V("sEV")

            @blk.sync
            def _(s):
                for ic in range(NIC):
                    if ic >= 2:
                        s.wait_ge(self.H("sSo"), so0 + 2 * (ic - 2) + 1)
                    self.mark(s.dma_start(out=spill_stage[:, (ic % 2) * T:(ic % 2) * T + T],
                                          in_=g_spill.ap()[ic * 128:(ic + 1) * 128, :]),
                              "sD", 16)
                    s.wait_ge(self.H("sD"), self.V("sD"))

            @blk.scalar
            def _(sc):
                for ic in range(NIC):
                    sc.wait_ge(self.H("sD"), d0 + 16 * (ic + 1))
                    if ic > 0:
                        sc.wait_ge(self.H("sEV"), ev0 + ic)
                    sc.activation(out=fy[:], in_=spill_stage[:, (ic % 2) * T:(ic % 2) * T + T],
                                  func=AF.Identity, scale=self.scal[:, B_C3:B_C3 + 1],
                                  bias=MCOL).then_inc(self.H("sSo"), 1)
                    self.inc("sSo")
                    sc.wait_ge(self.H("sDQ"), dq0 + ic + 1)
                    sc.activation(out=fsig[:], in_=gsb[:, ic * T:(ic + 1) * T],
                                  func=AF.Sigmoid, bias=0.0,
                                  scale=self.scal[:, B_S3:B_S3 + 1]).then_inc(self.H("sSo"), 1)
                    self.inc("sSo")

            @blk.vector
            def _(v):
                for ic in range(NIC):
                    v.wait_ge(self.H("sSo"), so0 + 2 * ic + 1)
                    i = v.tensor_scalar(out=gsb[:, ic * T:(ic + 1) * T], in0=fy[:],
                                        scalar1=MAGIC, scalar2=None, op0=ALU.subtract)
                    i.then_inc(self.H("sDQ"), 1)
                    self.inc("sDQ")
                    v.wait_ge(self.H("sSo"), so0 + 2 * ic + 2)
                    i = v.tensor_scalar(out=fy[:], in0=fsig[:], scalar1=65536.0,
                                        scalar2=65535.0, op0=ALU.mult, op1=ALU.min)
                    self.edge(v, i)
                    i = v.tensor_scalar(out=fm_[:], in0=fy[:], scalar1=MAGIC, scalar2=MAGIC,
                                        op0=ALU.add, op1=ALU.subtract)
                    self.edge(v, i)
                    i = v.tensor_tensor(out=gsb[:, ic * T:(ic + 1) * T],
                                        in0=gsb[:, ic * T:(ic + 1) * T], in1=fm_[:], op=ALU.mult)
                    self.edge(v, i)
                    i = v.tensor_reduce(out=self.pstage[:, ic:ic + 1],
                                        in_=gsb[:, ic * T:(ic + 1) * T],
                                        axis=AX.X, op=ALU.max, apply_absolute_value=True)
                    i.then_inc(self.H("sEV"), 1)
                    self.inc("sEV")
        with nc.Block() as blk:
            @blk.vector
            def _(v):
                v.tensor_reduce(out=self.pstage[:, 24:25], in_=self.pstage[:, 0:24],
                                axis=AX.X, op=ALU.max)
        self.global_absmax(self.pstage[:, 24:25], S_X0)
        self.allreduce_max(1)
        with nc.Block() as blk:
            @blk.vector
            def _(v):
                i = v.tensor_scalar(out=self.srow[0:1, S_S4:S_S4 + 1],
                                    in0=self.srow[0:1, S_X0:S_X0 + 1],
                                    scalar1=self.srow[0:1, S_S3:S_S3 + 1], scalar2=None,
                                    op0=ALU.mult)
                self.edge(v, i)
                i = v.tensor_scalar(out=self.srow[0:1, S_S4:S_S4 + 1],
                                    in0=self.srow[0:1, S_S4:S_S4 + 1],
                                    scalar1=INVQ / 65536.0, scalar2=1e-12,
                                    op0=ALU.mult, op1=ALU.max)
                self.edge(v, i)
                i = v.reciprocal(out=self.srow[0:1, S_C4:S_C4 + 1],
                                 in_=self.srow[0:1, S_S4:S_S4 + 1])
                self.edge(v, i)
                i = v.tensor_scalar(out=self.srow[0:1, S_C4:S_C4 + 1],
                                    in0=self.srow[0:1, S_C4:S_C4 + 1],
                                    scalar1=self.srow[0:1, S_S3:S_S3 + 1], scalar2=1.0 / 65536.0,
                                    op0=ALU.mult, op1=ALU.mult)
        self.bcast_scalars([(S_C4, B_C4)])

        with nc.Block() as blk:
            d0 = self.V("sD")
            so0 = self.V("sSo")
            ev0 = self.V("sEV")

            @blk.sync
            def _(s):
                for ic in range(NIC):
                    if ic >= 2:
                        s.wait_ge(self.H("sSo"), so0 + 2 * (ic - 2) + 2)
                    self.mark(s.dma_start(out=spill_stage[:, (ic % 2) * T:(ic % 2) * T + T],
                                          in_=u_spill.ap()[ic * 128:(ic + 1) * 128, :]),
                              "sD", 16)
                    s.wait_ge(self.H("sD"), self.V("sD"))

            @blk.scalar
            def _(sc):
                for ic in range(NIC):
                    if ic > 0:
                        sc.wait_ge(self.H("sEV"), ev0 + ic)
                    sc.activation(out=fy[:], in_=gsb[:, ic * T:(ic + 1) * T],
                                  func=AF.Identity, scale=self.scal[:, B_C4:B_C4 + 1],
                                  bias=MCOL).then_inc(self.H("sSo"), 1)
                    self.inc("sSo")
                    sc.wait_ge(self.H("sD"), d0 + 16 * (ic + 1))
                    sc.activation(out=fsig[:], in_=spill_stage[:, (ic % 2) * T:(ic % 2) * T + T],
                                  func=AF.Identity, scale=self.scal[:, B_C2:B_C2 + 1],
                                  bias=MCOL).then_inc(self.H("sSo"), 1)
                    self.inc("sSo")

            @blk.vector
            def _(v):
                for ic in range(NIC):
                    v.wait_ge(self.H("sSo"), so0 + 2 * ic + 1)
                    i = v.tensor_scalar(out=gsb[:, ic * T:(ic + 1) * T], in0=fy[:],
                                        scalar1=MAGIC, scalar2=None, op0=ALU.subtract)
                    self.edge(v, i)
                    v.wait_ge(self.H("sSo"), so0 + 2 * ic + 2)
                    i = v.tensor_scalar(out=fm_[:], in0=fsig[:],
                                        scalar1=MAGIC, scalar2=None, op0=ALU.subtract)
                    self.edge(v, i)
                    i = v.tensor_tensor(out=gsb[:, ic * T:(ic + 1) * T],
                                        in0=gsb[:, ic * T:(ic + 1) * T],
                                        in1=fm_[:], op=ALU.mult)
                    self.edge(v, i)
                    i = v.tensor_reduce(out=self.pstage[:, ic:ic + 1],
                                        in_=gsb[:, ic * T:(ic + 1) * T],
                                        axis=AX.X, op=ALU.max, apply_absolute_value=True)
                    i.then_inc(self.H("sEV"), 1)
                    self.inc("sEV")
        with nc.Block() as blk:
            @blk.vector
            def _(v):
                v.tensor_reduce(out=self.pstage[:, 24:25], in_=self.pstage[:, 0:24],
                                axis=AX.X, op=ALU.max)
        self.global_absmax(self.pstage[:, 24:25], S_X0)
        self.allreduce_max(1)
        with nc.Block() as blk:
            @blk.vector
            def _(v):
                i = v.tensor_scalar(out=self.srow[0:1, S_S5:S_S5 + 1],
                                    in0=self.srow[0:1, S_X0:S_X0 + 1],
                                    scalar1=self.srow[0:1, S_S4:S_S4 + 1], scalar2=None,
                                    op0=ALU.mult)
                self.edge(v, i)
                i = v.tensor_scalar(out=self.srow[0:1, S_S5:S_S5 + 1],
                                    in0=self.srow[0:1, S_S5:S_S5 + 1],
                                    scalar1=self.srow[0:1, S_S2:S_S2 + 1], scalar2=None,
                                    op0=ALU.mult)
                self.edge(v, i)
                i = v.tensor_scalar(out=self.srow[0:1, S_S5:S_S5 + 1],
                                    in0=self.srow[0:1, S_S5:S_S5 + 1],
                                    scalar1=INVQ, scalar2=1e-12, op0=ALU.mult, op1=ALU.max)
                self.edge(v, i)
                i = v.reciprocal(out=self.srow[0:1, S_C5:S_C5 + 1],
                                 in_=self.srow[0:1, S_S5:S_S5 + 1])
                self.edge(v, i)
                i = v.tensor_scalar(out=self.srow[0:1, S_C5:S_C5 + 1],
                                    in0=self.srow[0:1, S_C5:S_C5 + 1],
                                    scalar1=self.srow[0:1, S_S4:S_S4 + 1], scalar2=None,
                                    op0=ALU.mult)
                self.edge(v, i)
                i = v.tensor_scalar(out=self.srow[0:1, S_C5:S_C5 + 1],
                                    in0=self.srow[0:1, S_C5:S_C5 + 1],
                                    scalar1=self.srow[0:1, S_S2:S_S2 + 1], scalar2=None,
                                    op0=ALU.mult)
        self.bcast_scalars([(S_C5, B_C5), (S_S5, B_S5)])

        with nc.Block() as blk:
            so0 = self.V("sSo")
            ev0 = self.V("sEV")

            @blk.scalar
            def _(sc):
                sc.wait_ge(self.H("sG"), self.V("sG"))
                for ic in range(NIC):
                    if ic >= 2:
                        sc.wait_ge(self.H("sEV"), ev0 + ic - 1)
                    sc.activation(out=(fy, fsig)[ic % 2][:], in_=gsb[:, ic * T:(ic + 1) * T],
                                  func=AF.Identity, scale=self.scal[:, B_C5:B_C5 + 1],
                                  bias=MCOL).then_inc(self.H("sSo"), 1)
                    self.inc("sSo")

            @blk.vector
            def _(v):
                for ic in range(NIC):
                    v.wait_ge(self.H("sSo"), so0 + ic + 1)
                    i = v.tensor_scalar(out=gsb[:, ic * T:(ic + 1) * T],
                                        in0=(fy, fsig)[ic % 2][:],
                                        scalar1=MAGIC, scalar2=None, op0=ALU.subtract)
                    i.then_inc(self.H("sEV"), 1)
                    self.inc("sEV")

        s_gsr.close()
        s_fout = ExitStack()
        ffn_fm = self.sb(s_fout, "ffn_fm", [128, 8 * T], BF16, side="right")
        pdn = ExitStack()
        da = self.psum(pdn, "dna", [128, 512], F32)
        db = self.psum(pdn, "dnb", [128, 512], F32)

        def dn_mm(t_, it, buf):
            oc, tc = it
            return self.acc_mm(t_, buf[0:128, 0:512],
                        [(wdqT_down[:, ic * 1024 + oc * 128: ic * 1024 + (oc + 1) * 128],
                          gsb[:, ic * T + tc * 512: ic * T + tc * 512 + 512])
                         for ic in range(I3 // 128)])

        def dn_ev(sc, it, buf):
            oc, tc = it
            return sc.activation(out=ffn_fm[:, oc * T + tc * 512: oc * T + tc * 512 + 512],
                          in_=buf[0:128, 0:512], func=AF.Identity, bias=0.0,
                          scale=self.scal[:, B_S5:B_S5 + 1])

        self.mm_evac([(oc, tc) for oc in range(8) for tc in range(n512)], dn_mm, dn_ev,
                     [da, db], evac_eng="scalar")
        pdn.close()
        s_gs.close()
        ffs.close()

        ffn_tm = self.sb(s_fout, "ffn_tm", [128, NT * E], BF16)
        ptf = ExitStack()
        fa2 = self.psum(ptf, "ftA", [128, 1024], BF16)
        fb2 = self.psum(ptf, "ftB", [128, 1024], BF16)

        def f2t_mm(t_, tt, buf):
            for fc in range(8):
                li = t_.transpose(buf[:, fc * 128:(fc + 1) * 128],
                                  ffn_fm[:, fc * T + tt * 128: fc * T + (tt + 1) * 128],
                                  self.idb[:])
            return li

        def f2t_ev(v, tt, buf):
            return v.tensor_copy(out=ffn_tm[:, tt * E:(tt + 1) * E], in_=buf[:, 0:1024])

        self.mm_evac(list(range(NT)), f2t_mm, f2t_ev, [fa2, fb2])
        ptf.close()

        s_r3 = ExitStack()
        r_tm3 = self.sb(s_r3, "r_tm3", [128, NT * E], F32)
        with nc.Block() as blk:
            @blk.sync
            def _(s):
                self.mark(s.dma_start(out=r_tm3[:].rearrange("p (t e) -> p t e", e=E),
                                      in_=r_spill.ap().rearrange("(t p) e -> p t e", p=128)),
                          "sD", 16)
        self.ln_stats(lambda t: ffn_tm[:, t * E:(t + 1) * E], NT)
        self.ln_finish(NT)
        with nc.Block() as blk:
            @blk.vector
            def _(v):
                v.wait_ge(self.H("sD"), self.V("sD"))
                for t in range(NT):
                    i = v.scalar_tensor_tensor(out=r_tm3[:, t * E:(t + 1) * E],
                                               in0=ffn_tm[:, t * E:(t + 1) * E],
                                               scalar=self.lnab[:, 2 * t:2 * t + 1],
                                               in1=r_tm3[:, t * E:(t + 1) * E],
                                               op0=ALU.mult, op1=ALU.add)
                    self.edge(v, i)
                    i = v.tensor_scalar(out=r_tm3[:, t * E:(t + 1) * E],
                                        in0=r_tm3[:, t * E:(t + 1) * E],
                                        scalar1=self.lnab[:, 2 * t + 1:2 * t + 2],
                                        scalar2=None, op0=ALU.add)
                    self.edge(v, i)
        with nc.Block() as blk:
            @blk.sync
            def _(s):
                self.mark(s.dma_start(out=out_d.ap().rearrange("(t p) e -> p t e", p=128),
                                      in_=r_tm3[:].rearrange("p (t e) -> p t e", e=E)), "sD", 16)
                s.wait_ge(self.H("sD"), self.V("sD"))
        s_r3.close()
        s_fout.close()
        self.root.close()
        return self.nc


def _host_consts(T):
    half = DR // 2
    inv = (1.0 / 10000.0 ** (np.arange(half, dtype=np.float64) / half))
    ang = np.arange(T, dtype=np.float64)[None, :] * inv[:, None]
    cos16 = np.cos(ang)
    sin16 = np.sin(ang)
    cosT = np.tile(cos16, (8, 1)).astype(np.float32)
    sinT = np.tile(sin16, (8, 1)).astype(np.float32)
    identb = np.eye(128, dtype=np.float32).astype(ml_dtypes.bfloat16)
    identf = np.eye(128, dtype=np.float32)
    iota = np.arange(128, dtype=np.float32).reshape(128, 1)
    return cosT, sinT, identb, identf, iota


def _cmask():
    m = np.zeros((128, 128), dtype=np.float32)
    for k in range(128):
        m[k, :k] = -1e30
    return m


def _wqr_perm():
    perm = np.zeros(H * DR, dtype=np.int64)
    for hh in range(H):
        for halfsel in range(2):
            for i in range(16):
                tile = halfsel * 2 + hh // 8
                newc = tile * 128 + (hh % 8) * 16 + i
                perm[newc] = hh * 32 + halfsel * 16 + i
    return perm


_BUILD_CACHE = {}


def kernel(L0_attn_x, L0_attn_Wq, L0_attn_Wqr, L0_attn_Wkr, L0_attn_Wkvd,
           L0_attn_Wku, L0_attn_Wvu, L0_attn_Wo, L0_attn_lobo_log,
           L0_ffn_w_gate, L0_ffn_w_up, L0_ffn_w_down):
    T = 2048
    x = np.ascontiguousarray(np.asarray(L0_attn_x, dtype=np.float32))
    B = x.shape[0]
    cosT, sinT, identb, identf, iota = _host_consts(T)
    perm = _wqr_perm()
    wqr_p = np.ascontiguousarray(np.asarray(L0_attn_Wqr, dtype=np.float32)[:, perm])
    common = {
        "wq": np.ascontiguousarray(L0_attn_Wq, dtype=np.float32),
        "wqr": wqr_p,
        "wkr": np.ascontiguousarray(L0_attn_Wkr, dtype=np.float32),
        "wkvd": np.ascontiguousarray(L0_attn_Wkvd, dtype=np.float32),
        "wku": np.ascontiguousarray(L0_attn_Wku, dtype=np.float32),
        "wvu": np.ascontiguousarray(L0_attn_Wvu, dtype=np.float32),
        "wo": np.ascontiguousarray(L0_attn_Wo, dtype=np.float32),
        "lobo": np.ascontiguousarray(L0_attn_lobo_log, dtype=np.float32),
        "wup": np.ascontiguousarray(L0_ffn_w_up, dtype=np.float32),
        "wgate": np.ascontiguousarray(L0_ffn_w_gate, dtype=np.float32),
        "wdown": np.ascontiguousarray(L0_ffn_w_down, dtype=np.float32),
        "identb": identb, "identf": identf,
        "cosT": cosT, "sinT": sinT, "iota128": iota,
        "cmask": _cmask(),
    }
    if T not in _BUILD_CACHE:
        _BUILD_CACHE[T] = KB(T).build()
    nc = _BUILD_CACHE[T]
    in_maps = [dict(common, x=np.ascontiguousarray(x[c])) for c in range(B)]
    res = run_bass_kernel_spmd(nc, in_maps, core_ids=list(range(B)))
    return np.stack([res.results[c]["out"] for c in range(B)], axis=0)


# version 11 (final)
# speedup vs baseline: 1.3263x; 1.1402x over previous
import math
from contextlib import ExitStack
import numpy as np
import ml_dtypes
import concourse.bass as bass
import concourse.mybir as mybir
from concourse.bass_utils import run_bass_kernel_spmd

F32 = mybir.dt.float32
BF16 = mybir.dt.bfloat16
AF = mybir.ActivationFunctionType
ALU = mybir.AluOpType
AX = mybir.AxisListType

NCORES = 8
E = 1024
H = 16
DH = 64
DR = 32
DKV = 256
I3 = 3072
EPS = 1e-5
SCALE_ATT = 1.0 / math.sqrt(DH + DR)
INVQ = 1.0 / 32767.0
MAGIC = 12582912.0

S_X0, S_X1 = 0, 1
S_S1, S_I1 = 2, 3
S_S2, S_C2 = 4, 5
S_S3, S_C3 = 6, 7
S_S4, S_C4 = 8, 9
S_S5, S_C5 = 10, 11
S_LU, S_LG, S_LGS, S_LOR = 12, 13, 14, 15
B_I1, B_C2, B_C3, B_S3, B_C4, B_C5, B_S5 = 0, 1, 2, 3, 4, 5, 6


def cdiv(a, b):
    return (a + b - 1) // b


class KB:
    def __init__(self, T):
        self.T = T
        self.NT = T // 128
        self.nc = bass.Bass("TRN2", target_bir_lowering=False)
        self.c = {}
        self.root = ExitStack()

    def sem(self, n):
        self.c[n] = [self.root.enter_context(self.nc.semaphore(n)), 0]

    def H(self, n):
        return self.c[n][0]

    def V(self, n):
        return self.c[n][1]

    def inc(self, n, k=1):
        self.c[n][1] += k
        return self.c[n][1]

    def mark(self, inst, n, k=1):
        inst.then_inc(self.H(n), k)
        return self.inc(n, k)

    def edge(self, eng, inst, n="sVo"):
        v = self.mark(inst, n)
        eng.wait_ge(self.H(n), v)

    def edges(self, eng, insts, n="sVo"):
        for i in insts:
            v = self.mark(i, n)
        eng.wait_ge(self.H(n), v)

    def sb(self, es, name, shape, dt, side="left"):
        return es.enter_context(self.nc.sbuf_tensor(name, shape, dt, side=side))

    def psum(self, es, name, shape, dt):
        return es.enter_context(self.nc.psum_tensor(name, shape, dt))

    def acc_mm(self, t_, out_ap, pairs):
        n = len(pairs)
        li = None
        for i, (l, r) in enumerate(pairs):
            li = t_.matmul(out_ap, l, r, start=(i == 0), stop=(i == n - 1))
        return li

    def mm_evac(self, items, emit_mms, emit_evac, bufs, evac_eng="vector"):
        nb = len(bufs)
        with self.nc.Block() as blk:
            tb = self.V("sMM")
            vb = self.V("sEV")

            @blk.tensor
            def _(t_):
                for k, it in enumerate(items):
                    if k >= nb:
                        t_.wait_ge(self.H("sEV"), vb + k - nb + 1)
                    li = emit_mms(t_, it, bufs[k % nb])
                    li.then_inc(self.H("sMM"), 1)
            self.inc("sMM", len(items))

            def body(eng):
                for k, it in enumerate(items):
                    eng.wait_ge(self.H("sMM"), tb + k + 1)
                    li = emit_evac(eng, it, bufs[k % nb])
                    li.then_inc(self.H("sEV"), 1)
            (blk.vector if evac_eng == "vector" else blk.scalar)(body)
            self.inc("sEV", len(items))

    def transpose_tm_to_fm(self, src_tm, dst_fm, pa, pb):
        T, NT = self.T, self.NT
        groups = []
        for fc in range(8):
            for half in range(cdiv(NT, 8)):
                groups.append((fc, half, list(range(half * 8, min((half + 1) * 8, NT)))))

        def mms(t_, it, buf):
            fc, half, tts = it
            li = None
            for j, tt in enumerate(tts):
                li = t_.transpose(buf[:, j * 128:(j + 1) * 128],
                                  src_tm[:, tt * E + fc * 128: tt * E + (fc + 1) * 128],
                                  self.idb[:])
            return li

        def ev(v, it, buf):
            fc, half, tts = it
            n = len(tts) * 128
            return v.tensor_copy(out=dst_fm[:, fc * T + half * 1024: fc * T + half * 1024 + n],
                                 in_=buf[:, 0:n])

        self.mm_evac(groups, mms, ev, [pa, pb])

    def ln_stats(self, src_fn, n_tiles):
        with self.nc.Block() as blk:
            @blk.vector
            def _(v):
                for t in range(n_tiles):
                    ap = src_fn(t)
                    v.bn_stats(out=self.lnstat[:, 12 * t:12 * t + 6], in_=ap[:, 0:512])
                    v.bn_stats(out=self.lnstat[:, 12 * t + 6:12 * t + 12], in_=ap[:, 512:1024])

    def ln_finish(self, n_tiles):
        nc = self.nc
        mv2 = self.lnmv[:].rearrange("p (t two) -> p t two", two=2)
        ab2 = self.lnab[:].rearrange("p (t two) -> p t two", two=2)
        with nc.Block() as blk:
            @blk.vector
            def _(v):
                for t in range(n_tiles):
                    i = v.bn_aggr(out=self.lnmv[:, 2 * t:2 * t + 2],
                                  in_=self.lnstat[:, 12 * t:12 * t + 12])
                self.edge(v, i)
                v.tensor_scalar(out=mv2[:, 0:n_tiles, 1:2], in0=mv2[:, 0:n_tiles, 1:2],
                                scalar1=EPS, scalar2=None, op0=ALU.add)
        with nc.Block() as blk:
            @blk.scalar
            def _(sc):
                i1 = sc.activation(out=ab2[:, 0:n_tiles, 0:1], in_=mv2[:, 0:n_tiles, 1:2],
                                   func=AF.Ln, bias=0.0, scale=1.0)
                self.edge(sc, i1, "sSo")
                sc.activation(out=ab2[:, 0:n_tiles, 0:1], in_=ab2[:, 0:n_tiles, 0:1],
                              func=AF.Exp, bias=0.0, scale=-0.5)
        with nc.Block() as blk:
            @blk.vector
            def _(v):
                i1 = v.tensor_tensor(out=ab2[:, 0:n_tiles, 1:2], in0=mv2[:, 0:n_tiles, 0:1],
                                     in1=ab2[:, 0:n_tiles, 0:1], op=ALU.mult)
                self.edge(v, i1)
                v.tensor_scalar(out=ab2[:, 0:n_tiles, 1:2], in0=ab2[:, 0:n_tiles, 1:2],
                                scalar1=-1.0, scalar2=None, op0=ALU.mult)

    def allreduce_max(self, n):
        nc = self.nc
        with nc.Block() as blk:
            @blk.gpsimd
            def _(g):
                self.mark(g.dma_start(out=self.cc_in.ap()[0:1, 0:8],
                                      in_=self.srow[0:1, 0:8]), "sG", 16)
                g.wait_ge(self.H("sG"), self.V("sG"))
                self.mark(g.collective_compute("AllReduce", ALU.max,
                                               replica_groups=[list(range(NCORES))],
                                               ins=[self.cc_in.ap().opt()],
                                               outs=[self.cc_out.ap().opt()]), "sC", 1)
                g.wait_ge(self.H("sC"), self.V("sC"))
                self.mark(g.dma_start(out=self.srow[0:1, 0:n],
                                      in_=self.cc_out.ap()[0:1, 0:n]), "sG", 16)
                g.wait_ge(self.H("sG"), self.V("sG"))

    def bcast_scalars(self, pairs):
        nc = self.nc
        with nc.Block() as blk:
            @blk.gpsimd
            def _(g):
                for i, (sc_, _dc) in enumerate(pairs):
                    self.mark(g.dma_start(out=self.bc_d.ap()[0:1, i:i + 1],
                                          in_=self.srow[0:1, sc_:sc_ + 1]), "sG", 16)
                g.wait_ge(self.H("sG"), self.V("sG"))
                for i, (_sc, dc) in enumerate(pairs):
                    self.mark(g.dma_start(out=self.scal[:, dc:dc + 1],
                                          in_=self.bc_d.ap()[0:1, i:i + 1].broadcast_to((128, 1))),
                              "sG", 16)
                g.wait_ge(self.H("sG"), self.V("sG"))

    def global_absmax(self, pcol_ap, srow_col):
        with self.nc.Block() as blk:
            @blk.gpsimd
            def _(g):
                g.tensor_reduce(out=self.srow[0:1, srow_col:srow_col + 1], in_=pcol_ap,
                                axis=AX.XYZWC, op=ALU.max, apply_absolute_value=True)

    def emit_round(self, v, in_ap, c_ap, out_ap, y_ap, m_ap=None):
        i1 = v.tensor_scalar(out=y_ap, in0=in_ap, scalar1=c_ap, scalar2=MAGIC,
                             op0=ALU.mult, op1=ALU.add)
        self.edge(v, i1)
        return v.tensor_scalar(out=out_ap, in0=y_ap, scalar1=MAGIC, scalar2=None,
                               op0=ALU.subtract)

    def build(self):
        nc, T, NT = self.nc, self.T, self.NT
        n512 = T // 512
        d = nc.dram_tensor
        x_in = d("x", [T, E], F32, kind="ExternalInput")
        wq_d = d("wq", [E, H * DH], F32, kind="ExternalInput")
        wqr_d = d("wqr", [E, H * DR], F32, kind="ExternalInput")
        wkr_d = d("wkr", [E, DR], F32, kind="ExternalInput")
        wkvd_d = d("wkvd", [E, DKV], F32, kind="ExternalInput")
        wku_d = d("wku", [DKV, H * DH], F32, kind="ExternalInput")
        wvu_d = d("wvu", [DKV, H * DH], F32, kind="ExternalInput")
        wo_d = d("wo", [E, E], F32, kind="ExternalInput")
        lobo_d = d("lobo", [H], F32, kind="ExternalInput")
        wup_d = d("wup", [I3, E], F32, kind="ExternalInput")
        wgate_d = d("wgate", [I3, E], F32, kind="ExternalInput")
        wdown_d = d("wdown", [E, I3], F32, kind="ExternalInput")
        identb_d = d("identb", [128, 128], BF16, kind="ExternalInput")
        identf_d = d("identf", [128, 128], F32, kind="ExternalInput")
        cosT_d = d("cosT", [128, T], F32, kind="ExternalInput")
        sinT_d = d("sinT", [128, T], F32, kind="ExternalInput")
        iota_d = d("iota128", [128, 1], F32, kind="ExternalInput")
        cmask_d = d("cmask", [128, 128], F32, kind="ExternalInput")
        out_d = d("out", [T, E], F32, kind="ExternalOutput")
        r_spill = d("r_spill", [T, E], F32)
        r_d = d("r_bounce", [4, 128], F32)
        u_spill = d("u_spill", [I3, T], BF16)
        y_spill = d("y_spill", [H * 64, T], BF16)
        g_spill = d("g_spill", [I3, T], BF16)
        self.cc_in = d("cc_in", [1, 8], F32)
        self.cc_out = d("cc_out", [1, 8], F32)
        self.bc_d = d("bc_d", [1, 16], F32)

        for s in ["sD", "sG", "sC", "sMM", "sEV", "sEX", "sMK", "sTM", "sAV",
                  "sFZ", "sVo", "sSo", "sDQ", "sMX", "sDR"]:
            self.sem(s)

        rs = self.root
        self.idb = self.sb(rs, "idb", [128, 128], BF16)
        self.idf = self.sb(rs, "idf", [128, 128], F32)
        self.iota_sb = self.sb(rs, "iota", [128, 1], F32)
        self.c128_sb = self.sb(rs, "c128", [128, 1], F32)
        self.ones_b = self.sb(rs, "onesb", [128, 128], BF16)
        self.ones_f = self.sb(rs, "onesf", [128, 64], F32)
        self.lnstat = self.sb(rs, "lnstat", [128, 192], F32)
        self.lnmv = self.sb(rs, "lnmv", [128, 32], F32)
        self.lnab = self.sb(rs, "lnab", [128, 32], F32)
        self.scal = self.sb(rs, "scal", [128, 8], F32)
        self.srow = self.sb(rs, "srow", [1, 16], F32)
        self.crow_b = self.sb(rs, "crow_b", [1, 16], BF16)
        self.crep = self.sb(rs, "crep", [128, 16], F32)
        self.junk = self.sb(rs, "junk", [128, 1], F32)
        self.pstage = self.sb(rs, "pstage", [128, 32], F32)
        self.mstage = self.sb(rs, "mstage", [128, 16], BF16)
        self.cmask = self.sb(rs, "cmask_sb", [128, 128], F32)

        att = ExitStack()
        cos_sb = self.sb(att, "cos_sb", [128, T], F32)
        sin_sb = self.sb(att, "sin_sb", [128, T], F32)
        wvu_sb = self.sb(att, "wvu_sb", [128, 2 * 1024], BF16)
        wq_gs = [self.sb(att, "wq_g%d" % p, [128, 8 * 256], BF16) for p in range(2)]
        wku_gs = [self.sb(att, "wku_g%d" % p, [128, 2 * 256], BF16) for p in range(2)]
        wqr_gs = [self.sb(att, "wqr_g%d" % p, [128, 8 * 128], BF16) for p in range(2)]
        krr_a = self.sb(att, "krr_a", [16, T], BF16)
        krr_b = self.sb(att, "krr_b", [16, T], BF16)
        hp_q = self.sb(att, "hp_q", [64, 2 * T], BF16)
        rtmp = self.sb(att, "rtmp", [64, 2048], F32)
        s_fm = ExitStack()
        xn1_fm = self.sb(s_fm, "xn1_fm", [128, 8 * T], BF16)
        latent = self.sb(s_fm, "latent", [128, 2 * T], BF16)
        s_pro = ExitStack()
        wkvd_sb = self.sb(s_pro, "wkvd_sb", [128, 8 * 256], BF16)
        wkr_sb = self.sb(s_pro, "wkr_sb", [128, 8 * DR], BF16)
        s_rtm = ExitStack()
        r_tm = self.sb(s_rtm, "r_tm", [128, NT * E], F32)
        s_xn1t = ExitStack()
        xn1_tm = self.sb(s_xn1t, "xn1_tm", [128, NT * E], BF16, side="right")

        with nc.Block() as blk:
            @blk.sync
            def _(s):
                for dst, srcd in [(self.idb, identb_d), (self.idf, identf_d),
                                  (self.iota_sb, iota_d)]:
                    self.mark(s.dma_start(out=dst[:], in_=srcd.ap()), "sD", 16)
                self.mark(s.dma_start(out=r_tm[:].rearrange("p (t e) -> p t e", e=E),
                                      in_=x_in.ap().rearrange("(t p) e -> p t e", p=128)), "sD", 16)
                self.mark(s.dma_start(out=cos_sb[:], in_=cosT_d.ap()), "sD", 16)
                self.mark(s.dma_start(out=self.cmask[:], in_=cmask_d.ap()), "sD", 16)
                self.mark(s.dma_start(out=sin_sb[:], in_=sinT_d.ap()), "sD", 16)
            @blk.gpsimd
            def _(g):
                for dst, srcd, ch in [(wkvd_sb, wkvd_d, 8), (wkr_sb, wkr_d, 8),
                                      (wvu_sb, wvu_d, 2)]:
                    self.mark(g.dma_start(out=dst[:].rearrange("p (c f) -> p c f", c=ch),
                                          in_=srcd.ap().rearrange("(c p) f -> p c f", p=128)),
                              "sG", 16)
                self.mark(g.dma_start(out=self.srow[0:1, 0:H],
                                      in_=lobo_d.ap().unsqueeze(0)), "sG", 16)
            for name in ["vector", "scalar", "tensor"]:
                def wtr(eng):
                    eng.wait_ge(self.H("sD"), self.V("sD"))
                    eng.wait_ge(self.H("sG"), self.V("sG"))
                getattr(blk, name)(wtr)
            @blk.vector
            def _(v):
                v.memset(self.c128_sb[:], 128.0)
                v.memset(self.ones_f[:], 1.0)
                v.memset(self.ones_b[:], 1.0)
                v.memset(self.scal[:, 7:8], MAGIC)

        self.ln_stats(lambda t: r_tm[:, t * E:(t + 1) * E], NT)
        self.ln_finish(NT)
        with nc.Block() as blk:
            @blk.scalar
            def _(sc):
                for t in range(NT):
                    sc.activation(out=xn1_tm[:, t * E:(t + 1) * E],
                                  in_=r_tm[:, t * E:(t + 1) * E], func=AF.Identity,
                                  scale=self.lnab[:, 2 * t:2 * t + 1],
                                  bias=self.lnab[:, 2 * t + 1:2 * t + 2])
        with nc.Block() as blk:
            @blk.sync
            def _(s):
                self.mark(s.dma_start(out=r_spill.ap().rearrange("(t p) e -> p t e", p=128),
                                      in_=r_tm[:].rearrange("p (t e) -> p t e", e=E)), "sD", 16)
                s.wait_ge(self.H("sD"), self.V("sD"))
        s_rtm.close()

        pstr = ExitStack()
        pa = self.psum(pstr, "psA", [128, 1024], BF16)
        pb = self.psum(pstr, "psB", [128, 1024], BF16)
        self.transpose_tm_to_fm(xn1_tm, xn1_fm, pa, pb)
        pstr.close()
        s_xn1t.close()

        att_r = ExitStack()
        combQs = [self.sb(att_r, "combQ%d" % p, [128, 4 * T], BF16, side="right")
                  for p in range(2)]
        combKs = [self.sb(att_r, "combK%d" % p, [128, 4 * T], BF16, side="right")
                  for p in range(2)]
        vaugs = [self.sb(att_r, "vaug%d" % p, [128, NT * 4 * 65], BF16, side="right")
                 for p in range(2)]
        E_sb = self.sb(att_r, "E_sb", [128, 2 * 512], BF16, side="right")
        runmax = self.sb(att_r, "runmax", [128, T], BF16, side="right")
        ystage = self.sb(att_r, "ystage", [64, T], BF16, side="right")
        drow = self.sb(att_r, "drow", [65, 512], F32, side="right")
        dq_sb = self.sb(att_r, "dq_sb", [4, 128], F32, side="right")
        mq_sb = self.sb(att_r, "mq_sb", [4, 128], F32, side="right")
        rq_sb = self.sb(att_r, "rq_sb", [4, 128], F32, side="right")
        rbc = self.sb(att_r, "rbc", [64, 512], F32, side="right")

        php = ExitStack()
        ps_y = self.psum(php, "ps_y", [65, T], F32)
        ps_s0 = self.psum(php, "ps_s0", [128, 512], F32)
        ps_s1 = self.psum(php, "ps_s1", [128, 512], F32)
        psA = self.psum(php, "psA_att", [128, 1024], BF16)
        psB = self.psum(php, "psB_att", [128, 512], F32)
        ps_s = [ps_s0, ps_s1]
        ps_a = ps_s0.ap()
        ps_b = ps_s1.ap()
        psA_bf = psA.ap()

        with nc.Block() as blk:
            @blk.scalar
            def _(sc):
                sc.activation(out=self.srow[0:1, 0:H], in_=self.srow[0:1, 0:H],
                              func=AF.Exp, bias=0.0, scale=1.0)
        with nc.Block() as blk:
            @blk.vector
            def _(v):
                v.tensor_copy(out=self.crow_b[0:1, 0:H], in_=self.srow[0:1, 0:H])
        with nc.Block() as blk:
            @blk.tensor
            def _(t_):
                t_.matmul(ps_a[0:128, 0:16], self.ones_b[0:1, 0:128],
                          self.crow_b[0:1, 0:16], start=True, stop=True)
        with nc.Block() as blk:
            @blk.vector
            def _(v):
                i = v.tensor_copy(out=self.crep[:, 0:H], in_=ps_a[0:128, 0:16])
                i.then_inc(self.H("sEV"), 1)
                self.inc("sEV")
                for p in range(2):
                    v.memset(vaugs[p][:].rearrange("p (t h c) -> p t h c",
                                                   h=4, c=65)[:, :, :, 64:65], 1.0)

        def lat_mm(t_, it, buf):
            c, tc = it
            return self.acc_mm(t_, buf[0:128, 0:512],
                        [(wkvd_sb[:, ci * DKV + c * 128: ci * DKV + (c + 1) * 128],
                          xn1_fm[:, ci * T + tc * 512: ci * T + tc * 512 + 512])
                         for ci in range(8)])

        def lat_ev(v, it, buf):
            c, tc = it
            return v.tensor_copy(out=latent[:, c * T + tc * 512: c * T + tc * 512 + 512],
                                 in_=buf[0:128, 0:512])

        self.mm_evac([(c, tc) for c in range(2) for tc in range(n512)], lat_mm, lat_ev,
                     [ps_a, ps_b])

        def kr_mm(t_, tc, buf):
            self.acc_mm(t_, buf[0:16, 0:512],
                        [(wkr_sb[:, ci * DR: ci * DR + 16],
                          xn1_fm[:, ci * T + tc * 512: ci * T + tc * 512 + 512])
                         for ci in range(8)])
            return self.acc_mm(t_, buf[32:48, 0:512],
                        [(wkr_sb[:, ci * DR + 16: ci * DR + 32],
                          xn1_fm[:, ci * T + tc * 512: ci * T + tc * 512 + 512])
                         for ci in range(8)])

        def kr_ev(v, tc, buf):
            if tc >= 1:
                v.wait_ge(self.H("sEV"), self.V("sEV") + tc)
                v.wait_ge(self.H("sVo"), self.V("sVo"))
            cs = cos_sb[0:16, tc * 512:(tc + 1) * 512]
            sn = sin_sb[0:16, tc * 512:(tc + 1) * 512]
            fh, sh = buf[0:16, 0:512], buf[32:48, 0:512]
            i1 = v.tensor_tensor(out=rtmp[0:16, 0:512], in0=sh, in1=sn, op=ALU.mult)
            i2 = v.tensor_tensor(out=rtmp[0:16, 512:1024], in0=fh, in1=sn, op=ALU.mult)
            i3 = v.tensor_tensor(out=rtmp[0:16, 1024:1536], in0=fh, in1=cs, op=ALU.mult)
            i4 = v.tensor_tensor(out=rtmp[0:16, 1536:2048], in0=sh, in1=cs, op=ALU.mult)
            self.edges(v, [i1, i2, i3, i4])
            i5 = v.tensor_sub(out=krr_a[0:16, tc * 512:(tc + 1) * 512],
                              in0=rtmp[0:16, 1024:1536], in1=rtmp[0:16, 0:512])
            self.mark(i5, "sVo")
            return v.tensor_add(out=krr_b[0:16, tc * 512:(tc + 1) * 512],
                                in0=rtmp[0:16, 1536:2048], in1=rtmp[0:16, 512:1024])

        self.mm_evac(list(range(n512)), kr_mm, kr_ev, [ps_a, ps_b])
        s_pro.close()

        def grp_loads(eng, g):
            par = g % 2
            self.mark(eng.dma_start(
                out=wq_gs[par][:].rearrange("p (c f) -> p c f", c=8),
                in_=wq_d.ap().rearrange("(c p) f -> p c f", p=128)[:, :, g * 256:(g + 1) * 256]),
                "sG", 16)
            self.mark(eng.dma_start(
                out=wku_gs[par][:].rearrange("p (c f) -> p c f", c=2),
                in_=wku_d.ap().rearrange("(c p) f -> p c f", p=128)[:, :, g * 256:(g + 1) * 256]),
                "sG", 16)
            fh0 = (g // 2) * 128 + (g % 2) * 64
            sh0 = (2 + g // 2) * 128 + (g % 2) * 64
            self.mark(eng.dma_start(
                out=wqr_gs[par][:].rearrange("p (c f) -> p c f", c=8)[:, :, 0:64],
                in_=wqr_d.ap().rearrange("(c p) f -> p c f", p=128)[:, :, fh0:fh0 + 64]),
                "sG", 16)
            self.mark(eng.dma_start(
                out=wqr_gs[par][:].rearrange("p (c f) -> p c f", c=8)[:, :, 64:128],
                in_=wqr_d.ap().rearrange("(c p) f -> p c f", p=128)[:, :, sh0:sh0 + 64]),
                "sG", 16)

        with nc.Block() as blk:
            @blk.gpsimd
            def _(g):
                grp_loads(g, 0)
                g.wait_ge(self.H("sG"), self.V("sG"))
            @blk.tensor
            def _(t_):
                t_.wait_ge(self.H("sG"), self.V("sG"))

        def qc_mm(t_, it, buf):
            j, tc = it
            return self.acc_mm(t_, buf[0:64, 0:512],
                        [(wq_gs[0][:, ci * 256 + j * 64: ci * 256 + (j + 1) * 64],
                          xn1_fm[:, ci * T + tc * 512: ci * T + tc * 512 + 512])
                         for ci in range(8)])

        def qc_ev(v, it, buf):
            j, tc = it
            return v.tensor_copy(out=combQs[0][0:64, j * T + tc * 512: j * T + tc * 512 + 512],
                                 in_=buf[0:64, 0:512])

        self.mm_evac([(j, tc) for j in range(4) for tc in range(n512)], qc_mm, qc_ev,
                     [ps_a, ps_b])

        def kc_mm(t_, it, buf):
            j, tc = it
            return self.acc_mm(t_, buf[0:64, 0:512],
                        [(wku_gs[0][:, c * 256 + j * 64: c * 256 + (j + 1) * 64],
                          latent[:, c * T + tc * 512: c * T + tc * 512 + 512])
                         for c in range(2)])

        def kc_ev(v, it, buf):
            j, tc = it
            return v.tensor_copy(out=combKs[0][0:64, j * T + tc * 512: j * T + tc * 512 + 512],
                                 in_=buf[0:64, 0:512])

        self.mm_evac([(j, tc) for j in range(4) for tc in range(n512)], kc_mm, kc_ev,
                     [ps_a, ps_b])

        def qr_mm(t_, it, buf):
            tc, part = it
            return self.acc_mm(t_, buf[0:64, 0:512],
                        [(wqr_gs[0][:, ci * 128 + part * 64: ci * 128 + part * 64 + 64],
                          xn1_fm[:, ci * T + tc * 512: ci * T + tc * 512 + 512])
                         for ci in range(8)])

        def qr_ev(v, it, buf):
            tc, part = it
            if part == 0 and tc >= 1:
                v.wait_ge(self.H("sEV"), self.V("sEV") + 2 * tc)
                v.wait_ge(self.H("sVo"), self.V("sVo"))
            dst = rtmp[0:64, 0:512] if part == 0 else rtmp[0:64, 512:1024]
            i0 = v.tensor_copy(out=dst, in_=buf[0:64, 0:512])
            if part == 0:
                return i0
            v.wait_ge(self.H("sEV"), self.V("sEV") + 2 * tc + 1)
            self.edge(v, i0)
            cs = cos_sb[0:64, tc * 512:(tc + 1) * 512]
            sn = sin_sb[0:64, tc * 512:(tc + 1) * 512]
            fh, sh = rtmp[0:64, 0:512], rtmp[0:64, 512:1024]
            i1 = v.tensor_tensor(out=rtmp[0:64, 1024:1536], in0=sh, in1=sn, op=ALU.mult)
            i2 = v.tensor_tensor(out=rtmp[0:64, 1536:2048], in0=fh, in1=sn, op=ALU.mult)
            self.edges(v, [i1, i2])
            i3 = v.tensor_tensor(out=fh, in0=fh, in1=cs, op=ALU.mult)
            i4 = v.tensor_tensor(out=sh, in0=sh, in1=cs, op=ALU.mult)
            self.edges(v, [i3, i4])
            i5 = v.tensor_sub(out=hp_q[0:64, tc * 512: tc * 512 + 512],
                              in0=fh, in1=rtmp[0:64, 1024:1536])
            self.mark(i5, "sVo")
            return v.tensor_add(out=hp_q[0:64, T + tc * 512: T + tc * 512 + 512],
                                in0=sh, in1=rtmp[0:64, 1536:2048])

        self.mm_evac([(tc, part) for tc in range(n512) for part in range(2)],
                     qr_mm, qr_ev, [ps_a, ps_b])

        with nc.Block() as blk:
            @blk.sync
            def _(s):
                for j in range(4):
                    self.mark(s.dma_start(out=combQs[0][64:80, j * T:(j + 1) * T],
                                          in_=hp_q[j * 16:(j + 1) * 16, 0:T]), "sD", 16)
                    self.mark(s.dma_start(out=combQs[0][80:96, j * T:(j + 1) * T],
                                          in_=hp_q[j * 16:(j + 1) * 16, T:2 * T]), "sD", 16)
                    self.mark(s.dma_start(out=combKs[0][64:80, j * T:(j + 1) * T],
                                          in_=krr_a[0:16, 0:T]), "sD", 16)
                    self.mark(s.dma_start(out=combKs[0][80:96, j * T:(j + 1) * T],
                                          in_=krr_b[0:16, 0:T]), "sD", 16)
                s.wait_ge(self.H("sD"), self.V("sD"))

        def v_mm(t_, tt, buf):
            return self.acc_mm(t_, buf[0:128, 0:256],
                        [(latent[:, c * T + tt * 128: c * T + (tt + 1) * 128],
                          wvu_sb[:, c * 1024 + 0 * 256: c * 1024 + 1 * 256])
                         for c in range(2)])

        def v_ev(v, tt, buf):
            return v.tensor_copy(
                out=vaugs[0][:, tt * 260:(tt + 1) * 260].rearrange("p (h c) -> p h c", c=65)[:, :, 0:64],
                in_=buf[0:128, 0:256].rearrange("p (h c) -> p h c", c=64))

        self.mm_evac(list(range(NT)), v_mm, v_ev, [ps_a, ps_b])

        NB = n512
        PPH = sum(4 * b + 4 for b in range(NB))
        pieces = []
        for h in range(H):
            for b in range(NB - 1, -1, -1):
                for kt in range(4 * b + 4):
                    offp = b * 512 if kt < 4 * b else kt * 128
                    nn = 512 if kt < 4 * b else (b + 1) * 512 - kt * 128
                    pieces.append((h, b, kt, offp, nn))
        NPIE = len(pieces)
        geidx = {}
        for gi, pc in enumerate(pieces):
            geidx[(pc[0], pc[1])] = gi

        import os as _os
        pipe = PPH >= 40 and not _os.environ.get("K_NOPIPE")
        TRIG = ([("drow", 3), ("mT", 3), ("dq", 4), ("mred", 5), ("mqT", 6),
                 ("mqc", 7), ("rq", 8), ("rdma", 9), ("ys0", 12),
                 ("ys1", 13), ("spill", 14)] if pipe else
                [(nm, 1) for nm in ("drow", "mT", "dq", "mred", "mqT", "mqc",
                                    "rq", "rdma", "ys0", "ys1", "spill")])
        _tallow = _os.environ.get("K_TRIG")
        trig_at = {}
        for h in range(H):
            for b in reversed(range(NB)):
                for nm, o in TRIG:
                    if _tallow is not None and nm not in _tallow.split(","):
                        continue
                    trig_at.setdefault(geidx[(h, b)] + o, []).append((nm, h, b))

        tiles_at, repack_at, loads_at = {}, {}, {}
        grp_ntiles = {}
        for g in range(1, 4):
            tl = []
            for tc in range(n512):
                for part in range(2):
                    for hf in range(2):
                        tl.append(("qr", g, tc, part, hf))
                tl.append(("rope", g, tc))
            for j in range(4):
                for tc in range(n512):
                    for hf in range(2):
                        tl.append(("qc", g, j, tc, hf))
            for j in range(4):
                for tc in range(n512):
                    for hf in range(2):
                        tl.append(("kc", g, j, tc, hf))
            for tt in range(NT):
                tl.append(("va", g, tt))
            hh = 4 * (g - 1)
            base = 0 if hh == 0 else geidx[(hh - 1, 0)] + 1
            span = (7 * PPH // 2) if pipe else 3 * PPH
            ntl = len(tl)
            grp_ntiles[g] = ntl
            for s_, t_ in enumerate(tl):
                tiles_at.setdefault(base + s_ * span // ntl, []).append(t_)
            last_rope = 5 * n512 - 1
            repack_at.setdefault(base + last_rope * span // ntl + 2, []).append(g)
            loads_at.setdefault(base, []).append(g)

        DBG = int(_os.environ.get("K_DBG", "0"))
        EVs = {k: [] for k in ("t", "v", "c", "y", "g")}
        cnt = {s: self.V(s) for s in ("sMM", "sEV", "sEX", "sTM", "sAV",
                                      "sD", "sG", "sMX", "sDR", "sVo")}
        post = {("evg", 0): cnt["sEV"], ("rpk", 0): cnt["sD"],
                ("pmmlast", 0): ("sMM", cnt["sMM"])}

        def ADD(engk, waits, fn):
            w = {}
            for s_, v_ in waits:
                if v_ is not None and v_ > 0:
                    w[s_] = max(w.get(s_, 0), v_)
            EVs[engk].append((sorted(w.items()), fn))

        def prevbank(h, b):
            return (h, b + 1) if b + 1 < NB else (h - 1, 0)

        ptile_k = {1: 0, 2: 0, 3: 0}

        maxg = NPIE + 16 if DBG == 0 else (0 if DBG == 1 else NPIE + 16)
        if DBG == 2:
            trig_at = {}
        if DBG in (2, 3):
            tiles_at, repack_at, loads_at = {}, {}, {}
        for g in range(maxg):
            for gg in loads_at.get(g, []):
                pm = post[("pmmlast", gg - 2)] if gg >= 2 else None
                pv = cnt["sG"] + 64

                def ld_fn(eng, gg=gg, pv=pv):
                    grp_loads(eng, gg)
                    eng.wait_ge(self.H("sG"), pv)
                ADD("g", ([(pm[0], pm[1])] if pm else []), ld_fn)
                post[("loads", gg)] = pv
                cnt["sG"] += 64

            for gg in repack_at.get(g, []):
                par = gg % 2
                w = [("sEV", post[("rope", gg, n512 - 1)])]
                if gg >= 2:
                    w.append(("sMM", post[("mm", geidx[(4 * (gg - 2) + 3, 0)])]))

                pv = cnt["sD"] + 256

                def rpk_fn(s, par=par, pv=pv):
                    for j in range(4):
                        for dst, src in [(combQs[par][64:80, j * T:(j + 1) * T],
                                          hp_q[j * 16:(j + 1) * 16, 0:T]),
                                         (combQs[par][80:96, j * T:(j + 1) * T],
                                          hp_q[j * 16:(j + 1) * 16, T:2 * T]),
                                         (combKs[par][64:80, j * T:(j + 1) * T],
                                          krr_a[0:16, 0:T]),
                                         (combKs[par][80:96, j * T:(j + 1) * T],
                                          krr_b[0:16, 0:T])]:
                            s.dma_start(out=dst, in_=src).then_inc(self.H("sD"), 16)
                    s.wait_ge(self.H("sD"), pv)
                ADD("y", w, rpk_fn)
                cnt["sD"] += 256
                post[("rpk", gg)] = cnt["sD"]

            def emit_av(gp):
                h, b, kt, offp, nn = pieces[gp]
                par, j = (h // 4) % 2, h % 4
                slot = gp % 2
                w = [("sEX", post[("exp", gp)])]
                if gp < 2:
                    w.append(("sEV", post[("evg", 0)]))
                if kt == 0 and h > 0:
                    w += [("sTM", post.get(("ysTT", h - 1, b, 1))),
                          ("sDR", post.get(("drow", h - 1, b)))]

                def fn(t_, h=h, b=b, kt=kt, offp=offp, nn=nn, par=par, j=j, slot=slot):
                    t_.matmul(ps_y[0:65, offp:offp + nn],
                              vaugs[par][:, kt * 260 + j * 65: kt * 260 + (j + 1) * 65],
                              E_sb[:, slot * 512: slot * 512 + nn],
                              start=(kt == 0), stop=(kt == 4 * b + 3)
                              ).then_inc(self.H("sAV"), 1)
                ADD("t", w, fn)
                cnt["sAV"] += 1
                post[("av", gp)] = cnt["sAV"]

            if g < NPIE:
                h, b, kt, offp, nn = pieces[g]
                par, j = (h // 4) % 2, h % 4
                slot = g % 2
                diag = (offp == kt * 128)
                first_piece = (b == NB - 1 and kt == 0)
                w = []
                if g >= 2:
                    w.append(("sEX", post[("exp", g - 2)]))
                if first_piece and h % 4 == 0:
                    grp = h // 4
                    if grp == 0:
                        w += [("sEV", post[("evg", 0)]), ("sD", post[("rpk", 0)])]
                    else:
                        w += [("sEV", post.get(("pev", grp, grp_ntiles[grp] - 1 - n512))),
                              ("sD", post.get(("rpk", grp)))]

                def sc_fn(t_, nn=nn, par=par, j=j, kt=kt, offp=offp, diag=diag, slot=slot):
                    i1 = t_.matmul(ps_s[slot].ap()[0:128, 0:nn],
                                   combKs[par][0:96, j * T + kt * 128: j * T + (kt + 1) * 128],
                                   combQs[par][0:96, j * T + offp: j * T + offp + nn],
                                   start=True, stop=not diag)
                    i1.then_inc(self.H("sMM"), 1)
                    if diag:
                        t_.matmul(ps_s[slot].ap()[0:128, 0:128], self.idf[:],
                                  self.cmask[:, 0:128], start=False, stop=True
                                  ).then_inc(self.H("sMM"), 1)
                ADD("t", w, sc_fn)
                cnt["sMM"] += 2 if diag else 1
                post[("mm", g)] = cnt["sMM"]
                if g >= 1:
                    emit_av(g - 1)
            if g == NPIE:
                emit_av(NPIE - 1)

            for (nm, h, b) in trig_at.get(g, []):
                pb = prevbank(h, b)
                if nm == "drow":
                    w = [("sAV", post[("av", geidx[(h, b)])])]
                    if ("dq",) + pb in post:
                        w.append(("sD", post[("dq",) + pb]))

                    def fn(sc, b=b):
                        sc.activation(out=drow[64:65, 0:512],
                                      in_=ps_y[64:65, b * 512:(b + 1) * 512],
                                      func=AF.Identity, bias=0.0, scale=1.0
                                      ).then_inc(self.H("sDR"), 1)
                    ADD("c", w, fn)
                    cnt["sDR"] += 1
                    post[("drow", h, b)] = cnt["sDR"]
                elif nm == "mT":
                    w = [("sTM", post[("tm", geidx[(h, b)])])]
                    if ("mred",) + pb in post:
                        w.append(("sTM", post[("mred",) + pb]))
                    if ("mqc",) + pb in post:
                        w.append(("sEV", post[("mqc",) + pb]))

                    def fn(t_, b=b):
                        for qi in range(4):
                            t_.transpose(psA_bf[0:128, qi * 128:(qi + 1) * 128],
                                         runmax[:, (4 * b + qi) * 128:(4 * b + qi + 1) * 128],
                                         self.idb[:]).then_inc(self.H("sMX"), 1)
                    ADD("t", w, fn)
                    cnt["sMX"] += 4
                    post[("mT", h, b)] = cnt["sMX"]
                elif nm == "dq":
                    w = [("sDR", post[("drow", h, b)])]
                    if ("recip",) + pb in post:
                        w.append(("sEV", post[("recip",) + pb]))
                        w.append(("sVo", post[("rqstt",) + pb]))
                    pv = cnt["sD"] + 64

                    def fn(s, pv=pv):
                        for qi in range(4):
                            s.dma_start(out=dq_sb[qi:qi + 1, 0:128],
                                        in_=drow[64:65, qi * 128:(qi + 1) * 128]
                                        ).then_inc(self.H("sD"), 16)
                        s.wait_ge(self.H("sD"), pv)
                    ADD("y", w, fn)
                    cnt["sD"] += 64
                    post[("dq", h, b)] = cnt["sD"]
                elif nm == "mred":
                    w = [("sMX", post[("mT", h, b)])]

                    def fn(v, b=b):
                        for qi in range(4):
                            v.tensor_reduce(out=self.mstage[:, 4 * b + qi: 4 * b + qi + 1],
                                            in_=psA_bf[0:128, qi * 128:(qi + 1) * 128],
                                            axis=AX.X, op=ALU.max).then_inc(self.H("sTM"), 1)
                    ADD("v", w, fn)
                    cnt["sTM"] += 4
                    post[("mred", h, b)] = cnt["sTM"]
                elif nm == "mqT":
                    w = [("sTM", post[("mred", h, b)])]
                    if ("mqc",) + pb in post:
                        w.append(("sEV", post[("mqc",) + pb]))

                    def fn(t_, b=b):
                        t_.transpose(psA_bf[0:4, 512:640], self.mstage[:, 4 * b:4 * b + 4],
                                     self.idb[:]).then_inc(self.H("sMX"), 1)
                    ADD("t", w, fn)
                    cnt["sMX"] += 1
                    post[("mqT", h, b)] = cnt["sMX"]
                elif nm == "mqc":
                    w = [("sMX", post[("mqT", h, b)])]
                    if ("recip",) + pb in post:
                        w.append(("sEV", post[("recip",) + pb]))
                        w.append(("sVo", post[("rqstt",) + pb]))

                    def fn(v):
                        v.tensor_copy(out=mq_sb[0:4, 0:128],
                                      in_=psA_bf[0:4, 512:640]).then_inc(self.H("sEV"), 1)
                    ADD("v", w, fn)
                    cnt["sEV"] += 1
                    post[("mqc", h, b)] = cnt["sEV"]
                elif nm == "rq":
                    w = [("sD", post[("dq", h, b)]), ("sEV", post[("mqc", h, b)])]
                    if ("rdma",) + pb in post:
                        w.append(("sD", post[("rdma",) + pb]))
                    vo = cnt["sVo"] + 1

                    def fn(v, h=h, vo=vo):
                        i1 = v.scalar_tensor_tensor(out=rq_sb[0:4, 0:128],
                                                    in0=mq_sb[0:4, 0:128],
                                                    scalar=self.crep[0:4, h:h + 1],
                                                    in1=dq_sb[0:4, 0:128],
                                                    op0=ALU.mult, op1=ALU.add)
                        i1.then_inc(self.H("sVo"), 1)
                        v.wait_ge(self.H("sVo"), vo)
                        v.reciprocal(out=rq_sb[0:4, 0:128],
                                     in_=rq_sb[0:4, 0:128]).then_inc(self.H("sEV"), 1)
                    ADD("v", w, fn)
                    cnt["sVo"] += 1
                    post[("rqstt", h, b)] = cnt["sVo"]
                    cnt["sEV"] += 1
                    post[("recip", h, b)] = cnt["sEV"]
                elif nm == "rdma":
                    w = [("sEV", post[("recip", h, b)])]
                    if ("ysTT", pb[0], pb[1], 1) in post:
                        w.append(("sTM", post[("ysTT", pb[0], pb[1], 1)]))
                    pv1 = cnt["sD"] + 16
                    pv2 = cnt["sD"] + 80

                    def fn(s, pv1=pv1, pv2=pv2):
                        s.dma_start(out=r_d.ap(), in_=rq_sb[0:4, 0:128]
                                    ).then_inc(self.H("sD"), 16)
                        s.wait_ge(self.H("sD"), pv1)
                        for q in range(4):
                            s.dma_start(out=rbc[0:64, q * 128:(q + 1) * 128],
                                        in_=r_d.ap()[q:q + 1, :].broadcast_to((64, 128))
                                        ).then_inc(self.H("sD"), 16)
                        s.wait_ge(self.H("sD"), pv2)
                    ADD("y", w, fn)
                    cnt["sD"] += 80
                    post[("rdma", h, b)] = cnt["sD"]
                elif nm in ("ys0", "ys1"):
                    hf = int(nm[-1])
                    w = [("sD", post[("rdma", h, b)]),
                         ("sAV", post[("av", geidx[(h, b)])])]
                    if h > 0 and ("spill", h - 1, b) in post:
                        w.append(("sD", post[("spill", h - 1, b)]))
                    vo = cnt["sVo"] + 1

                    def fn(v, b=b, hf=hf, vo=vo):
                        cc = b * 512 + hf * 256
                        i1 = v.tensor_copy(out=ystage[0:64, cc:cc + 256],
                                           in_=ps_y[0:64, cc:cc + 256])
                        i1.then_inc(self.H("sVo"), 1)
                        v.wait_ge(self.H("sVo"), vo)
                        v.tensor_tensor(out=ystage[0:64, cc:cc + 256],
                                        in0=ystage[0:64, cc:cc + 256],
                                        in1=rbc[0:64, hf * 256:(hf + 1) * 256],
                                        op=ALU.mult).then_inc(self.H("sTM"), 1)
                    ADD("v", w, fn)
                    cnt["sVo"] += 1
                    cnt["sTM"] += 1
                    post[("ysTT", h, b, hf)] = cnt["sTM"]
                elif nm == "spill":
                    w = [("sTM", post[("ysTT", h, b, 1)])]
                    pv = cnt["sD"] + 16

                    def fn(s, h=h, b=b, pv=pv):
                        s.dma_start(out=y_spill.ap()[h * 64:(h + 1) * 64, b * 512:(b + 1) * 512],
                                    in_=ystage[0:64, b * 512:(b + 1) * 512]
                                    ).then_inc(self.H("sD"), 16)
                        s.wait_ge(self.H("sD"), pv)
                    ADD("y", w, fn)
                    cnt["sD"] += 16
                    post[("spill", h, b)] = cnt["sD"]

            if g < NPIE:
                h, b, kt, offp, nn = pieces[g]
                par, j = (h // 4) % 2, h % 4
                slot = g % 2
                diag = (offp == kt * 128)
                w = [("sMM", post[("mm", g)])]
                if g >= 2:
                    w += [("sTM", post[("tm", g - 2)]), ("sAV", post[("av", g - 2)])]

                def ex_fn(sc, nn=nn, slot=slot):
                    sc.activation(out=E_sb[:, slot * 512: slot * 512 + nn],
                                  in_=ps_s[slot].ap()[0:128, 0:nn], func=AF.Exp,
                                  bias=0.0, scale=SCALE_ATT).then_inc(self.H("sEX"), 1)
                ADD("c", w, ex_fn)
                cnt["sEX"] += 1
                post[("exp", g)] = cnt["sEX"]
                w = [("sEX", post[("exp", g)])]
                if kt == 0:
                    if h > 0:
                        w.append(("sMX", post.get(("mT", h - 1, b))))
                        w.append(("sTM", post[("tm", geidx[(h - 1, b)])]))
                else:
                    w.append(("sTM", cnt["sTM"]))

                def tm_fn(v, nn=nn, offp=offp, slot=slot, kt=kt):
                    if kt == 0:
                        i = v.tensor_copy(out=runmax[:, offp:offp + nn],
                                          in_=E_sb[:, slot * 512: slot * 512 + nn])
                    else:
                        i = v.tensor_max(out=runmax[:, offp:offp + nn],
                                         in0=runmax[:, offp:offp + nn],
                                         in1=E_sb[:, slot * 512: slot * 512 + nn])
                    i.then_inc(self.H("sTM"), 1)
                ADD("v", w, tm_fn)
                cnt["sTM"] += 1
                post[("tm", g)] = cnt["sTM"]
            for t in tiles_at.get(g, []):
                kind, gg = t[0], t[1]
                par = gg % 2
                if kind == "rope":
                    tc = t[2]
                    w = [("sD", post[("rpk", gg - 1)]), ("sEV", cnt["sEV"])]
                    vo = cnt["sVo"]

                    def fn(v, tc=tc, vo=vo):
                        cs = cos_sb[0:64, tc * 512:(tc + 1) * 512]
                        sn = sin_sb[0:64, tc * 512:(tc + 1) * 512]
                        fh, sh = rtmp[0:64, 0:512], rtmp[0:64, 512:1024]
                        i1 = v.tensor_tensor(out=rtmp[0:64, 1024:1536], in0=sh, in1=sn, op=ALU.mult)
                        i2 = v.tensor_tensor(out=rtmp[0:64, 1536:2048], in0=fh, in1=sn, op=ALU.mult)
                        i1.then_inc(self.H("sVo"), 1)
                        i2.then_inc(self.H("sVo"), 1)
                        v.wait_ge(self.H("sVo"), vo + 2)
                        i3 = v.tensor_tensor(out=fh, in0=fh, in1=cs, op=ALU.mult)
                        i4 = v.tensor_tensor(out=sh, in0=sh, in1=cs, op=ALU.mult)
                        i3.then_inc(self.H("sVo"), 1)
                        i4.then_inc(self.H("sVo"), 1)
                        v.wait_ge(self.H("sVo"), vo + 4)
                        v.tensor_sub(out=hp_q[0:64, tc * 512: tc * 512 + 512],
                                     in0=fh, in1=rtmp[0:64, 1024:1536]
                                     ).then_inc(self.H("sVo"), 1)
                        v.tensor_add(out=hp_q[0:64, T + tc * 512: T + tc * 512 + 512],
                                     in0=sh, in1=rtmp[0:64, 1536:2048]
                                     ).then_inc(self.H("sEV"), 1)
                    ADD("v", w, fn)
                    cnt["sVo"] += 5
                    cnt["sEV"] += 1
                    post[("rope", gg, tc)] = cnt["sEV"]
                    continue
                k = ptile_k[gg]
                ptile_k[gg] += 1
                w = [("sG", post[("loads", gg)])]
                if k >= 1:
                    w.append(("sEV", post[("pev", gg, k - 1)]))
                elif gg >= 2:
                    w.append(("sEV", post[("pev", gg - 1, grp_ntiles[gg - 1] - 1 - n512)]))
                if kind == "qc":
                    _, _, j, tc, hf = t

                    def mmfn(t_, par=par, j=j, tc=tc, hf=hf):
                        li = None
                        for ci in range(8):
                            li = t_.matmul(psB.ap()[0:64, 0:256],
                                           wq_gs[par][:, ci * 256 + j * 64: ci * 256 + (j + 1) * 64],
                                           xn1_fm[:, ci * T + tc * 512 + hf * 256:
                                                  ci * T + tc * 512 + hf * 256 + 256],
                                           start=(ci == 0), stop=(ci == 7))
                        li.then_inc(self.H("sMX"), 1)

                    def evfn(v, par=par, j=j, tc=tc, hf=hf):
                        v.tensor_copy(out=combQs[par][0:64, j * T + tc * 512 + hf * 256:
                                                      j * T + tc * 512 + hf * 256 + 256],
                                      in_=psB.ap()[0:64, 0:256]).then_inc(self.H("sEV"), 1)
                elif kind == "kc":
                    _, _, j, tc, hf = t

                    def mmfn(t_, par=par, j=j, tc=tc, hf=hf):
                        li = None
                        for c in range(2):
                            li = t_.matmul(psB.ap()[0:64, 0:256],
                                           wku_gs[par][:, c * 256 + j * 64: c * 256 + (j + 1) * 64],
                                           latent[:, c * T + tc * 512 + hf * 256:
                                                  c * T + tc * 512 + hf * 256 + 256],
                                           start=(c == 0), stop=(c == 1))
                        li.then_inc(self.H("sMX"), 1)

                    def evfn(v, par=par, j=j, tc=tc, hf=hf):
                        v.tensor_copy(out=combKs[par][0:64, j * T + tc * 512 + hf * 256:
                                                      j * T + tc * 512 + hf * 256 + 256],
                                      in_=psB.ap()[0:64, 0:256]).then_inc(self.H("sEV"), 1)
                elif kind == "qr":
                    _, _, tc, part, hf = t

                    def mmfn(t_, par=par, tc=tc, part=part, hf=hf):
                        li = None
                        for ci in range(8):
                            li = t_.matmul(psB.ap()[0:64, 0:256],
                                           wqr_gs[par][:, ci * 128 + part * 64: ci * 128 + part * 64 + 64],
                                           xn1_fm[:, ci * T + tc * 512 + hf * 256:
                                                  ci * T + tc * 512 + hf * 256 + 256],
                                           start=(ci == 0), stop=(ci == 7))
                        li.then_inc(self.H("sMX"), 1)

                    def evfn(v, tc=tc, part=part, hf=hf):
                        v.tensor_copy(out=rtmp[0:64, part * 512 + hf * 256:
                                              part * 512 + hf * 256 + 256],
                                      in_=psB.ap()[0:64, 0:256]).then_inc(self.H("sEV"), 1)
                else:
                    _, _, tt = t

                    def mmfn(t_, par=par, gg=gg, tt=tt):
                        li = None
                        for c in range(2):
                            li = t_.matmul(psB.ap()[0:128, 0:256],
                                           latent[:, c * T + tt * 128: c * T + (tt + 1) * 128],
                                           wvu_sb[:, c * 1024 + gg * 256: c * 1024 + (gg + 1) * 256],
                                           start=(c == 0), stop=(c == 1))
                        li.then_inc(self.H("sMX"), 1)

                    def evfn(v, par=par, tt=tt):
                        v.tensor_copy(
                            out=vaugs[par][:, tt * 260:(tt + 1) * 260]
                            .rearrange("p (h c) -> p h c", c=65)[:, :, 0:64],
                            in_=psB.ap()[0:128, 0:256].rearrange("p (h c) -> p h c", c=64)
                        ).then_inc(self.H("sEV"), 1)
                ADD("t", w, mmfn)
                cnt["sMX"] += 1
                post[("pmm", gg, k)] = cnt["sMX"]
                post[("pmmlast", gg)] = ("sMX", cnt["sMX"])
                ADD("v", [("sMX", post[("pmm", gg, k)]), ("sEV", cnt["sEV"]),
                          ("sVo", cnt["sVo"])], evfn)
                cnt["sEV"] += 1
                post[("pev", gg, k)] = cnt["sEV"]

        with nc.Block() as blk:
            @blk.tensor
            def _(t_):
                for w, fn in EVs["t"]:
                    for s_, v_ in w:
                        t_.wait_ge(self.H(s_), v_)
                    fn(t_)
            @blk.vector
            def _(v):
                for w, fn in EVs["v"]:
                    for s_, v_ in w:
                        v.wait_ge(self.H(s_), v_)
                    fn(v)
            @blk.scalar
            def _(sc):
                for w, fn in EVs["c"]:
                    for s_, v_ in w:
                        sc.wait_ge(self.H(s_), v_)
                    fn(sc)
            @blk.sync
            def _(s):
                for w, fn in EVs["y"]:
                    for s_, v_ in w:
                        s.wait_ge(self.H(s_), v_)
                    fn(s)
            @blk.gpsimd
            def _(eng):
                for w, fn in EVs["g"]:
                    for s_, v_ in w:
                        eng.wait_ge(self.H(s_), v_)
                    fn(eng)

        for s_ in cnt:
            self.c[s_][1] = cnt[s_]

        php.close()
        att_r.close()
        s_fm.close()
        att.close()

        s_heads = ExitStack()
        wo_sb = self.sb(s_heads, "wo_sb", [128, 8 * 1024], BF16)
        s_wo = ExitStack()
        r_tm2 = self.sb(s_wo, "r_tm2", [128, NT * E], F32, side="right")
        attn_tm = self.sb(s_wo, "attn_tm", [128, NT * E], BF16, side="right")
        s_afm = ExitStack()
        attn_fm = self.sb(s_afm, "attn_fm", [128, 8 * T], BF16, side="right")
        s_yfm = ExitStack()
        y_fm = self.sb(s_yfm, "y_fm", [128, 8 * T], BF16, side="right")

        with nc.Block() as blk:
            @blk.sync
            def _(s):
                self.mark(s.dma_start(out=r_tm2[:].rearrange("p (t e) -> p t e", e=E),
                                      in_=r_spill.ap().rearrange("(t p) e -> p t e", p=128)),
                          "sD", 16)
                self.mark(s.dma_start(out=y_fm[:].rearrange("p (c t) -> p c t", c=8),
                                      in_=y_spill.ap().rearrange("(c p) t -> p c t", p=128)),
                          "sD", 16)
            @blk.gpsimd
            def _(g):
                self.mark(g.dma_start(out=wo_sb[:].rearrange("p (c f) -> p c f", c=8),
                                      in_=wo_d.ap().rearrange("(c p) f -> p c f", p=128)), "sG", 16)
            @blk.tensor
            def _(t_):
                t_.wait_ge(self.H("sG"), self.V("sG"))
                t_.wait_ge(self.H("sD"), self.V("sD"))

        pwo = ExitStack()
        wps_a = self.psum(pwo, "wps_a", [128, 512], F32)
        wps_b = self.psum(pwo, "wps_b", [128, 512], F32)

        def wo_mm(t_, it, buf):
            oc, tc = it
            return self.acc_mm(t_, buf[0:128, 0:512],
                        [(wo_sb[:, ci * 1024 + oc * 128: ci * 1024 + (oc + 1) * 128],
                          y_fm[:, ci * T + tc * 512: ci * T + tc * 512 + 512])
                         for ci in range(8)])

        def wo_ev(v, it, buf):
            oc, tc = it
            return v.tensor_copy(out=attn_fm[:, oc * T + tc * 512: oc * T + tc * 512 + 512],
                          in_=buf[0:128, 0:512])

        self.mm_evac([(oc, tc) for oc in range(8) for tc in range(n512)], wo_mm, wo_ev,
                     [wps_a, wps_b])
        pwo.close()
        s_yfm.close()
        s_heads.close()

        ptw = ExitStack()
        pa2 = self.psum(ptw, "wpsA", [128, 1024], BF16)
        pb2 = self.psum(ptw, "wpsB", [128, 1024], BF16)

        def fm2tm_mm(t_, tt, buf):
            for fc in range(8):
                li = t_.transpose(buf[:, fc * 128:(fc + 1) * 128],
                                  attn_fm[:, fc * T + tt * 128: fc * T + (tt + 1) * 128],
                                  self.idb[:])
            return li

        def fm2tm_ev(v, tt, buf):
            return v.tensor_copy(out=attn_tm[:, tt * E:(tt + 1) * E], in_=buf[:, 0:1024])

        self.mm_evac(list(range(NT)), fm2tm_mm, fm2tm_ev, [pa2, pb2])
        ptw.close()
        s_afm.close()

        self.ln_stats(lambda t: attn_tm[:, t * E:(t + 1) * E], NT)
        self.ln_finish(NT)
        with nc.Block() as blk:
            @blk.vector
            def _(v):
                v.wait_ge(self.H("sD"), self.V("sD"))
                for t in range(NT):
                    i = v.scalar_tensor_tensor(out=r_tm2[:, t * E:(t + 1) * E],
                                               in0=attn_tm[:, t * E:(t + 1) * E],
                                               scalar=self.lnab[:, 2 * t:2 * t + 1],
                                               in1=r_tm2[:, t * E:(t + 1) * E],
                                               op0=ALU.mult, op1=ALU.add)
                    self.edge(v, i)
                    i = v.tensor_scalar(out=r_tm2[:, t * E:(t + 1) * E],
                                        in0=r_tm2[:, t * E:(t + 1) * E],
                                        scalar1=self.lnab[:, 2 * t + 1:2 * t + 2],
                                        scalar2=None, op0=ALU.add)
                    self.edge(v, i)

        self.ln_stats(lambda t: r_tm2[:, t * E:(t + 1) * E], NT)
        self.ln_finish(NT)
        s_xn3 = ExitStack()
        xn3 = self.sb(s_xn3, "xn3", [128, NT * E], F32)
        ybuf = self.sb(s_xn3, "ybuf", [128, E], F32)
        mbuf = self.sb(s_xn3, "mbuf", [128, E], F32)
        with nc.Block() as blk:
            @blk.scalar
            def _(sc):
                for t in range(NT):
                    sc.activation(out=xn3[:, t * E:(t + 1) * E],
                                  in_=r_tm2[:, t * E:(t + 1) * E], func=AF.Identity,
                                  scale=self.lnab[:, 2 * t:2 * t + 1],
                                  bias=self.lnab[:, 2 * t + 1:2 * t + 2])
        with nc.Block() as blk:
            @blk.sync
            def _(s):
                self.mark(s.dma_start(out=r_spill.ap().rearrange("(t p) e -> p t e", p=128),
                                      in_=r_tm2[:].rearrange("p (t e) -> p t e", e=E)), "sD", 16)
                s.wait_ge(self.H("sD"), self.V("sD"))
        with nc.Block() as blk:
            @blk.vector
            def _(v):
                for t in range(NT):
                    i = v.tensor_reduce(out=self.pstage[:, t:t + 1],
                                        in_=xn3[:, t * E:(t + 1) * E],
                                        axis=AX.X, op=ALU.max, apply_absolute_value=True)
                self.edge(v, i)
                v.tensor_reduce(out=self.pstage[:, 16:17], in_=self.pstage[:, 0:NT],
                                axis=AX.X, op=ALU.max)
        self.global_absmax(self.pstage[:, 16:17], S_X0)
        self.allreduce_max(1)
        with nc.Block() as blk:
            @blk.vector
            def _(v):
                i = v.tensor_scalar(out=self.srow[0:1, S_S1:S_S1 + 1],
                                    in0=self.srow[0:1, S_X0:S_X0 + 1],
                                    scalar1=INVQ, scalar2=1e-12, op0=ALU.mult, op1=ALU.max)
                self.edge(v, i)
                v.reciprocal(out=self.srow[0:1, S_I1:S_I1 + 1],
                             in_=self.srow[0:1, S_S1:S_S1 + 1])
        self.bcast_scalars([(S_I1, B_I1)])

        s_wo.close()

        s_xqf = ExitStack()
        xq_fm = self.sb(s_xqf, "xq_fm", [128, 8 * T], BF16, side="right")
        s_xqt = ExitStack()
        xq_tm = self.sb(s_xqt, "xq_tm", [128, NT * E], BF16, side="right")
        with nc.Block() as blk:
            @blk.vector
            def _(v):
                v.wait_ge(self.H("sG"), self.V("sG"))
                for t in range(NT):
                    i = self.emit_round(v, xn3[:, t * E:(t + 1) * E], self.scal[:, B_I1:B_I1 + 1],
                                        xq_tm[:, t * E:(t + 1) * E], ybuf[:], mbuf[:])
                    self.edge(v, i)
        s_xn3.close()

        ffs = ExitStack()
        spill_stage = self.sb(ffs, "spill_stage", [128, 2 * T], BF16)
        wdqT_down = self.sb(ffs, "wdqT_down", [128, 24 * 1024], BF16)
        s_dq = ExitStack()
        dqst = self.sb(s_dq, "dqst", [128, 2 * I3 + 2 * (I3 // 32) + 2 * I3], F32)
        o_nat, o_s, o_inv, o_t, o_m = (0, 2 * I3, 2 * I3 + I3 // 32,
                                       2 * I3 + 2 * (I3 // 32),
                                       3 * I3 + 2 * (I3 // 32))
        wdq_nat = self.sb(s_dq, "wdq_nat", [128, 2 * I3], BF16)

        ptx = ExitStack()
        pa3 = self.psum(ptx, "fpsA", [128, 1024], BF16)
        pb3 = self.psum(ptx, "fpsB", [128, 1024], BF16)
        self.transpose_tm_to_fm(xq_tm, xq_fm, pa3, pb3)
        ptx.close()
        s_xqt.close()

        def dequant_weight(w_d, wdqT, O, nI, tag=[0]):
            nblk = nI // 32
            noc = O // 128
            nfc = nI // 128
            fgroups = [list(range(gg, min(gg + 8, nfc))) for gg in range(0, nfc, 8)]
            ng = len(fgroups)
            tag[0] += 1
            pdq = ExitStack()
            pqa = self.psum(pdq, "dqA%d" % tag[0], [128, 1024], BF16)
            pqb = self.psum(pdq, "dqB%d" % tag[0], [128, 1024], BF16)
            pqs = [pqa, pqb]
            with nc.Block() as blk:
                d0 = self.V("sD")
                dq0 = self.V("sDQ")
                mm0 = self.V("sMM")
                ev0 = self.V("sEV")

                @blk.sync
                def _(s):
                    for oc in range(noc):
                        if oc >= 2:
                            s.wait_ge(self.H("sDQ"), dq0 + oc - 1)
                        self.mark(s.dma_start(out=dqst[:, o_nat + (oc % 2) * nI:
                                                       o_nat + (oc % 2) * nI + nI],
                                              in_=w_d.ap()[oc * 128:(oc + 1) * 128, :]),
                                  "sD", 16)
                        s.wait_ge(self.H("sD"), self.V("sD"))

                @blk.tensor
                def _(t_):
                    for oc in range(noc):
                        t_.wait_ge(self.H("sDQ"), dq0 + oc + 1)
                        for g, fcg in enumerate(fgroups):
                            k = oc * ng + g
                            if k >= 2:
                                t_.wait_ge(self.H("sEV"), ev0 + k - 1)
                            buf = pqs[k % 2]
                            for jj, fc in enumerate(fcg):
                                li = t_.transpose(
                                    buf[:, jj * 128:(jj + 1) * 128],
                                    wdq_nat[:, (oc % 2) * nI + fc * 128:
                                            (oc % 2) * nI + (fc + 1) * 128],
                                    self.idb[:])
                            li.then_inc(self.H("sMM"), 1)
                self.inc("sMM", noc * ng)

                @blk.vector
                def _(v):
                    for oc in range(noc):
                        v.wait_ge(self.H("sD"), d0 + 16 * (oc + 1))
                        if oc >= 1:
                            v.wait_ge(self.H("sDQ"), dq0 + oc)
                        if oc >= 2:
                            v.wait_ge(self.H("sMM"), mm0 + (oc - 1) * ng)
                        nat = dqst[:, o_nat + (oc % 2) * nI: o_nat + (oc % 2) * nI + nI]
                        nat3 = nat.rearrange("p (b i) -> p b i", i=32)
                        i = v.tensor_reduce(out=dqst[:, o_s:o_s + nblk].unsqueeze(2),
                                            in_=nat3, axis=AX.X, op=ALU.max,
                                            apply_absolute_value=True)
                        self.edge(v, i)
                        i = v.tensor_scalar(out=dqst[:, o_s:o_s + nblk],
                                            in0=dqst[:, o_s:o_s + nblk],
                                            scalar1=1.0 / 7.0, scalar2=1e-12,
                                            op0=ALU.mult, op1=ALU.max)
                        self.edge(v, i)
                        i = v.reciprocal(out=dqst[:, o_inv:o_inv + nblk],
                                         in_=dqst[:, o_s:o_s + nblk])
                        self.edge(v, i)
                        inv_bc = dqst[:, o_inv:o_inv + nblk].unsqueeze(2).broadcast_to((128, nblk, 32))
                        s_bc = dqst[:, o_s:o_s + nblk].unsqueeze(2).broadcast_to((128, nblk, 32))
                        tb = dqst[:, o_t:o_t + nI]
                        mb = dqst[:, o_m:o_m + nI]
                        i = v.tensor_tensor(out=tb.rearrange("p (b i) -> p b i", i=32),
                                            in0=nat3, in1=inv_bc, op=ALU.mult)
                        self.edge(v, i)
                        i = v.tensor_scalar(out=mb, in0=tb, scalar1=MAGIC, scalar2=MAGIC,
                                            op0=ALU.add, op1=ALU.subtract)
                        self.edge(v, i)
                        i = v.tensor_tensor(out=wdq_nat[:, (oc % 2) * nI:
                                                        (oc % 2) * nI + nI].rearrange(
                                                "p (b i) -> p b i", i=32),
                                            in0=mb.rearrange("p (b i) -> p b i", i=32),
                                            in1=s_bc, op=ALU.mult)
                        i.then_inc(self.H("sDQ"), 1)
                        self.inc("sDQ")
                        if oc > 0:
                            for g in range(ng):
                                k = (oc - 1) * ng + g
                                v.wait_ge(self.H("sMM"), mm0 + k + 1)
                                n = len(fgroups[g]) * 128
                                base = (oc - 1) * 128
                                li = v.tensor_copy(
                                    out=wdqT[:].rearrange("p (f o) -> p f o", o=O)[
                                        :, fgroups[g][0]:fgroups[g][0] + len(fgroups[g]),
                                        base:base + 128],
                                    in_=pqs[k % 2][:, 0:n].rearrange("p (f o) -> p f o", o=128))
                                li.then_inc(self.H("sEV"), 1)
                                self.inc("sEV")
                    for g in range(ng):
                        k = (noc - 1) * ng + g
                        v.wait_ge(self.H("sMM"), mm0 + k + 1)
                        n = len(fgroups[g]) * 128
                        base = (noc - 1) * 128
                        li = v.tensor_copy(
                            out=wdqT[:].rearrange("p (f o) -> p f o", o=O)[
                                :, fgroups[g][0]:fgroups[g][0] + len(fgroups[g]),
                                base:base + 128],
                            in_=pqs[k % 2][:, 0:n].rearrange("p (f o) -> p f o", o=128))
                        li.then_inc(self.H("sEV"), 1)
                        self.inc("sEV")
            pdq.close()

        def big_mms(wdqT, spill_d, srow_dst, tag=[0]):
            noc = I3 // 128
            tag[0] += 1
            pps = ExitStack()
            fa = self.psum(pps, "ffa%d" % tag[0], [128, 512], F32)
            fb = self.psum(pps, "ffb%d" % tag[0], [128, 512], F32)
            items = [(oc, tc) for oc in range(noc) for tc in range(n512)]
            with nc.Block() as blk:
                mm0 = self.V("sMM")
                ev0 = self.V("sEV")
                tm0 = self.V("sTM")
                dg0 = self.V("sG")

                @blk.tensor
                def _(t_):
                    for k, (oc, tc) in enumerate(items):
                        if k >= 2:
                            t_.wait_ge(self.H("sEV"), ev0 + k - 1)
                        buf = (fa, fb)[k % 2]
                        li = self.acc_mm(t_, buf[0:128, 0:512],
                                         [(wdqT[:, fc * I3 + oc * 128: fc * I3 + (oc + 1) * 128],
                                           xq_fm[:, fc * T + tc * 512: fc * T + tc * 512 + 512])
                                          for fc in range(8)])
                        li.then_inc(self.H("sMM"), 1)
                self.inc("sMM", len(items))

                @blk.scalar
                def _(sc):
                    for k, (oc, tc) in enumerate(items):
                        sc.wait_ge(self.H("sMM"), mm0 + k + 1)
                        if oc >= 2 and tc == 0:
                            sc.wait_ge(self.H("sG"), dg0 + 16 * (oc - 1))
                            sc.wait_ge(self.H("sTM"), tm0 + oc - 1)
                        sc.activation(out=spill_stage[:, (oc % 2) * T + tc * 512:
                                                      (oc % 2) * T + tc * 512 + 512],
                                      in_=(fa, fb)[k % 2][0:128, 0:512], func=AF.Identity,
                                      bias=0.0, scale=1.0).then_inc(self.H("sEV"), 1)
                        self.inc("sEV")

                @blk.vector
                def _(v):
                    for oc in range(noc):
                        v.wait_ge(self.H("sEV"), ev0 + (oc + 1) * n512)
                        v.tensor_reduce(out=self.pstage[:, oc:oc + 1],
                                        in_=spill_stage[:, (oc % 2) * T:(oc % 2) * T + T],
                                        axis=AX.X, op=ALU.max,
                                        apply_absolute_value=True).then_inc(self.H("sTM"), 1)
                self.inc("sTM", noc)

                @blk.gpsimd
                def _(g):
                    for oc in range(noc):
                        g.wait_ge(self.H("sTM"), tm0 + oc + 1)
                        self.mark(g.dma_start(out=spill_d.ap()[oc * 128:(oc + 1) * 128, :],
                                              in_=spill_stage[:, (oc % 2) * T:(oc % 2) * T + T]),
                                  "sG", 16)
                        g.wait_ge(self.H("sG"), self.V("sG"))
            pps.close()
            with nc.Block() as blk:
                @blk.vector
                def _(v):
                    v.tensor_reduce(out=self.pstage[:, 24:25], in_=self.pstage[:, 0:noc],
                                    axis=AX.X, op=ALU.max)
            self.global_absmax(self.pstage[:, 24:25], srow_dst)

        s_wu = ExitStack()
        wdqT_up = self.sb(s_wu, "wdqT_up", [128, 8 * I3], BF16)
        dequant_weight(wup_d, wdqT_up, I3, E)
        big_mms(wdqT_up, u_spill, S_LU)
        s_wu.close()
        s_wg = ExitStack()
        wdqT_g = self.sb(s_wg, "wdqT_g", [128, 8 * I3], BF16)
        dequant_weight(wgate_d, wdqT_g, I3, E)
        big_mms(wdqT_g, g_spill, S_LG)
        s_wg.close()
        s_xqf.close()

        dequant_weight(wdown_d, wdqT_down, 1024, I3)
        s_dq.close()

        with nc.Block() as blk:
            @blk.vector
            def _(v):
                i = v.tensor_copy(out=self.srow[0:1, S_X0:S_X0 + 1],
                                  in_=self.srow[0:1, S_LU:S_LU + 1])
                i = v.tensor_copy(out=self.srow[0:1, S_X1:S_X1 + 1],
                                  in_=self.srow[0:1, S_LG:S_LG + 1])
        self.allreduce_max(2)
        with nc.Block() as blk:
            @blk.vector
            def _(v):
                i = v.tensor_scalar(out=self.srow[0:1, S_S2:S_S2 + 1],
                                    in0=self.srow[0:1, S_X0:S_X0 + 1],
                                    scalar1=self.srow[0:1, S_S1:S_S1 + 1], scalar2=None,
                                    op0=ALU.mult)
                self.edge(v, i)
                i = v.tensor_scalar(out=self.srow[0:1, S_S2:S_S2 + 1],
                                    in0=self.srow[0:1, S_S2:S_S2 + 1],
                                    scalar1=INVQ, scalar2=1e-12, op0=ALU.mult, op1=ALU.max)
                self.edge(v, i)
                i = v.reciprocal(out=self.srow[0:1, S_C2:S_C2 + 1],
                                 in_=self.srow[0:1, S_S2:S_S2 + 1])
                self.edge(v, i)
                i = v.tensor_scalar(out=self.srow[0:1, S_C2:S_C2 + 1],
                                    in0=self.srow[0:1, S_C2:S_C2 + 1],
                                    scalar1=self.srow[0:1, S_S1:S_S1 + 1], scalar2=None,
                                    op0=ALU.mult)
                self.edge(v, i)
                i = v.tensor_scalar(out=self.srow[0:1, S_S3:S_S3 + 1],
                                    in0=self.srow[0:1, S_X1:S_X1 + 1],
                                    scalar1=self.srow[0:1, S_S1:S_S1 + 1], scalar2=None,
                                    op0=ALU.mult)
                self.edge(v, i)
                i = v.tensor_scalar(out=self.srow[0:1, S_S3:S_S3 + 1],
                                    in0=self.srow[0:1, S_S3:S_S3 + 1],
                                    scalar1=INVQ, scalar2=1e-12, op0=ALU.mult, op1=ALU.max)
                self.edge(v, i)
                i = v.reciprocal(out=self.srow[0:1, S_C3:S_C3 + 1],
                                 in_=self.srow[0:1, S_S3:S_S3 + 1])
                self.edge(v, i)
                i = v.tensor_scalar(out=self.srow[0:1, S_C3:S_C3 + 1],
                                    in0=self.srow[0:1, S_C3:S_C3 + 1],
                                    scalar1=self.srow[0:1, S_S1:S_S1 + 1], scalar2=None,
                                    op0=ALU.mult)
        self.bcast_scalars([(S_C2, B_C2), (S_C3, B_C3), (S_S3, B_S3)])

        s_gs = ExitStack()
        gsb = self.sb(s_gs, "gsb", [128, 24 * T], BF16)
        s_gsr = ExitStack()
        fy = self.sb(s_gsr, "fy", [128, T], F32, side="right")
        fm_ = self.sb(s_gsr, "fm_", [128, T], F32, side="right")
        fsig = self.sb(s_gsr, "fsig", [128, T], F32, side="right")

        NIC = I3 // 128
        MCOL = self.scal[:, 7:8]

        with nc.Block() as blk:
            d0 = self.V("sD")
            so0 = self.V("sSo")
            dq0 = self.V("sDQ")
            ev0 = self.V("sEV")

            @blk.sync
            def _(s):
                for ic in range(NIC):
                    if ic >= 2:
                        s.wait_ge(self.H("sSo"), so0 + 2 * (ic - 2) + 1)
                    self.mark(s.dma_start(out=spill_stage[:, (ic % 2) * T:(ic % 2) * T + T],
                                          in_=g_spill.ap()[ic * 128:(ic + 1) * 128, :]),
                              "sD", 16)
                    s.wait_ge(self.H("sD"), self.V("sD"))

            @blk.scalar
            def _(sc):
                for ic in range(NIC):
                    sc.wait_ge(self.H("sD"), d0 + 16 * (ic + 1))
                    if ic > 0:
                        sc.wait_ge(self.H("sEV"), ev0 + ic)
                    sc.activation(out=fy[:], in_=spill_stage[:, (ic % 2) * T:(ic % 2) * T + T],
                                  func=AF.Identity, scale=self.scal[:, B_C3:B_C3 + 1],
                                  bias=MCOL).then_inc(self.H("sSo"), 1)
                    self.inc("sSo")
                    sc.wait_ge(self.H("sDQ"), dq0 + ic + 1)
                    sc.activation(out=fsig[:], in_=gsb[:, ic * T:(ic + 1) * T],
                                  func=AF.Sigmoid, bias=0.0,
                                  scale=self.scal[:, B_S3:B_S3 + 1]).then_inc(self.H("sSo"), 1)
                    self.inc("sSo")

            @blk.vector
            def _(v):
                for ic in range(NIC):
                    v.wait_ge(self.H("sSo"), so0 + 2 * ic + 1)
                    i = v.tensor_scalar(out=gsb[:, ic * T:(ic + 1) * T], in0=fy[:],
                                        scalar1=MAGIC, scalar2=None, op0=ALU.subtract)
                    i.then_inc(self.H("sDQ"), 1)
                    self.inc("sDQ")
                    v.wait_ge(self.H("sSo"), so0 + 2 * ic + 2)
                    i = v.tensor_scalar(out=fy[:], in0=fsig[:], scalar1=65536.0,
                                        scalar2=65535.0, op0=ALU.mult, op1=ALU.min)
                    self.edge(v, i)
                    i = v.tensor_scalar(out=fm_[:], in0=fy[:], scalar1=MAGIC, scalar2=MAGIC,
                                        op0=ALU.add, op1=ALU.subtract)
                    self.edge(v, i)
                    i = v.tensor_tensor(out=gsb[:, ic * T:(ic + 1) * T],
                                        in0=gsb[:, ic * T:(ic + 1) * T], in1=fm_[:], op=ALU.mult)
                    self.edge(v, i)
                    i = v.tensor_reduce(out=self.pstage[:, ic:ic + 1],
                                        in_=gsb[:, ic * T:(ic + 1) * T],
                                        axis=AX.X, op=ALU.max, apply_absolute_value=True)
                    i.then_inc(self.H("sEV"), 1)
                    self.inc("sEV")
        with nc.Block() as blk:
            @blk.vector
            def _(v):
                v.tensor_reduce(out=self.pstage[:, 24:25], in_=self.pstage[:, 0:24],
                                axis=AX.X, op=ALU.max)
        self.global_absmax(self.pstage[:, 24:25], S_X0)
        self.allreduce_max(1)
        with nc.Block() as blk:
            @blk.vector
            def _(v):
                i = v.tensor_scalar(out=self.srow[0:1, S_S4:S_S4 + 1],
                                    in0=self.srow[0:1, S_X0:S_X0 + 1],
                                    scalar1=self.srow[0:1, S_S3:S_S3 + 1], scalar2=None,
                                    op0=ALU.mult)
                self.edge(v, i)
                i = v.tensor_scalar(out=self.srow[0:1, S_S4:S_S4 + 1],
                                    in0=self.srow[0:1, S_S4:S_S4 + 1],
                                    scalar1=INVQ / 65536.0, scalar2=1e-12,
                                    op0=ALU.mult, op1=ALU.max)
                self.edge(v, i)
                i = v.reciprocal(out=self.srow[0:1, S_C4:S_C4 + 1],
                                 in_=self.srow[0:1, S_S4:S_S4 + 1])
                self.edge(v, i)
                i = v.tensor_scalar(out=self.srow[0:1, S_C4:S_C4 + 1],
                                    in0=self.srow[0:1, S_C4:S_C4 + 1],
                                    scalar1=self.srow[0:1, S_S3:S_S3 + 1], scalar2=1.0 / 65536.0,
                                    op0=ALU.mult, op1=ALU.mult)
        self.bcast_scalars([(S_C4, B_C4)])

        with nc.Block() as blk:
            d0 = self.V("sD")
            so0 = self.V("sSo")
            ev0 = self.V("sEV")

            @blk.sync
            def _(s):
                for ic in range(NIC):
                    if ic >= 2:
                        s.wait_ge(self.H("sSo"), so0 + 2 * (ic - 2) + 2)
                    self.mark(s.dma_start(out=spill_stage[:, (ic % 2) * T:(ic % 2) * T + T],
                                          in_=u_spill.ap()[ic * 128:(ic + 1) * 128, :]),
                              "sD", 16)
                    s.wait_ge(self.H("sD"), self.V("sD"))

            @blk.scalar
            def _(sc):
                for ic in range(NIC):
                    if ic > 0:
                        sc.wait_ge(self.H("sEV"), ev0 + ic)
                    sc.activation(out=fy[:], in_=gsb[:, ic * T:(ic + 1) * T],
                                  func=AF.Identity, scale=self.scal[:, B_C4:B_C4 + 1],
                                  bias=MCOL).then_inc(self.H("sSo"), 1)
                    self.inc("sSo")
                    sc.wait_ge(self.H("sD"), d0 + 16 * (ic + 1))
                    sc.activation(out=fsig[:], in_=spill_stage[:, (ic % 2) * T:(ic % 2) * T + T],
                                  func=AF.Identity, scale=self.scal[:, B_C2:B_C2 + 1],
                                  bias=MCOL).then_inc(self.H("sSo"), 1)
                    self.inc("sSo")

            @blk.vector
            def _(v):
                for ic in range(NIC):
                    v.wait_ge(self.H("sSo"), so0 + 2 * ic + 1)
                    i = v.tensor_scalar(out=gsb[:, ic * T:(ic + 1) * T], in0=fy[:],
                                        scalar1=MAGIC, scalar2=None, op0=ALU.subtract)
                    self.edge(v, i)
                    v.wait_ge(self.H("sSo"), so0 + 2 * ic + 2)
                    i = v.tensor_scalar(out=fm_[:], in0=fsig[:],
                                        scalar1=MAGIC, scalar2=None, op0=ALU.subtract)
                    self.edge(v, i)
                    i = v.tensor_tensor(out=gsb[:, ic * T:(ic + 1) * T],
                                        in0=gsb[:, ic * T:(ic + 1) * T],
                                        in1=fm_[:], op=ALU.mult)
                    self.edge(v, i)
                    i = v.tensor_reduce(out=self.pstage[:, ic:ic + 1],
                                        in_=gsb[:, ic * T:(ic + 1) * T],
                                        axis=AX.X, op=ALU.max, apply_absolute_value=True)
                    i.then_inc(self.H("sEV"), 1)
                    self.inc("sEV")
        with nc.Block() as blk:
            @blk.vector
            def _(v):
                v.tensor_reduce(out=self.pstage[:, 24:25], in_=self.pstage[:, 0:24],
                                axis=AX.X, op=ALU.max)
        self.global_absmax(self.pstage[:, 24:25], S_X0)
        self.allreduce_max(1)
        with nc.Block() as blk:
            @blk.vector
            def _(v):
                i = v.tensor_scalar(out=self.srow[0:1, S_S5:S_S5 + 1],
                                    in0=self.srow[0:1, S_X0:S_X0 + 1],
                                    scalar1=self.srow[0:1, S_S4:S_S4 + 1], scalar2=None,
                                    op0=ALU.mult)
                self.edge(v, i)
                i = v.tensor_scalar(out=self.srow[0:1, S_S5:S_S5 + 1],
                                    in0=self.srow[0:1, S_S5:S_S5 + 1],
                                    scalar1=self.srow[0:1, S_S2:S_S2 + 1], scalar2=None,
                                    op0=ALU.mult)
                self.edge(v, i)
                i = v.tensor_scalar(out=self.srow[0:1, S_S5:S_S5 + 1],
                                    in0=self.srow[0:1, S_S5:S_S5 + 1],
                                    scalar1=INVQ, scalar2=1e-12, op0=ALU.mult, op1=ALU.max)
                self.edge(v, i)
                i = v.reciprocal(out=self.srow[0:1, S_C5:S_C5 + 1],
                                 in_=self.srow[0:1, S_S5:S_S5 + 1])
                self.edge(v, i)
                i = v.tensor_scalar(out=self.srow[0:1, S_C5:S_C5 + 1],
                                    in0=self.srow[0:1, S_C5:S_C5 + 1],
                                    scalar1=self.srow[0:1, S_S4:S_S4 + 1], scalar2=None,
                                    op0=ALU.mult)
                self.edge(v, i)
                i = v.tensor_scalar(out=self.srow[0:1, S_C5:S_C5 + 1],
                                    in0=self.srow[0:1, S_C5:S_C5 + 1],
                                    scalar1=self.srow[0:1, S_S2:S_S2 + 1], scalar2=None,
                                    op0=ALU.mult)
        self.bcast_scalars([(S_C5, B_C5), (S_S5, B_S5)])

        with nc.Block() as blk:
            so0 = self.V("sSo")
            ev0 = self.V("sEV")

            @blk.scalar
            def _(sc):
                sc.wait_ge(self.H("sG"), self.V("sG"))
                for ic in range(NIC):
                    if ic >= 2:
                        sc.wait_ge(self.H("sEV"), ev0 + ic - 1)
                    sc.activation(out=(fy, fsig)[ic % 2][:], in_=gsb[:, ic * T:(ic + 1) * T],
                                  func=AF.Identity, scale=self.scal[:, B_C5:B_C5 + 1],
                                  bias=MCOL).then_inc(self.H("sSo"), 1)
                    self.inc("sSo")

            @blk.vector
            def _(v):
                for ic in range(NIC):
                    v.wait_ge(self.H("sSo"), so0 + ic + 1)
                    i = v.tensor_scalar(out=gsb[:, ic * T:(ic + 1) * T],
                                        in0=(fy, fsig)[ic % 2][:],
                                        scalar1=MAGIC, scalar2=None, op0=ALU.subtract)
                    i.then_inc(self.H("sEV"), 1)
                    self.inc("sEV")

        s_gsr.close()
        s_fout = ExitStack()
        ffn_fm = self.sb(s_fout, "ffn_fm", [128, 8 * T], BF16, side="right")
        pdn = ExitStack()
        da = self.psum(pdn, "dna", [128, 512], F32)
        db = self.psum(pdn, "dnb", [128, 512], F32)

        def dn_mm(t_, it, buf):
            oc, tc = it
            return self.acc_mm(t_, buf[0:128, 0:512],
                        [(wdqT_down[:, ic * 1024 + oc * 128: ic * 1024 + (oc + 1) * 128],
                          gsb[:, ic * T + tc * 512: ic * T + tc * 512 + 512])
                         for ic in range(I3 // 128)])

        def dn_ev(sc, it, buf):
            oc, tc = it
            return sc.activation(out=ffn_fm[:, oc * T + tc * 512: oc * T + tc * 512 + 512],
                          in_=buf[0:128, 0:512], func=AF.Identity, bias=0.0,
                          scale=self.scal[:, B_S5:B_S5 + 1])

        self.mm_evac([(oc, tc) for oc in range(8) for tc in range(n512)], dn_mm, dn_ev,
                     [da, db], evac_eng="scalar")
        pdn.close()
        s_gs.close()
        ffs.close()

        ffn_tm = self.sb(s_fout, "ffn_tm", [128, NT * E], BF16)
        ptf = ExitStack()
        fa2 = self.psum(ptf, "ftA", [128, 1024], BF16)
        fb2 = self.psum(ptf, "ftB", [128, 1024], BF16)

        def f2t_mm(t_, tt, buf):
            for fc in range(8):
                li = t_.transpose(buf[:, fc * 128:(fc + 1) * 128],
                                  ffn_fm[:, fc * T + tt * 128: fc * T + (tt + 1) * 128],
                                  self.idb[:])
            return li

        def f2t_ev(v, tt, buf):
            return v.tensor_copy(out=ffn_tm[:, tt * E:(tt + 1) * E], in_=buf[:, 0:1024])

        self.mm_evac(list(range(NT)), f2t_mm, f2t_ev, [fa2, fb2])
        ptf.close()

        s_r3 = ExitStack()
        r_tm3 = self.sb(s_r3, "r_tm3", [128, NT * E], F32)
        with nc.Block() as blk:
            @blk.sync
            def _(s):
                self.mark(s.dma_start(out=r_tm3[:].rearrange("p (t e) -> p t e", e=E),
                                      in_=r_spill.ap().rearrange("(t p) e -> p t e", p=128)),
                          "sD", 16)
        self.ln_stats(lambda t: ffn_tm[:, t * E:(t + 1) * E], NT)
        self.ln_finish(NT)
        with nc.Block() as blk:
            @blk.vector
            def _(v):
                v.wait_ge(self.H("sD"), self.V("sD"))
                for t in range(NT):
                    i = v.scalar_tensor_tensor(out=r_tm3[:, t * E:(t + 1) * E],
                                               in0=ffn_tm[:, t * E:(t + 1) * E],
                                               scalar=self.lnab[:, 2 * t:2 * t + 1],
                                               in1=r_tm3[:, t * E:(t + 1) * E],
                                               op0=ALU.mult, op1=ALU.add)
                    self.edge(v, i)
                    i = v.tensor_scalar(out=r_tm3[:, t * E:(t + 1) * E],
                                        in0=r_tm3[:, t * E:(t + 1) * E],
                                        scalar1=self.lnab[:, 2 * t + 1:2 * t + 2],
                                        scalar2=None, op0=ALU.add)
                    self.edge(v, i)
        with nc.Block() as blk:
            @blk.sync
            def _(s):
                self.mark(s.dma_start(out=out_d.ap().rearrange("(t p) e -> p t e", p=128),
                                      in_=r_tm3[:].rearrange("p (t e) -> p t e", e=E)), "sD", 16)
                s.wait_ge(self.H("sD"), self.V("sD"))
        s_r3.close()
        s_fout.close()
        self.root.close()
        return self.nc


def _host_consts(T):
    half = DR // 2
    inv = (1.0 / 10000.0 ** (np.arange(half, dtype=np.float64) / half))
    ang = np.arange(T, dtype=np.float64)[None, :] * inv[:, None]
    cos16 = np.cos(ang)
    sin16 = np.sin(ang)
    cosT = np.tile(cos16, (8, 1)).astype(np.float32)
    sinT = np.tile(sin16, (8, 1)).astype(np.float32)
    identb = np.eye(128, dtype=np.float32).astype(ml_dtypes.bfloat16)
    identf = np.eye(128, dtype=np.float32)
    iota = np.arange(128, dtype=np.float32).reshape(128, 1)
    return cosT, sinT, identb, identf, iota


def _cmask():
    m = np.zeros((128, 128), dtype=np.float32)
    for k in range(128):
        m[k, :k] = -1e30
    return m


def _wqr_perm():
    perm = np.zeros(H * DR, dtype=np.int64)
    for hh in range(H):
        for halfsel in range(2):
            for i in range(16):
                tile = halfsel * 2 + hh // 8
                newc = tile * 128 + (hh % 8) * 16 + i
                perm[newc] = hh * 32 + halfsel * 16 + i
    return perm


_BUILD_CACHE = {}


def kernel(L0_attn_x, L0_attn_Wq, L0_attn_Wqr, L0_attn_Wkr, L0_attn_Wkvd,
           L0_attn_Wku, L0_attn_Wvu, L0_attn_Wo, L0_attn_lobo_log,
           L0_ffn_w_gate, L0_ffn_w_up, L0_ffn_w_down):
    T = 2048
    x = np.ascontiguousarray(np.asarray(L0_attn_x, dtype=np.float32))
    B = x.shape[0]
    cosT, sinT, identb, identf, iota = _host_consts(T)
    perm = _wqr_perm()
    wqr_p = np.ascontiguousarray(np.asarray(L0_attn_Wqr, dtype=np.float32)[:, perm])
    common = {
        "wq": np.ascontiguousarray(L0_attn_Wq, dtype=np.float32),
        "wqr": wqr_p,
        "wkr": np.ascontiguousarray(L0_attn_Wkr, dtype=np.float32),
        "wkvd": np.ascontiguousarray(L0_attn_Wkvd, dtype=np.float32),
        "wku": np.ascontiguousarray(L0_attn_Wku, dtype=np.float32),
        "wvu": np.ascontiguousarray(L0_attn_Wvu, dtype=np.float32),
        "wo": np.ascontiguousarray(L0_attn_Wo, dtype=np.float32),
        "lobo": np.ascontiguousarray(L0_attn_lobo_log, dtype=np.float32),
        "wup": np.ascontiguousarray(L0_ffn_w_up, dtype=np.float32),
        "wgate": np.ascontiguousarray(L0_ffn_w_gate, dtype=np.float32),
        "wdown": np.ascontiguousarray(L0_ffn_w_down, dtype=np.float32),
        "identb": identb, "identf": identf,
        "cosT": cosT, "sinT": sinT, "iota128": iota,
        "cmask": _cmask(),
    }
    if T not in _BUILD_CACHE:
        _BUILD_CACHE[T] = KB(T).build()
    nc = _BUILD_CACHE[T]
    in_maps = [dict(common, x=np.ascontiguousarray(x[c])) for c in range(B)]
    res = run_bass_kernel_spmd(nc, in_maps, core_ids=list(range(B)))
    return np.stack([res.results[c]["out"] for c in range(B)], axis=0)
